# version 13
# baseline (speedup 1.0000x reference)
import contextlib
import numpy as np
import ml_dtypes
import concourse.bass as bass
import concourse.mybir as mybir
from concourse.bass_utils import run_bass_kernel_spmd

F32 = mybir.dt.float32
BF16 = mybir.dt.bfloat16
AF = mybir.ActivationFunctionType
ALU = mybir.AluOpType
AX = mybir.AxisListType
BF = ml_dtypes.bfloat16

D = 1024
NCH = 8
HEADS = 16
HD = 64
NEG = -1.0e30



def _pack_bias(b):
    return np.ascontiguousarray(b.reshape(-1, 128).T).astype(np.float32)


def _rope_tables(T, d=HD, theta=10000.0):
    inv = 1.0 / theta ** (np.arange(0, d, 2, dtype=np.float64) / d)
    t = np.arange(T, dtype=np.float64)
    fr = np.outer(inv, t)
    return np.cos(fr).astype(np.float32), np.sin(fr).astype(np.float32)


def _rope_perm():
    p = []
    for h in range(HEADS):
        base = h * HD
        p += [base + i for i in range(0, HD, 2)]
        p += [base + i for i in range(1, HD, 2)]
    return np.array(p)


def tile_w(W, kcn, mw):
    K, M = W.shape
    assert K == kcn * 128 and M % mw == 0
    G = M // mw
    return np.ascontiguousarray(
        W.reshape(kcn, 128, G, mw).transpose(2, 1, 0, 3))


def prep_weights(iv):
    w = {}
    w["l0a_w"] = np.ascontiguousarray(iv["L0_attn_a_c_attn_kernel"].T)
    w["l0a_b"] = _pack_bias(iv["L0_attn_a_c_attn_bias"])
    w["l0a_wo"] = np.ascontiguousarray(iv["L0_attn_a_c_proj_kernel"].T)
    w["l0a_bo"] = _pack_bias(iv["L0_attn_a_c_proj_bias"])
    wq = iv["L0_attn_b_w_qkv"]
    perm = _rope_perm()
    cols = np.concatenate([perm, 1024 + perm, 2048 + np.arange(1024)])
    w["l0b_w"] = np.ascontiguousarray(wq[:, cols])
    w["l0b_wo"] = np.ascontiguousarray(iv["L0_attn_b_w_out"])
    w["l1a_w"] = np.ascontiguousarray(np.concatenate(
        [iv["L1_attn_a_Wq"].T, iv["L1_attn_a_Wk"].T, iv["L1_attn_a_Wv"].T], axis=1))
    w["l1a_b"] = _pack_bias(np.concatenate(
        [iv["L1_attn_a_bq"], iv["L1_attn_a_bk"], iv["L1_attn_a_bv"]]))
    w["l1a_wo"] = np.ascontiguousarray(iv["L1_attn_a_Wo"].T)
    w["l1a_bo"] = _pack_bias(iv["L1_attn_a_bo"])
    deint = np.concatenate([np.arange(0, 1024, 2), np.arange(1, 1024, 2)])
    w["moe_r"] = np.ascontiguousarray(iv["L0_ffn_router_weight"])
    w["moe_gu"] = np.ascontiguousarray(iv["L0_ffn_gate_up_proj"][:, :, deint])
    gbd = iv["L0_ffn_gate_up_proj_bias"][:, deint]
    w["moe_gb"] = np.ascontiguousarray(gbd.reshape(16, 8, 128).transpose(2, 0, 1).reshape(128, 128)).astype(np.float32)
    w["moe_dn"] = np.ascontiguousarray(iv["L0_ffn_down_proj"])
    w["moe_db"] = np.ascontiguousarray(iv["L0_ffn_down_proj_bias"])
    w["l1f_w1"] = tile_w(iv["L1_ffn_w1_w"].T, 8, 512)
    w["l1f_w2"] = tile_w(iv["L1_ffn_w2_w"].T, 8, 512)
    w["l1f_h1"] = tile_w(iv["L1_ffn_h1_w"].T, 32, 128)
    w["l1f_h2"] = tile_w(iv["L1_ffn_h2_w"].T, 32, 128)
    w["l1f_w3"] = tile_w(iv["L1_ffn_w3_w"].T, 32, 128)
    w["l1f_s"] = np.concatenate([
        _pack_bias(iv["L1_ffn_w1_s"]), _pack_bias(iv["L1_ffn_w2_s"]),
        _pack_bias(iv["L1_ffn_h1_s"]), _pack_bias(iv["L1_ffn_h2_s"]),
        _pack_bias(iv["L1_ffn_w3_s"])], axis=1)
    w["l1b_w"] = tile_w(np.ascontiguousarray(np.concatenate(
        [iv["L1_attn_b_Wq"], iv["L1_attn_b_Wk"], iv["L1_attn_b_Wv"], iv["L1_attn_b_Wg"]],
        axis=1)), 8, 512)
    ba = np.zeros((1024, 64), dtype=np.float32)
    ba[:, 0:16] = iv["L1_attn_b_Wb"]
    ba[:, 32:48] = iv["L1_attn_b_Wa"]
    w["l1b_ba"] = tile_w(ba, 8, 64)
    w["l1b_wo"] = tile_w(np.ascontiguousarray(iv["L1_attn_b_Wo"]), 8, 512)
    w["l1b_dtb"] = np.ascontiguousarray(iv["L1_attn_b_dt_bias"][:, None]).astype(np.float32)
    w["l1b_alog"] = np.ascontiguousarray(iv["L1_attn_b_A_log"][:, None]).astype(np.float32)
    w["l1b_nw"] = np.ascontiguousarray(np.tile(iv["L1_attn_b_norm_w"], 2)[:, None]).astype(np.float32)
    for nm in ("l0a_w", "l0b_w", "l1a_w"):
        w[nm] = tile_w(w[nm], 8, 256)
    for nm in ("l0a_wo", "l0b_wo", "l1a_wo"):
        w[nm] = tile_w(w[nm], 8, 1024)
    w["moe_gu"] = np.ascontiguousarray(np.stack([tile_w(e, 8, 1024) for e in w["moe_gu"]])[:, 0])
    w["moe_dn"] = np.ascontiguousarray(np.stack(
        [tile_w(np.ascontiguousarray(e), 4, 1024) for e in w["moe_dn"]])[:, 0])
    return w


def make_consts(T):
    cb = np.zeros((128, 256), dtype=BF)
    cb[:, 0:128] = np.eye(128)
    cb[:, 128:256] = 1.0
    q = np.arange(128)[:, None]
    k = np.arange(128)[None, :]
    cf = np.where(k <= q, 0.0, NEG).astype(np.float32)
    cos1, sin1 = _rope_tables(T)
    ropet = np.zeros((128, 2 * T), dtype=BF)
    for qd in range(4):
        ropet[qd * 32:(qd + 1) * 32, 0:T] = cos1
        ropet[qd * 32:(qd + 1) * 32, T:2 * T] = sin1
    cind = np.zeros((16, 16 * 128), dtype=BF)
    for e in range(16):
        cind[e, e * 128:(e + 1) * 128] = 1.0
    cf2 = np.zeros((128, 128), dtype=np.float32)
    j = np.arange(64)[:, None]
    cm = np.arange(64)[None, :]
    cf2[0:64, 0:64] = (j <= cm).astype(np.float32)
    cf2[64:128, 0:64] = cf2[0:64, 0:64]
    cf2[0:64, 64:128] = (j < cm).astype(np.float32)
    cf2[64:128, 64:128] = cf2[0:64, 64:128]
    cf2i = np.zeros((128, 64), dtype=np.float32)
    cf2i[0:64] = np.eye(64, dtype=np.float32)
    cf2i[64:128] = np.eye(64, dtype=np.float32)
    cf2 = np.concatenate([cf2, cf2i], axis=1)
    cfi = np.eye(128, dtype=np.float32)
    return {"cb": cb, "cf": cf, "ropet": ropet, "cind": cind,
            "cf2": cf2, "cfi": cfi}



class Prog:
    def __init__(self, nc, stack):
        self.nc = nc
        self.stack = stack
        self._sems = {}
        self.cnt = {}
        self._waited = {}
        self._uid = 0

    def uid(self):
        self._uid += 1
        return self._uid

    def sem(self, name):
        if name not in self._sems:
            self._sems[name] = self.stack.enter_context(self.nc.semaphore(name))
            self.cnt[name] = 0
        return self._sems[name]

    def inc(self, inst, name, n=1):
        inst.then_inc(self.sem(name), n)
        self.cnt[name] += n
        return (name, self.cnt[name])

    def at(self, name):
        return (name, self.cnt.get(name, 0))

    def _flat(self, mss):
        for ms in mss:
            if ms is None:
                continue
            if isinstance(ms, list):
                yield from self._flat(ms)
            elif isinstance(ms[0], tuple):
                yield from self._flat(ms)
            else:
                yield ms

    def wait(self, eng, *mss):
        key = id(eng)
        w = self._waited.setdefault(key, {})
        for name, v in self._flat(mss):
            if v <= 0 or w.get(name, 0) >= v:
                continue
            eng.wait_ge(self.sem(name), v)
            w[name] = v

    def finish(self):
        for name, v in sorted(self.cnt.items()):
            if name.startswith("d") and v > 0:
                self.nc.sync.wait_ge(self.sem(name), v)

    def barrier(self):
        nc = self.nc
        nc.all_engine_barrier()
        for e in (nc.tensor, nc.vector, nc.scalar, nc.gpsimd, nc.sync):
            w = self._waited.setdefault(id(e), {})
            for s in ("tns", "vec", "act", "pool", "sp"):
                if s in self.cnt:
                    w[s] = max(w.get(s, 0), self.cnt[s])



class Ctx:
    pass


def seg3(buf_slice, d=64):
    return buf_slice.rearrange("p (h d) -> p h d", d=d)


def segb(col_slice, n, d=64):
    p = col_slice.shape[0]
    return col_slice.unsqueeze(2).broadcast_to((p, n, d))


def emit_moe(pg, c, T, resid_ms):
    nc = pg.nc
    V, S, TE = nc.vector, nc.scalar, nc.tensor
    NT = 512
    ntiles = T // NT
    nblk = T // 128
    E = 16
    es = contextlib.ExitStack()
    _u = "_%d" % pg.uid()
    sb = lambda name, shape, dt: es.enter_context(nc.sbuf_tensor(name + _u, shape, dt))
    c.xnfull = sb("xnfull", [128, NCH * T], BF16)
    c.rtr32 = sb("rtr32", [128, NCH * 16], F32)
    c.gub = sb("gub", [128, 128], F32)
    c.dnb = sb("dnb", [16, D], BF16)
    c.indm = sb("indm", [16, 16 * 128], BF16)
    c.scoresT = sb("scoresT", [16, T], BF16)
    c.scbf = sb("scbf", [128, 16], BF16)
    c.col1 = sb("col1", [128, 1], F32)
    c.col2 = sb("col2", [128, 1], F32)
    c.col3 = sb("col3", [128, 1], F32)
    c.col4 = sb("col4", [128, 1], F32)
    c.rrow = sb("rrow", [1, T], BF16)
    c.mrow = sb("mrow", [1, T], BF16)
    c.crow = sb("crow", [1, 16], F32)
    c.crb = sb("crb", [128, 16], F32)
    c.onescf = sb("onescf", [128, 1], F32)
    c.wgu = [sb("wgu0", [128, NCH * 1024], BF16), sb("wgu1", [128, NCH * 1024], BF16)]
    c.wdn = [sb("wdn0", [128, 4 * 1024], BF16), sb("wdn1", [128, 4 * 1024], BF16)]
    c.actbuf = sb("actbuf", [128, 4 * 512], BF16)
    c.msc = sb("msc", [128, 3 * 512], F32)
    ldi = pg.inc(nc.sync.dma_start(out=c.indm[:], in_=c.cindd[:, :]), "dc5", 16)
    c.ms["indm_ms"] = ldi

    rt = pg.inc(nc.sync.dma_start(
        out=c.rtr32[:].rearrange("p (kc e) -> p kc e", e=16),
        in_=c.wdram["moe_r"].rearrange("(kc p) e -> p kc e", p=128)), "drt", 16)
    gb = pg.inc(nc.sync.dma_start(out=c.gub[:], in_=c.wdram["moe_gb"][:, :]), "dgb", 16)
    db = pg.inc(nc.gpsimd.dma_start(out=c.dnb[:], in_=c.wdram["moe_db"][:, :]), "ddb", 16)

    for t in range(ntiles):
        ln_ms = emit_ln(pg, c, T, t * NT, NT, resid_ms, c.xnfull, T, save_r=True)

    oc = pg.inc(nc.vector.memset(c.onescf[:], 1.0), "vec")
    pg.wait(TE, rt, oc, c.ms.get("pp_free0"), c.ms.get("rows_read"))
    for kc in range(NCH):
        mmc = nc.tensor.matmul(c.ps_a[0:1, 0:16], c.onescf[:],
                               c.rtr32[:].rearrange("p (kc e) -> p kc e", e=16)[:, kc, :],
                               start=(kc == 0), stop=(kc == NCH - 1))
    mmc_ms = pg.inc(mmc, "tns")
    pg.wait(V, mmc_ms)
    crc = pg.inc(V.tensor_copy(c.crow[:], c.ps_a[0:1, 0:16]), "vec")
    pg.wait(TE, crc)
    crb_mm = pg.inc(nc.tensor.matmul(c.ps_a[:, 16:32], c.ones_row_f, c.crow[0:1, :],
                                     start=True, stop=True), "tns")
    pg.wait(V, crb_mm)
    crb = pg.inc(V.tensor_copy(c.crb[:], c.ps_a[:, 16:32]), "vec")
    c.ms["pp_free0"] = crb

    for blk in range(nblk):
        t0 = blk * 128
        pg.wait(TE, rt, resid_ms, c.ms.get("ps_o_free"), c.ms.get("rows_free"),
                c.ms.get("rows_read"), c.ms.get("rcol_free"))
        for kc in range(NCH):
            g_mm = nc.tensor.matmul(c.ps_o[:, 0:16],
                                    c.resid[:, kc * T + t0: kc * T + t0 + 128],
                                    c.rtr32[:].rearrange("p (kc e) -> p kc e", e=16)[:, kc, :],
                                    start=(kc == 0), stop=(kc == NCH - 1))
        nc.tensor.matmul(c.ps_b[:, 0:1], c.rrow[0:1, t0:t0 + 128],
                         c.constb[0:1, 128:129], start=True, stop=True)
        rmm = pg.inc(nc.tensor.matmul(c.ps_b[:, 1:2], c.mrow[0:1, t0:t0 + 128],
                                      c.constb[0:1, 128:129], start=True, stop=True), "tns")
        pg.wait(V, rmm, c.ms.get("msc_free"))
        cp = pg.inc(V.tensor_copy(c.col1[:], c.ps_b[:, 0:1]), "vec")
        cp2 = pg.inc(V.tensor_copy(c.col2[:], c.ps_b[:, 1:2]), "vec")
        c.ms["rcol_free"] = cp2
        pg.wait(V, cp, cp2, crb)
        pg.inc(V.tensor_scalar(c.msc[:, 0:16], c.ps_o[:, 0:16], c.col1[:], None, ALU.mult), "vec")
        lg = pg.inc(V.tensor_scalar(c.msc[:, 16:32], c.crb[:], c.col2[:], None, ALU.mult), "vec")
        c.ms["ps_o_free"] = lg
        pg.wait(V, lg)
        lg = pg.inc(V.tensor_tensor(c.msc[:, 0:16], c.msc[:, 0:16], c.msc[:, 16:32],
                                    ALU.subtract), "vec")
        pg.wait(V, lg)
        pg.inc(V.tensor_reduce(c.col1[:], c.msc[:, 0:16], AX.X, ALU.max), "vec")
        pg.wait(V, pg.at("vec"))
        pg.inc(V.tensor_scalar(c.msc[:, 16:32], c.msc[:, 0:16], c.col1[:], None, ALU.is_equal), "vec")
        pg.wait(V, pg.at("vec"))
        pg.inc(V.tensor_scalar(c.msc[:, 32:48], c.msc[:, 16:32], NEG, None, ALU.mult), "vec")
        pg.wait(V, pg.at("vec"))
        pg.inc(V.tensor_tensor(c.msc[:, 32:48], c.msc[:, 0:16], c.msc[:, 32:48], ALU.add), "vec")
        pg.wait(V, pg.at("vec"))
        pg.inc(V.tensor_reduce(c.col2[:], c.msc[:, 32:48], AX.X, ALU.max), "vec")
        pg.wait(V, pg.at("vec"))
        d12 = pg.inc(V.tensor_tensor(c.col3[:], c.col1[:], c.col2[:], ALU.subtract), "vec")
        pg.wait(S, d12)
        sg = pg.inc(S.activation(c.col3[:], c.col3[:], AF.Sigmoid), "act")
        pg.wait(V, sg)
        pg.inc(V.tensor_scalar(c.col4[:], c.col3[:], -1.0, 1.0, ALU.mult, ALU.add), "vec")
        pg.inc(V.tensor_scalar(c.msc[:, 48:64], c.msc[:, 32:48], c.col2[:], None, ALU.is_equal), "vec")
        pg.wait(V, pg.at("vec"))
        pg.inc(V.tensor_scalar(c.msc[:, 16:32], c.msc[:, 16:32], c.col3[:], None, ALU.mult), "vec")
        pg.inc(V.tensor_scalar(c.msc[:, 48:64], c.msc[:, 48:64], c.col4[:], None, ALU.mult), "vec")
        pg.wait(V, pg.at("vec"))
        pg.inc(V.tensor_tensor(c.msc[:, 16:32], c.msc[:, 16:32], c.msc[:, 48:64], ALU.add), "vec")
        pg.wait(V, pg.at("vec"))
        sb_ms = pg.inc(V.tensor_copy(c.scbf[:], c.msc[:, 16:32]), "vec")
        c.ms["msc_free"] = sb_ms
        pg.wait(TE, sb_ms, c.ms.get("ps_t_free"))
        tr = pg.inc(nc.tensor.transpose(c.ps_t[0:16, :], c.scbf[:], c.identb), "tns")
        pg.wait(V, tr)
        stc = pg.inc(V.tensor_copy(c.scoresT[:, blk * 128:(blk + 1) * 128], c.ps_t[0:16, :]), "vec")
        c.ms["ps_t_free"] = stc
    route_ms = pg.at("vec")

    def load_gu(e, slot):
        src = c.wdram["moe_gu"][e, :, :, :].rearrange("p kc m -> p (kc m)")
        pg.wait(nc.gpsimd, c.ms.get(("wgu_user", slot)))
        return pg.inc(nc.gpsimd.dma_start(out=c.wgu[slot][:], in_=src), "dgu%d" % slot, 16)

    def load_dn(e, slot):
        src = c.wdram["moe_dn"][e, :, :, :].rearrange("p kc m -> p (kc m)")
        pg.wait(nc.gpsimd, c.ms.get(("wdn_user", slot)))
        return pg.inc(nc.gpsimd.dma_start(out=c.wdn[slot][:], in_=src), "ddn%d" % slot, 16)

    gu_ms = {0: load_gu(0, 0)}
    dn_ms = {0: load_dn(0, 0)}
    for e in range(E):
        slot = e % 2
        if e + 1 < E:
            gu_ms[e + 1] = load_gu(e + 1, (e + 1) % 2)
            dn_ms[e + 1] = load_dn(e + 1, (e + 1) % 2)
        wguv = c.wgu[slot][:].rearrange("p (kc m) -> p kc m", m=1024)
        wdnv = c.wdn[slot][:].rearrange("p (kc m) -> p kc m", m=1024)
        for t in range(ntiles):
            t0 = t * NT
            pg.wait(TE, route_ms, c.ms.get("indm_ms"), c.ms.get("bcast_read"),
                    c.ms.get("scores_free"))
            bcm = pg.inc(nc.tensor.matmul(c.ps_s[:, 0:NT],
                                          c.indm[:, e * 128:(e + 1) * 128],
                                          c.scoresT[:, t0:t0 + NT],
                                          start=True, stop=True), "tns")
            for pr in range(4):
                pg.wait(TE, gu_ms[e], ln_ms, c.ms.get("pp_free0"), c.ms.get("pp_free1"),
                        c.ms.get("rows_read"))
                for kc in range(NCH):
                    nc.tensor.matmul(c.ps_a[:, 0:NT], wguv[:, kc, pr * 128:(pr + 1) * 128],
                                     c.xnfull[:, kc * T + t0: kc * T + t0 + NT],
                                     start=(kc == 0), stop=(kc == NCH - 1))
                for kc in range(NCH):
                    mm = nc.tensor.matmul(c.ps_b[:, 0:NT],
                                          wguv[:, kc, (pr + 4) * 128:(pr + 5) * 128],
                                          c.xnfull[:, kc * T + t0: kc * T + t0 + NT],
                                          start=(kc == 0), stop=(kc == NCH - 1))
                mm_ms = pg.inc(mm, "tns")
                c.ms[("wgu_user", slot)] = mm_ms
                pg.wait(S, mm_ms, gb, c.ms.get("msc01_free"))
                pg.inc(S.activation(c.msc[:, 0:NT], c.ps_a[:, 0:NT], AF.Identity,
                                    bias=c.gub[:, e * 8 + pr: e * 8 + pr + 1]), "act")
                g_ms = pg.at("act")
                pg.inc(S.activation(c.msc[:, NT:2 * NT], c.ps_b[:, 0:NT], AF.Identity,
                                    bias=c.gub[:, e * 8 + pr + 4: e * 8 + pr + 5]), "act")
                u_ms = pg.at("act")
                c.ms["pp_free0"] = g_ms
                c.ms["pp_free1"] = u_ms
                pg.wait(V, g_ms)
                gm = pg.inc(V.tensor_scalar(c.msc[:, 0:NT], c.msc[:, 0:NT], 7.0, None, ALU.min), "vec")
                pg.wait(V, u_ms)
                um = pg.inc(V.tensor_scalar(c.msc[:, NT:2 * NT], c.msc[:, NT:2 * NT],
                                            7.0, -7.0, ALU.min, ALU.max), "vec")
                pg.wait(S, gm)
                sgl = pg.inc(S.activation(c.msc[:, 2 * NT:3 * NT], c.msc[:, 0:NT],
                                          AF.Sigmoid, scale=1.702), "act")
                pg.wait(V, sgl, um)
                pg.inc(V.tensor_tensor(c.msc[:, 2 * NT:3 * NT], c.msc[:, 0:NT],
                                       c.msc[:, 2 * NT:3 * NT], ALU.mult), "vec")
                pg.inc(V.tensor_scalar(c.msc[:, NT:2 * NT], c.msc[:, NT:2 * NT],
                                       1.0, None, ALU.add), "vec")
                pg.wait(V, pg.at("vec"))
                a0 = pg.inc(V.tensor_tensor(c.msc[:, NT:2 * NT], c.msc[:, NT:2 * NT],
                                            c.msc[:, 2 * NT:3 * NT], ALU.mult), "vec")
                pg.wait(V, a0, bcm, c.ms.get("act_user"))
                av = pg.inc(V.tensor_tensor(c.actbuf[:, pr * NT:(pr + 1) * NT],
                                            c.msc[:, NT:2 * NT], c.ps_s[:, 0:NT], ALU.mult), "vec")
                c.ms["msc01_free"] = av
                c.ms["bcast_read"] = av
            act_ms = pg.at("vec")
            for m in range(NCH):
                pg.wait(TE, act_ms, dn_ms[e], db, c.ms.get("ps_d_free"))
                first = True
                if e == 0:
                    nc.tensor.matmul(c.ps_s[:, 512:512 + NT],
                                     c.dnb[:, m * 128:(m + 1) * 128],
                                     c.scoresT[:, t0:t0 + NT], start=True, stop=False)
                    first = False
                for kc in range(4):
                    mm = nc.tensor.matmul(c.ps_s[:, 512:512 + NT],
                                          wdnv[:, kc, m * 128:(m + 1) * 128],
                                          c.actbuf[:, kc * NT:(kc + 1) * NT],
                                          start=first and (kc == 0), stop=(kc == 3))
                dmm = pg.inc(mm, "tns")
                c.ms[("wdn_user", slot)] = dmm
                pg.wait(V, dmm, resid_ms)
                ra = pg.inc(V.tensor_tensor(c.resid[:, m * T + t0: m * T + t0 + NT],
                                            c.resid[:, m * T + t0: m * T + t0 + NT],
                                            c.ps_s[:, 512:512 + NT], ALU.add), "vec")
                c.ms["ps_d_free"] = ra
            c.ms["act_user"] = pg.at("tns")
    pg.barrier()
    es.close()
    return pg.at("vec")


def emit_dnet(pg, c, T, resid_ms):
    nc = pg.nc
    V, S, TE = nc.vector, nc.scalar, nc.tensor
    LP = 256
    npass = T // LP
    nstr = LP // 128
    H, K = HEADS, HD
    es = contextlib.ExitStack()
    _u = "_%d" % pg.uid()
    sb = lambda name, shape, dt: es.enter_context(nc.sbuf_tensor(name + _u, shape, dt))
    LPB = LP
    c.wbas = sb("wbas", [128, 512], BF16)
    c.dtb = sb("dtb", [16, 1], F32)
    c.alog = sb("alog", [16, 1], F32)
    c.nega = sb("nega", [16, 1], F32)
    c.nwcol = sb("nwcol", [128, 1], F32)
    c.cf2 = sb("cf2", [128, 192], F32)
    c.identf = sb("identf", [128, 128], F32)
    c.one16 = sb("one16", [16, 1], F32)
    c.eps6 = sb("eps6", [128, 1], F32)
    c.eps5 = sb("eps5", [128, 1], F32)
    c.q_tok = sb("q_tok", [128, 2 * 1024], BF16)
    c.k_tok = sb("k_tok", [128, 2 * 1024], BF16)
    c.v_tok = sb("v_tok", [128, 2 * 1024], BF16)
    c.g_tok = sb("g_tok", [128, 2 * 1024], BF16)
    c.vtmp2 = sb("vtmp2", [128, LPB], BF16)
    c.dbrows = sb("dbrows", [64, LPB], F32)
    c.dsc = sb("dsc", [16, 2 * LPB], F32)
    c.dbcols = sb("dbcols", [128, 128], F32)
    c.gcols = sb("gcols", [128, 32], F32)
    c.egcols = sb("egcols", [128, 32], F32)
    c.grows = sb("grows", [16, LPB], F32)
    c.colsb = sb("colsb", [128, 96], BF16)
    c.gflat = sb("gflat", [1, 2048], F32)
    c.ecols = sb("ecols", [64, 32], F32)
    c.ecolsb = sb("ecolsb", [128, 32], BF16)
    c.l2c = sb("l2c", [128, 32], F32)
    c.l2cb = sb("l2cb", [128, 16], BF16)
    c.mscb = sb("mscb", [128, 1024], BF16)
    c.cscr = sb("cscr", [64, 4096], BF16)
    c.kT = sb("kT", [64, 1024], BF16)
    c.qT = sb("qT", [64, 1024], BF16)
    c.qeT = sb("qeT", [64, 1024], BF16)
    c.kbT = sb("kbT", [64, 1024], BF16)
    c.ltbuf = sb("ltbuf", [64, 1024], F32)
    c.ltbuf2 = sb("ltbuf2", [64, 1024], F32)
    c.pt_all = sb("pt_all", [64, 1024], BF16)
    c.pkT = sb("pkT", [64, 1024], BF16)
    c.solvp = sb("solvp", [64, 1024], BF16)
    c.ybuf = sb("ybuf", [64, 2048], BF16)
    c.kcdT = sb("kcdT", [64, 1024], BF16)
    c.vnew = sb("vnew", [64, 1024], BF16)
    c.at_all = sb("at_all", [64, 1024], BF16)
    c.s_all = sb("s_all", [64, 1024], F32)
    c.s_b = sb("s_b", [64, 1024], BF16)
    c.och = sb("och", [64, 1024], BF16)
    c.otb = sb("otb", [128, NCH * LPB], BF16)
    c.bslot = [sb("bslot0", [128, 4096], BF16), sb("bslot1", [128, 4096], BF16)]
    c.msc = sb("msc", [128, 3 * 512], F32)
    xnq = sb("xnq_d", [128, NCH * LP], BF16)
    ldf = pg.inc(nc.sync.dma_start(out=c.identf[:], in_=c.cfid[:, :]), "dc7", 16)
    c.ms["identf_ms"] = ldf

    def t64(src_ap, dst_ap, idx, extra=None, sbase=0):
        s = idx % 3
        stg = (c.ps_t[0:64, 0:64],
               c.ps_s[0:64, 1024:1056].bitcast(BF16),
               c.ps_s[0:64, 1536:1568].bitcast(BF16))[s]
        pg.wait(TE, c.ms.get(("t64", s)), extra,
                c.ms.get("ps_t_free") if s == 0 else None)
        tr = pg.inc(nc.tensor.transpose(
            stg, src_ap, c.identb[sbase:sbase + 64, sbase:sbase + 64]), "tns")
        eng = V if idx % 2 == 0 else S
        pg.wait(eng, tr)
        cp = pg.inc((eng.tensor_copy if eng is V else eng.copy)(dst_ap, stg),
                    "vec" if eng is V else "act")
        c.ms[("t64", s)] = cp
        if s == 0:
            c.ms["ps_t_free"] = cp
        return cp

    wba = pg.inc(nc.gpsimd.dma_start(
        out=c.wbas[:], in_=c.wdram["l1b_ba"][0].rearrange("p kc m -> p (kc m)")), "dba", 16)
    dtb = pg.inc(nc.sync.dma_start(out=c.dtb[:], in_=c.wdram["l1b_dtb"][:, :]), "ddt", 16)
    alg = pg.inc(nc.sync.dma_start(out=c.alog[:], in_=c.wdram["l1b_alog"][:, :]), "dal", 16)
    nwl = pg.inc(nc.sync.dma_start(out=c.nwcol[:], in_=c.wdram["l1b_nw"][:, :]), "dnw", 16)
    c2 = pg.inc(nc.sync.dma_start(out=c.cf2[:], in_=c.cf2d[:, :]), "dc6", 16)
    pg.wait(S, alg)
    ea = pg.inc(S.activation(c.nega[:], c.alog[:], AF.Exp), "act")
    pg.wait(V, ea)
    nega = pg.inc(V.tensor_scalar(c.nega[:], c.nega[:], -1.0, None, ALU.mult), "vec")
    pg.wait(V, c.ms.get("onecol16_init"))
    one16 = pg.inc(V.memset(c.one16[:], 1.0), "vec")
    c.ms["onecol16_init"] = one16
    eps6 = pg.inc(V.memset(c.eps6[:], 1e-6), "vec")
    eps5 = pg.inc(V.memset(c.eps5[:], 1e-5), "vec")
    pg.wait(V, c.ms.get("s_user"))
    pg.inc(V.memset(c.s_all[:], 0.0), "vec")
    sb0 = pg.inc(V.memset(c.s_b[:], 0.0), "vec")
    s_ready = sb0

    for q in range(npass):
        t0 = q * LP
        ln_ms = emit_ln(pg, c, T, t0, LP, resid_ms, xnq, LP, dst_t0=0)
        wd = c.wdram["l1b_w"]
        wr = {0: load_w_group(pg, c, wd, 0, NCH, 512, 0, buf=c.bslot, key="bs", sem="dq")}
        for g in range(8):
            sl = g % 2
            if g + 1 < 8:
                wr[g + 1] = load_w_group(pg, c, wd, g + 1, NCH, 512, (g + 1) % 2,
                                         buf=c.bslot, key="bs", sem="dq")
            wv = c.bslot[sl][:, 0:4096].rearrange("p (kc m) -> p kc m", m=512)
            for mi in range(4):
                m = g * 4 + mi
                par = m % 2
                psum = c.ps_a if par == 0 else c.ps_b
                pg.wait(TE, wr[g], ln_ms, c.ms.get("pp_free%d" % par))
                for kc in range(NCH):
                    mm = nc.tensor.matmul(psum[:, 0:LP], wv[:, kc, mi * 128:(mi + 1) * 128],
                                          xnq[:, kc * LP:(kc + 1) * LP],
                                          start=(kc == 0), stop=(kc == NCH - 1))
                mm_ms = pg.inc(mm, "tns")
                c.ms[("bs_user", sl)] = mm_ms
                tensor_i, ch = divmod(m, 8)
                dstbuf = (c.q_tok, c.k_tok, c.v_tok, c.g_tok)[tensor_i]
                pg.wait(S, mm_ms, c.ms.get("vt_free"))
                ev = pg.inc(S.activation(c.vtmp2[:, 0:LP], psum[:, 0:LP], AF.Identity), "act")
                c.ms["pp_free%d" % par] = ev
                for st in range(nstr):
                    pg.wait(TE, ev, c.ms.get("ps_t_free"))
                    tr = pg.inc(nc.tensor.transpose(
                        c.ps_t[:], c.vtmp2[:, st * 128:(st + 1) * 128], c.identb), "tns")
                    pg.wait(V, tr)
                    cp = pg.inc(V.tensor_copy(
                        dstbuf[:, st * 1024 + ch * 128: st * 1024 + (ch + 1) * 128],
                        c.ps_t[:]), "vec")
                    c.ms["ps_t_free"] = cp
                c.ms["vt_free"] = pg.at("tns")
        pg.wait(TE, wba, ln_ms, c.ms.get("dbps_free"))
        for kc in range(NCH):
            mm = nc.tensor.matmul(c.ps_s[0:64, 0:LP],
                                  c.wbas[:, 0:512].rearrange("p (kc m) -> p kc m", m=64)[:, kc, :],
                                  xnq[:, kc * LP:(kc + 1) * LP],
                                  start=(kc == 0), stop=(kc == NCH - 1))
        ba_ms = pg.inc(mm, "tns")
        pg.wait(S, ba_ms, dtb, c.ms.get("dbrows_user"))
        bet = pg.inc(S.activation(c.dbrows[32:48, 0:LP], c.ps_s[0:16, 0:LP], AF.Sigmoid), "act")
        aid = pg.inc(S.activation(c.dsc[0:16, 0:LP], c.ps_s[32:48, 0:LP], AF.Identity,
                                  bias=c.dtb[:, 0:1]), "act")
        c.ms["dbps_free"] = aid
        c.ms[("scanps", 0)] = [c.ms.get(("scanps", 0)), bet, aid]
        pg.wait(S, aid)
        ab = pg.inc(S.activation(c.dsc[0:16, LP:2 * LP], c.dsc[0:16, 0:LP], AF.Abs), "act")
        pg.wait(S, ab)
        ex = pg.inc(S.activation(c.dsc[0:16, LP:2 * LP], c.dsc[0:16, LP:2 * LP], AF.Exp,
                                 scale=-1.0), "act")
        pg.wait(S, ex, one16)
        ln1 = pg.inc(S.activation(c.dsc[0:16, LP:2 * LP], c.dsc[0:16, LP:2 * LP], AF.Ln,
                                  bias=c.one16[:, 0:1]), "act")
        rl = pg.inc(S.activation(c.dsc[0:16, 0:LP], c.dsc[0:16, 0:LP], AF.Relu), "act")
        pg.wait(V, ln1, rl, nega)
        sp = pg.inc(V.tensor_tensor(c.dsc[0:16, 0:LP], c.dsc[0:16, 0:LP],
                                    c.dsc[0:16, LP:2 * LP], ALU.add), "vec")
        pg.wait(V, sp)
        dec = pg.inc(V.tensor_scalar(c.dbrows[0:16, 0:LP], c.dsc[0:16, 0:LP],
                                     c.nega[:, 0:1], None, ALU.mult), "vec")
        pg.wait(V, bet, dec)
        bd_ms = pg.at("vec")
        for st in range(nstr):
            pg.wait(TE, bd_ms, c2, c.ms.get("ps_t_free"), c.ms.get("dbcols_user"),
                    c.ms.get("identf_ms"))
            tr = pg.inc(nc.tensor.transpose(c.ps_o[:, 0:64], c.dbrows[:, st * 128:(st + 1) * 128],
                                            c.identf[0:64, 0:64]), "tns")
            pg.wait(V, tr)
            dbc = pg.inc(V.tensor_copy(c.dbcols[:, st * 64:(st + 1) * 64], c.ps_o[:, 0:64]), "vec")
            c.ms["ps_t_free"] = dbc
            for half in range(2):
                ch = st * 2 + half
                pg.wait(TE, dbc, c.ms.get("gc_ps_free"), c.ms.get("pp_free1"))
                gmm = pg.inc(nc.tensor.matmul(
                    c.ps_b[0:64, 0:16], c.cf2[half * 64:(half + 1) * 64 - 64 + 64, 0:64]
                    if False else c.cf2[half * 64: half * 64 + 64, 0:64],
                    c.dbcols[half * 64:(half + 1) * 64, st * 64: st * 64 + 16],
                    start=True, stop=True), "tns")
                pg.wait(V, gmm)
                gcp = pg.inc(V.tensor_copy(
                    c.gcols[half * 64:(half + 1) * 64, st * 16:(st + 1) * 16],
                    c.ps_b[0:64, 0:16]), "vec")
                c.ms["gc_ps_free"] = gcp
        gc_ms = pg.at("vec")
        c.ms["gl_ps_free"] = [c.ms.get("gl_ps_free"), gc_ms]
        pg.wait(S, gc_ms)
        ega = pg.inc(S.activation(c.egcols[:, 0:nstr * 16], c.gcols[:, 0:nstr * 16], AF.Exp), "act")
        for st in range(nstr):
            pg.wait(TE, gc_ms, c.ms.get("gr_ps_free"))
            tr = pg.inc(nc.tensor.transpose(c.ps_o[0:16, 0:128],
                                            c.gcols[:, st * 16:(st + 1) * 16], c.identf[:, :]), "tns")
            pg.wait(V, tr)
            grc = pg.inc(V.tensor_copy(c.grows[:, st * 128:(st + 1) * 128], c.ps_o[0:16, 0:128]), "vec")
            c.ms["gr_ps_free"] = grc
        gr_ms = pg.at("vec")
        pg.wait(nc.sync, gr_ms, c.ms.get("gscr_read"))
        gw = pg.inc(nc.sync.dma_start(out=c.g_scr[:, :], in_=c.grows[0:16, 0:LP]), "dgw", 16)

        pg.wait(V, ega, gc_ms)
        bcast_cols = pg.inc(V.tensor_copy(
            c.colsb[:, 0:nstr * 16].rearrange("p (s x) -> p s x", x=16),
            c.dbcols[:, 0:nstr * 64].rearrange("p (s x) -> p s x", x=64)[:, :, 32:48]), "vec")
        egb = pg.inc(V.tensor_copy(c.colsb[:, 64:64 + nstr * 16], c.egcols[:, 0:nstr * 16]), "vec")
        pg.wait(V, bcast_cols, egb)
        beg = pg.inc(V.tensor_tensor(c.colsb[:, 32:32 + nstr * 16],
                                     c.colsb[:, 0:nstr * 16],
                                     c.colsb[:, 64:64 + nstr * 16], ALU.mult), "vec")
        pg.wait(V, beg)
        for st in range(nstr):
            for which, buf in (("q", c.q_tok), ("k", c.k_tok)):
                sl = buf[:, st * 1024:(st + 1) * 1024]
                pg.wait(S, pg.at("vec"), c.ms.get("msc_l2_free"))
                sq = pg.inc(S.activation(c.mscb[:, 0:1024], sl, AF.Square), "act")
                pg.wait(V, sq)
                rd = pg.inc(V.tensor_reduce(c.l2c[:, 0:16], seg3(c.mscb[:, 0:1024]),
                                            AX.X, ALU.add), "vec")
                c.ms["msc_l2_free"] = rd
                pg.wait(S, rd, eps6)
                sr = pg.inc(S.activation(c.l2c[:, 16:32], c.l2c[:, 0:16], AF.Sqrt,
                                         bias=c.eps6[:, 0:1]), "act")
                pg.wait(V, sr)
                rc = pg.inc(V.reciprocal(c.l2c[:, 0:16], c.l2c[:, 16:32]), "vec")
                pg.wait(V, rc)
                if which == "q":
                    rc = pg.inc(V.tensor_scalar(c.l2c[:, 0:16], c.l2c[:, 0:16], 0.125,
                                                None, ALU.mult), "vec")
                    pg.wait(V, rc)
                cb_ = pg.inc(V.tensor_copy(c.l2cb[:, 0:16], c.l2c[:, 0:16]), "vec")
                pg.wait(V, cb_)
                nm = pg.inc(V.tensor_tensor(seg3(sl), seg3(sl), segb(c.l2cb[:, 0:16], 16),
                                            ALU.mult), "vec")
            vsl = c.v_tok[:, st * 1024:(st + 1) * 1024]
            pg.wait(V, pg.at("vec"))
            pg.inc(V.tensor_tensor(seg3(vsl), seg3(vsl),
                                   segb(c.colsb[:, st * 16:(st + 1) * 16], 16), ALU.mult), "vec")
        phn_ms = pg.at("vec")

        def issue_gfr(j):
            sl = j % 2
            pg.wait(nc.sync, gw, c.ms.get(("gflatu", sl)))
            ms = pg.inc(nc.sync.dma_start(
                out=c.gflat[0:1, sl * 1024:(sl + 1) * 1024].rearrange("p (h t) -> p h t", t=64),
                in_=c.g_scr.rearrange("h (cc t) -> cc h t", t=64)[j:j + 1]), "dgr%d" % sl, 16)
            c.ms[("gfr", j)] = ms
            return ms

        issue_gfr(0)
        for cc in range(LP // 64):
            st = cc // 2
            half = cc % 2
            pb = half * 64
            gsl = cc % 2
            gview = c.gflat[0:1, gsl * 1024:(gsl + 1) * 1024]
            ctok = lambda buf: buf[:, st * 1024:(st + 1) * 1024]
            if cc + 1 < LP // 64:
                issue_gfr(cc + 1)
            gfr = c.ms.pop(("gfr", cc))
            c.ms["gscr_read"] = [c.ms.get("gscr_read") if cc == 0 else None, gfr]
            pg.wait(TE, gfr, c.ms.get("lt_ps_free"), c.ms.get(("t64", 1)),
                    c.ms.get(("t64", 2)), c.ms.get(("pktst", 0)), c.ms.get(("pktst", 1)))
            nc.tensor.matmul(c.ps_s[0:64, 1024:1536], c.onesf[0:1, 0:64],
                             gview[:, 0:512], start=True, stop=True)
            lmm = pg.inc(nc.tensor.matmul(c.ps_s[0:64, 1536:2048], c.onesf[0:1, 0:64],
                                          gview[:, 512:1024], start=True, stop=True), "tns")
            gseg = segb(c.gcols[pb:pb + 64, st * 16:(st + 1) * 16], 16)
            stseg = c.cf2[0:64, 64:128].unsqueeze(1).broadcast_to((64, 16, 64))
            pg.wait(V, lmm, c.ms.get("msc_lt_free"))
            d1 = pg.inc(V.tensor_tensor(seg3(c.msc[0:64, 0:1024]),
                                        seg3(c.ps_s[0:64, 1024:2048]),
                                        gseg, ALU.subtract), "vec")
            pg.wait(V, d1)
            d2 = pg.inc(V.tensor_tensor(seg3(c.msc[0:64, 0:1024]), seg3(c.msc[0:64, 0:1024]),
                                        stseg, ALU.mult), "vec")
            pg.wait(S, d2)
            d3 = pg.inc(S.activation(c.msc[0:64, 0:1024], c.msc[0:64, 0:1024], AF.Exp), "act")
            pg.wait(V, d3)
            lt_ms = pg.inc(V.tensor_tensor(seg3(c.ltbuf[0:64, 0:1024]), seg3(c.msc[0:64, 0:1024]),
                                           stseg, ALU.mult), "vec")
            c.ms["msc_lt_free"] = lt_ms
            eyeseg = c.cf2[0:64, 128:192].unsqueeze(1).broadcast_to((64, 16, 64))
            pg.wait(V, lt_ms, c.ms.get("lt2_user"))
            lt2_ms = pg.inc(V.tensor_tensor(seg3(c.ltbuf2[0:64, 0:1024]),
                                            seg3(c.ltbuf[0:64, 0:1024]),
                                            eyeseg, ALU.add), "vec")
            pg.wait(TE, gfr, c.ms.get("gl_ps_free"), c.ms.get("pp_free1"))
            glmm = pg.inc(nc.tensor.matmul(
                c.ps_b[0:64, 16:32], c.onesf[0:1, 0:64],
                gview.rearrange("p (h t) -> p h t", t=64)[:, :, 63:64].squeeze(2),
                start=True, stop=True), "tns")
            pg.wait(V, glmm, c.ms.get("ecols_user"))
            ei = pg.inc(V.tensor_tensor(c.ecols[0:64, 0:16],
                                        c.ps_b[0:64, 16:32],
                                        c.gcols[pb:pb + 64, st * 16:(st + 1) * 16],
                                        ALU.subtract), "vec")
            pg.wait(S, ei)
            ei2 = pg.inc(S.activation(c.ecols[0:64, 0:16], c.ecols[0:64, 0:16], AF.Exp), "act")
            egl = pg.inc(S.activation(c.ecols[0:64, 16:32], c.ps_b[0:64, 16:32], AF.Exp), "act")
            c.ms["gl_ps_free"] = egl
            c.ms["sv_ps_free1"] = [c.ms.get("sv_ps_free1"), egl]
            c.ms["sv_ps_free0"] = [c.ms.get("sv_ps_free0"), c.ms.get("pp_free0")]
            c.ms["pt_ps_free0"] = [c.ms.get("pt_ps_free0"), c.ms.get("pp_free0")]
            c.ms["pt_ps_free1"] = [c.ms.get("pt_ps_free1"), c.ms.get("pp_free1"), egl]
            c.ms["at_ps_free0"] = [c.ms.get("at_ps_free0"), c.ms.get("pp_free0")]
            c.ms["at_ps_free1"] = [c.ms.get("at_ps_free1"), c.ms.get("pp_free1"), egl]
            bseg = segb(c.colsb[pb:pb + 64, st * 16:(st + 1) * 16], 16)
            egseg = segb(c.colsb[pb:pb + 64, 64 + st * 16: 64 + (st + 1) * 16], 16)
            begseg = segb(c.colsb[pb:pb + 64, 32 + st * 16: 32 + (st + 1) * 16], 16)
            pg.wait(V, phn_ms, ei2, egl, c.ms.get("cscr_user"))
            eib = pg.inc(V.tensor_copy(c.ecolsb[pb:pb + 64, 0:16], c.ecols[0:64, 0:16]), "vec")
            eglb = pg.inc(V.tensor_copy(c.ecolsb[0:64, 16:32], c.ecols[0:64, 16:32]), "vec")
            kb = pg.inc(V.tensor_tensor(seg3(c.cscr[0:64, 0:1024]),
                                        seg3(ctok(c.k_tok)[pb:pb + 64, :]),
                                        bseg, ALU.mult), "vec")
            kbe = pg.inc(V.tensor_tensor(seg3(c.cscr[0:64, 1024:2048]),
                                         seg3(ctok(c.k_tok)[pb:pb + 64, :]),
                                         begseg, ALU.mult), "vec")
            eiseg = segb(c.ecolsb[pb:pb + 64, 0:16], 16)
            pg.wait(V, eib)
            ke1 = pg.inc(V.tensor_tensor(seg3(c.cscr[0:64, 2048:3072]),
                                         seg3(ctok(c.k_tok)[pb:pb + 64, :]),
                                         eiseg, ALU.mult), "vec")
            qe = pg.inc(V.tensor_tensor(seg3(c.cscr[0:64, 3072:4096]),
                                        seg3(ctok(c.q_tok)[pb:pb + 64, :]),
                                        egseg, ALU.mult), "vec")
            pg.wait(TE, kb, qe, phn_ms, lt_ms, c.ms.get("fm_user"))
            ii = 0
            for h in range(H):
                for srcb, dstc, sbase in (
                        (ctok(c.k_tok)[pb:pb + 64, h * 64:(h + 1) * 64], c.kT, pb),
                        (ctok(c.q_tok)[pb:pb + 64, h * 64:(h + 1) * 64], c.qT, pb),
                        (c.cscr[0:64, 3072 + h * 64: 3072 + (h + 1) * 64], c.qeT, 0),
                        (c.cscr[0:64, h * 64:(h + 1) * 64], c.kbT, 0)):
                    t64(srcb, dstc[0:64, h * 64:(h + 1) * 64], ii, sbase=sbase)
                    ii += 1
            fm_ms = [pg.at("vec"), pg.at("act")]
            for hg in range(4):
                pg.wait(TE, fm_ms, lt_ms, c.ms.get("pt_ps_free%d" % (hg % 2)))
                for hh in range(4):
                    h = hg * 4 + hh
                    mmp = nc.tensor.matmul(
                        (c.ps_a if hg % 2 == 0 else c.ps_b)[0:64, hh * 64:(hh + 1) * 64],
                        c.kT[0:64, h * 64:(h + 1) * 64],
                        c.kbT[0:64, h * 64:(h + 1) * 64], start=True, stop=True)
                mmp_ms = pg.inc(mmp, "tns")
                pg.wait(V, mmp_ms, c.ms.get("ptall_user"))
                ptc = pg.inc(V.tensor_tensor(
                    c.pt_all[0:64, hg * 256:(hg + 1) * 256],
                    (c.ps_a if hg % 2 == 0 else c.ps_b)[0:64, 0:256],
                    c.ltbuf[0:64, hg * 256:(hg + 1) * 256], ALU.mult), "vec")
                c.ms["pt_ps_free%d" % (hg % 2)] = ptc
            pt_ms = pg.at("vec")
            pg.wait(V, pt_ms, c.ms.get("solv_user"))
            yview = c.ybuf[0:64, :].rearrange("p (h two d) -> p h two d", two=2, d=64)
            yi1 = pg.inc(V.tensor_copy(yview[:, :, 0:1, :].squeeze(2),
                                       seg3(ctok(c.v_tok)[pb:pb + 64, :])), "vec")
            yi2 = pg.inc(V.tensor_copy(yview[:, :, 1:2, :].squeeze(2),
                                       seg3(c.cscr[0:64, 1024:2048])), "vec")
            for h in range(H):
                t64(c.pt_all[0:64, h * 64:(h + 1) * 64],
                    c.solvp[0:64, h * 64:(h + 1) * 64], h, extra=pt_ms)
            pg.wait(V, pg.at("act"))
            pg.inc(V.tensor_copy(c.pkT[0:64, :], c.pt_all[0:64, :]), "vec")
            sv_ms = pg.at("vec")
            svreg = [c.ps_a[0:64, 0:192], c.ps_b[0:64, 0:192],
                     c.ps_s[0:64, 0:192], c.ps_s[0:64, 512:704]]
            for it in range(6):
                lastiter = (it == 5)
                for h in range(H):
                    par = h % 4
                    psum = svreg[par]
                    pg.wait(TE, sv_ms, yi1, yi2, c.ms.get("sv_ps_free%d" % par),
                            c.ms.get(("scanps", par - 2)) if par >= 2 else None)
                    if not lastiter:
                        nc.tensor.matmul(psum[:, 0:64], c.pkT[0:64, h * 64:(h + 1) * 64],
                                         c.solvp[0:64, h * 64:(h + 1) * 64],
                                         start=True, stop=True)
                    smm = pg.inc(nc.tensor.matmul(
                        psum[:, 64:192], c.pkT[0:64, h * 64:(h + 1) * 64],
                        c.ybuf[0:64, h * 128:(h + 1) * 128], start=True, stop=True), "tns")
                    pg.wait(V, smm)
                    yup = pg.inc(V.tensor_tensor(c.ybuf[0:64, h * 128:(h + 1) * 128],
                                                 c.ybuf[0:64, h * 128:(h + 1) * 128],
                                                 psum[:, 64:192], ALU.add), "vec")
                    if not lastiter:
                        pk = pg.inc(V.tensor_copy(c.solvp[0:64, h * 64:(h + 1) * 64],
                                                  psum[:, 0:64]), "vec")
                        c.ms["sv_ps_free%d" % par] = pk
                        c.ms[("pkready", h)] = pk
                    else:
                        c.ms["sv_ps_free%d" % par] = yup
                if not lastiter:
                    pkstg = [c.ps_s[0:64, 1024:1056].bitcast(BF16),
                             c.ps_s[0:64, 1536:1568].bitcast(BF16)]
                    for h in range(H):
                        ss = h % 2
                        pg.wait(TE, c.ms.get(("pkready", h)), c.ms.get(("pktst", ss)), lt_ms)
                        tr = pg.inc(nc.tensor.transpose(pkstg[ss],
                                                        c.solvp[0:64, h * 64:(h + 1) * 64],
                                                        c.identb[0:64, 0:64]), "tns")
                        eng = V if h % 2 == 0 else S
                        pg.wait(eng, tr)
                        cp = pg.inc((eng.tensor_copy if eng is V else eng.copy)(
                            c.pkT[0:64, h * 64:(h + 1) * 64], pkstg[ss]),
                            "vec" if eng is V else "act")
                        c.ms[("pktst", ss)] = cp
                    sv_ms = [pg.at("vec"), pg.at("act")]
            y_ms = pg.at("vec")
            for h in range(H):
                t64(c.ybuf[0:64, h * 128 + 64: (h + 1) * 128],
                    c.kcdT[0:64, h * 64:(h + 1) * 64], h, extra=y_ms)
            kcd_ms = [pg.at("vec"), pg.at("act")]
            for hg in range(4):
                pg.wait(TE, kcd_ms, s_ready, c.ms.get(("scanps", hg)))
                for hh in range(4):
                    h = hg * 4 + hh
                    mmw = nc.tensor.matmul(
                        c.ps_s[0:64, hg * 512 + hh * 64: hg * 512 + (hh + 1) * 64],
                        c.kcdT[0:64, h * 64:(h + 1) * 64],
                        c.s_b[0:64, h * 64:(h + 1) * 64],
                        start=True, stop=True)
                w_ms = pg.inc(mmw, "tns")
                pg.wait(V, w_ms, y_ms, c.ms.get("vnew_user"))
                vn = pg.inc(V.tensor_tensor(
                    seg3(c.vnew[0:64, hg * 256:(hg + 1) * 256]),
                    c.ybuf[0:64, :].rearrange("p (h two d) -> p h two d", two=2, d=64)[
                        :, hg * 4:(hg + 1) * 4, 0:1, :].squeeze(2),
                    seg3(c.ps_s[0:64, hg * 512:hg * 512 + 256]), ALU.subtract), "vec")
            vn_ms = pg.at("vec")
            for hg in range(4):
                par = hg % 2
                pg.wait(TE, fm_ms, c.ms.get("at_ps_free%d" % par))
                for hh in range(4):
                    h = hg * 4 + hh
                    mma = nc.tensor.matmul((c.ps_a if par == 0 else c.ps_b)[0:64, hh * 64:(hh + 1) * 64],
                                           c.kT[0:64, h * 64:(h + 1) * 64],
                                           c.qT[0:64, h * 64:(h + 1) * 64],
                                           start=True, stop=True)
                at_ms = pg.inc(mma, "tns")
                pg.wait(V, at_ms, lt2_ms, c.ms.get("atall_user"))
                atc = pg.inc(V.tensor_tensor(
                    c.at_all[0:64, hg * 256:(hg + 1) * 256],
                    (c.ps_a if par == 0 else c.ps_b)[0:64, 0:256],
                    c.ltbuf2[0:64, hg * 256:(hg + 1) * 256], ALU.mult), "vec")
                c.ms["at_ps_free%d" % par] = atc
            at_all_ms = pg.at("vec")
            for hg in range(4):
                pg.wait(TE, at_all_ms, vn_ms, s_ready)
                for hh in range(4):
                    h = hg * 4 + hh
                    off = hg * 512 + hh * 64
                    nc.tensor.matmul(c.ps_s[0:64, off:off + 64],
                                     c.qeT[0:64, h * 64:(h + 1) * 64],
                                     c.s_b[0:64, h * 64:(h + 1) * 64],
                                     start=True, stop=False)
                    mo = nc.tensor.matmul(c.ps_s[0:64, off:off + 64],
                                          c.at_all[0:64, h * 64:(h + 1) * 64],
                                          c.vnew[0:64, h * 64:(h + 1) * 64],
                                          start=False, stop=True)
                mo_ms = pg.inc(mo, "tns")
                pg.wait(V, mo_ms, c.ms.get("och_user"))
                oc = pg.inc(V.tensor_copy(c.och[0:64, hg * 256:(hg + 1) * 256],
                                          c.ps_s[0:64, hg * 512:hg * 512 + 256]), "vec")
                c.ms[("scanps", hg)] = oc
            och_ms = pg.at("vec")
            pg.wait(TE, vn_ms, ke1, och_ms, c.ms.get("supd_ps_free"),
                    c.ms.get(("t64", 1)), c.ms.get(("t64", 2)),
                    c.ms.get(("pktst", 0)), c.ms.get(("pktst", 1)))
            for h in range(H):
                msu = nc.tensor.matmul(c.ps_s[0:64, 1024 + h * 64: 1024 + (h + 1) * 64],
                                       c.cscr[0:64, 2048 + h * 64: 2048 + (h + 1) * 64],
                                       c.vnew[0:64, h * 64:(h + 1) * 64],
                                       start=True, stop=True)
            su_ms = pg.inc(msu, "tns")
            eglseg = segb(c.ecolsb[0:64, 16:32], 16)
            pg.wait(V, su_ms, eglb, och_ms)
            s1 = pg.inc(V.tensor_tensor(seg3(c.s_all[0:64, :]), seg3(c.s_all[0:64, :]),
                                        eglseg, ALU.mult), "vec")
            pg.wait(V, s1)
            s2 = pg.inc(V.tensor_tensor(c.s_all[0:64, :], c.s_all[0:64, :],
                                        c.ps_s[0:64, 1024:2048], ALU.add), "vec")
            c.ms["supd_ps_free"] = s2
            c.ms["lt_ps_free"] = s2
            pg.wait(V, s2)
            s_ready = pg.inc(V.tensor_copy(c.s_b[0:64, :], c.s_all[0:64, :]), "vec")
            for _hg in range(4):
                c.ms[("scanps", _hg)] = [c.ms.get(("scanps", _hg)), s2]
            c.ms["lt2_user"] = pg.at("tns")
            c.ms["ecols_user"] = s_ready
            c.ms["pp_free0"] = s_ready
            c.ms["pp_free1"] = s_ready
            c.ms["bcast_free"] = s_ready
            c.ms["cscr_user"] = pg.at("tns")
            c.ms["vnew_user"] = pg.at("tns")
            c.ms["fm_user"] = pg.at("tns")
            c.ms["solv_user"] = pg.at("tns")
            c.ms["ptall_user"] = pg.at("tns")
            c.ms["atall_user"] = pg.at("tns")
            c.ms[("gflatu", gsl)] = pg.at("tns")
            pg.wait(S, och_ms, c.ms.get("msc_l2_free"))
            sqo = pg.inc(S.activation(c.mscb[0:64, 0:1024], c.och[0:64, :], AF.Square), "act")
            pg.wait(V, sqo)
            rdo = pg.inc(V.tensor_reduce(c.l2c[0:64, 0:16], seg3(c.mscb[0:64, 0:1024]),
                                         AX.X, ALU.add), "vec")
            pg.wait(V, rdo)
            sc1 = pg.inc(V.tensor_scalar(c.l2c[0:64, 0:16], c.l2c[0:64, 0:16], 1.0 / 64,
                                         None, ALU.mult), "vec")
            pg.wait(S, sc1, eps5)
            sr = pg.inc(S.activation(c.l2c[0:64, 16:32], c.l2c[0:64, 0:16], AF.Sqrt,
                                     bias=c.eps5[0:64, 0:1]), "act")
            pg.wait(V, sr)
            rco = pg.inc(V.reciprocal(c.l2c[0:64, 0:16], c.l2c[0:64, 16:32]), "vec")
            pg.wait(V, rco)
            rcb = pg.inc(V.tensor_copy(c.l2cb[0:64, 0:16], c.l2c[0:64, 0:16]), "vec")
            rseg = segb(c.l2cb[0:64, 0:16], 16)
            pg.wait(S, phn_ms, rdo)
            sgt = pg.inc(S.activation(c.cscr[0:64, 0:1024], ctok(c.g_tok)[pb:pb + 64, :],
                                      AF.Silu), "act")
            c.ms["msc_l2_free"] = sgt
            pg.wait(V, rcb, sgt)
            o1 = pg.inc(V.tensor_tensor(seg3(c.och[0:64, :]), seg3(c.och[0:64, :]),
                                        rseg, ALU.mult), "vec")
            pg.wait(V, o1)
            o2 = pg.inc(V.tensor_tensor(c.och[0:64, :], c.och[0:64, :],
                                        c.cscr[0:64, 0:1024], ALU.mult), "vec")
            c.ms["cscr_user"] = [c.ms["cscr_user"], o2]
            for h in range(H):
                t64(c.och[0:64, h * 64:(h + 1) * 64],
                    c.otb[(h % 2) * 64:(h % 2) * 64 + 64,
                          (h // 2) * LP + cc * 64: (h // 2) * LP + (cc + 1) * 64],
                    h, extra=[o2, c.ms.get("ot_user")])
            c.ms["och_user"] = pg.at("tns")
        ot_ms = [pg.at("vec"), pg.at("act")]
        wod = c.wdram["l1b_wo"]
        for g2 in range(2):
            wms = load_w_group(pg, c, wod, g2, NCH, 512, g2 % 2, buf=c.bslot, key="bs", sem="dq")
            pg.wait(V, wms, nwl)
            wsc = pg.inc(V.tensor_scalar(c.bslot[g2 % 2][:, 0:4096], c.bslot[g2 % 2][:, 0:4096],
                                         c.nwcol[:, 0:1], None, ALU.mult), "vec")
            wv = c.bslot[g2 % 2][:, 0:4096].rearrange("p (kc m) -> p kc m", m=512)
            for mi in range(4):
                m = g2 * 4 + mi
                par = m % 2
                psum = c.ps_a if par == 0 else c.ps_b
                pg.wait(TE, wsc, ot_ms, c.ms.get("pp_free%d" % par),
                        c.ms.get("sv_ps_free0"), c.ms.get("sv_ps_free1"),
                        c.ms.get("at_ps_free0"), c.ms.get("at_ps_free1"))
                for kc in range(NCH):
                    mm = nc.tensor.matmul(psum[:, 0:LP], wv[:, kc, mi * 128:(mi + 1) * 128],
                                          c.otb[:, kc * LP: (kc + 1) * LP],
                                          start=(kc == 0), stop=(kc == NCH - 1))
                mm_ms = pg.inc(mm, "tns")
                c.ms[("bs_user", g2 % 2)] = mm_ms
                pg.wait(V, mm_ms, resid_ms, c.ms.get("sc_free"))
                sv = pg.inc(V.tensor_copy(c.sc[:, 0:LP], psum[:, 0:LP]), "vec")
                c.ms["pp_free%d" % par] = sv
                pg.wait(V, sv)
                ra = pg.inc(V.tensor_tensor(c.resid[:, m * T + t0: m * T + t0 + LP],
                                            c.resid[:, m * T + t0: m * T + t0 + LP],
                                            c.sc[:, 0:LP], ALU.add), "vec")
                c.ms["sc_free"] = ra
        c.ms["ot_user"] = pg.at("tns")
    c.ms["s_user"] = pg.at("tns")
    pg.barrier()
    es.close()
    return pg.at("vec")


def emit_mlp(pg, c, T, resid_ms):
    nc = pg.nc
    V, S, TE = nc.vector, nc.scalar, nc.tensor
    NT = 512
    ntiles = T // NT
    es = contextlib.ExitStack()
    _u = "_%d" % pg.uid()
    sb = lambda name, shape, dt: es.enter_context(nc.sbuf_tensor(name + _u, shape, dt))
    c.mslot = [sb("mslot%d" % i, [128, 4096], BF16) for i in range(4)]
    c.hbuf1 = sb("hbuf1", [128, 32 * 512], BF16)
    c.hbuf2 = sb("hbuf2", [128, 32 * 512], BF16)
    c.l1fs = sb("l1fs", [128, 136], F32)
    c.msc = sb("msc", [128, 3 * 512], F32)
    xnq = sb("xnq", [128, NCH * NT], BF16)
    sload = pg.inc(nc.sync.dma_start(out=c.l1fs[:], in_=c.wdram["l1f_s"][:, :]), "dfs", 16)

    def scol(which, m):
        off = {"w1": 0, "w2": 32, "h1": 64, "h2": 96, "w3": 128}[which]
        return c.l1fs[:, off + m: off + m + 1]

    for q in range(ntiles):
        t0 = q * NT
        ln_ms = emit_ln(pg, c, T, t0, NT, resid_ms, xnq, NT, dst_t0=0)
        w1d, w2d = c.wdram["l1f_w1"], c.wdram["l1f_w2"]
        wr = {0: (load_w_group(pg, c, w1d, 0, 8, 512, 0, buf=c.mslot, key="ms", sem="dm"),
                  load_w_group(pg, c, w2d, 0, 8, 512, 1, buf=c.mslot, key="ms", sem="dm"))}
        for g in range(8):
            sl = (g % 2) * 2
            if g + 1 < 8:
                nsl = ((g + 1) % 2) * 2
                wr[g + 1] = (
                    load_w_group(pg, c, w1d, g + 1, 8, 512, nsl, buf=c.mslot, key="ms", sem="dm"),
                    load_w_group(pg, c, w2d, g + 1, 8, 512, nsl + 1, buf=c.mslot, key="ms", sem="dm"))
            v1 = c.mslot[sl][:, 0:4096].rearrange("p (kc m) -> p kc m", m=512)
            v2 = c.mslot[sl + 1][:, 0:4096].rearrange("p (kc m) -> p kc m", m=512)
            for mi in range(4):
                m = g * 4 + mi
                pg.wait(TE, wr[g][0], ln_ms, c.ms.get("pp_free0"), c.ms.get("rows_read"))
                for kc in range(NCH):
                    mm1 = nc.tensor.matmul(c.ps_a[:, 0:NT], v1[:, kc, mi * 128:(mi + 1) * 128],
                                           xnq[:, kc * NT: (kc + 1) * NT],
                                           start=(kc == 0), stop=(kc == NCH - 1))
                pg.wait(TE, wr[g][1], c.ms.get("pp_free1"))
                for kc in range(NCH):
                    mm2 = nc.tensor.matmul(c.ps_b[:, 0:NT], v2[:, kc, mi * 128:(mi + 1) * 128],
                                           xnq[:, kc * NT: (kc + 1) * NT],
                                           start=(kc == 0), stop=(kc == NCH - 1))
                mm_ms = pg.inc(mm2, "tns")
                c.ms[("ms_user", sl)] = mm_ms
                c.ms[("ms_user", sl + 1)] = mm_ms
                pg.wait(S, mm_ms, sload, c.ms.get("msc0_free"))
                a1 = pg.inc(S.activation(c.msc[:, 0:NT], c.ps_a[:, 0:NT], AF.Silu,
                                         scale=scol("w1", m)), "act")
                c.ms["pp_free0"] = a1
                pg.wait(V, mm_ms, sload, c.ms.get("msc1_free"))
                a2 = pg.inc(V.tensor_scalar(c.msc[:, NT:2 * NT], c.ps_b[:, 0:NT],
                                            scol("w2", m), None, ALU.mult), "vec")
                c.ms["pp_free1"] = a2
                pg.wait(V, a1, a2)
                hv = pg.inc(V.tensor_tensor(c.hbuf1[:, m * NT:(m + 1) * NT],
                                            c.msc[:, 0:NT], c.msc[:, NT:2 * NT], ALU.mult), "vec")
                c.ms["msc0_free"] = hv
                c.ms["msc1_free"] = hv
        h_done = pg.at("vec")

        def stage(wname, src, dst, sname, src_ready):
            wd = c.wdram[wname]
            rr = {0: load_w_group(pg, c, wd, 0, 32, 128, 0, buf=c.mslot, key="ms", sem="dm")}
            last = None
            for m in range(32):
                sl2 = m % 2
                if m + 1 < 32:
                    rr[m + 1] = load_w_group(pg, c, wd, m + 1, 32, 128, (m + 1) % 2,
                                             buf=c.mslot, key="ms", sem="dm")
                wv = c.mslot[sl2][:, 0:4096].rearrange("p (kc m) -> p kc m", m=128)
                par = m % 2
                psum = c.ps_a if par == 0 else c.ps_b
                pg.wait(TE, rr[m], src_ready, c.ms.get("pp_free%d" % par))
                for kc in range(32):
                    mm = nc.tensor.matmul(psum[:, 0:NT], wv[:, kc, :],
                                          src[:, kc * NT:(kc + 1) * NT],
                                          start=(kc == 0), stop=(kc == 31))
                mm_ms = pg.inc(mm, "tns")
                c.ms[("ms_user", sl2)] = mm_ms
                pg.wait(S, mm_ms, c.ms.get("dst_war"))
                last = pg.inc(S.activation(dst[:, m * NT:(m + 1) * NT], psum[:, 0:NT],
                                           AF.Gelu, scale=scol(sname, m)), "act")
                c.ms["pp_free%d" % par] = last
            return pg.at("act")

        g1_done = stage("l1f_h1", c.hbuf1, c.hbuf2, "h1", h_done)
        c.ms["dst_war"] = pg.at("tns")
        g2_done = stage("l1f_h2", c.hbuf2, c.hbuf1, "h2", g1_done)
        c.ms["dst_war"] = None

        w3d = c.wdram["l1f_w3"]
        rr = {0: load_w_group(pg, c, w3d, 0, 32, 128, 0, buf=c.mslot, key="ms", sem="dm")}
        for m in range(NCH):
            sl2 = m % 2
            if m + 1 < NCH:
                rr[m + 1] = load_w_group(pg, c, w3d, m + 1, 32, 128, (m + 1) % 2,
                                         buf=c.mslot, key="ms", sem="dm")
            wv = c.mslot[sl2][:, 0:4096].rearrange("p (kc m) -> p kc m", m=128)
            par = m % 2
            psum = c.ps_a if par == 0 else c.ps_b
            pg.wait(TE, rr[m], g2_done, c.ms.get("pp_free%d" % par))
            for kc in range(32):
                mm = nc.tensor.matmul(psum[:, 0:NT], wv[:, kc, :],
                                      c.hbuf1[:, kc * NT:(kc + 1) * NT],
                                      start=(kc == 0), stop=(kc == 31))
            mm_ms = pg.inc(mm, "tns")
            c.ms[("ms_user", sl2)] = mm_ms
            pg.wait(V, mm_ms, sload, c.ms.get("sc_free"))
            sv = pg.inc(V.tensor_scalar(c.sc[:, 0:NT], psum[:, 0:NT],
                                        scol("w3", m), None, ALU.mult), "vec")
            c.ms["pp_free%d" % par] = sv
            pg.wait(V, sv, resid_ms)
            ra = pg.inc(V.tensor_tensor(c.resid[:, m * T + t0: m * T + t0 + NT],
                                        c.resid[:, m * T + t0: m * T + t0 + NT],
                                        c.sc[:, 0:NT], ALU.add), "vec")
            c.ms["sc_free"] = ra
    pg.barrier()
    es.close()
    return pg.at("vec")


def emit_ln(pg, c, T, t0, tn, resid_ms, dst, dstT, save_r=False, dst_t0=None):
    nc = pg.nc
    V, S, TE = nc.vector, nc.scalar, nc.tensor
    if dst_t0 is None:
        dst_t0 = t0
    mm = None
    for ch in range(NCH):
        sl = ch % 2
        rs = c.resid[:, ch * T + t0: ch * T + t0 + tn]
        pg.wait(V, resid_ms, c.ms.get(("xbslot", sl)))
        cp = pg.inc(V.tensor_copy(c.xb[:, sl * tn:(sl + 1) * tn], rs), "vec")
        pg.wait(S, cp)
        sq = pg.inc(S.activation(c.sq[:, sl * tn:(sl + 1) * tn],
                                 c.xb[:, sl * tn:(sl + 1) * tn], AF.Square), "act")
        pg.wait(TE, cp, sq)
        if ch == 0:
            pg.wait(TE, c.ms.get("pp_free0"), c.ms.get("pp_free1"), c.ms.get("rows_read"))
        nc.tensor.matmul(c.ps_a[0:1, 0:tn], c.ones_col, c.xb[:, sl * tn:(sl + 1) * tn],
                         start=(ch == 0), stop=(ch == NCH - 1), skip_group_check=True)
        mm = nc.tensor.matmul(c.ps_b[0:1, 0:tn], c.ones_col, c.sq[:, sl * tn:(sl + 1) * tn],
                              start=(ch == 0), stop=(ch == NCH - 1), skip_group_check=True)
        c.ms[("xbslot", sl)] = pg.inc(mm, "tns")
    mm_ms = pg.at("tns")
    pg.wait(V, mm_ms, c.ms.get("rows_free"))
    pg.inc(V.tensor_scalar(c.r_mean[:, 0:tn], c.ps_a[0:1, 0:tn], 1.0 / D, None, ALU.mult), "vec")
    rr = pg.inc(V.tensor_scalar(c.r_var[:, 0:tn], c.ps_b[0:1, 0:tn], 1.0 / D, None, ALU.mult), "vec")
    c.ms["rows_read"] = rr
    pg.wait(V, rr)
    pg.inc(V.tensor_tensor(c.r_t1[:, 0:tn], c.r_mean[:, 0:tn], c.r_mean[:, 0:tn], ALU.mult), "vec")
    pg.wait(V, pg.at("vec"))
    vv = pg.inc(V.tensor_tensor(c.r_var[:, 0:tn], c.r_var[:, 0:tn], c.r_t1[:, 0:tn], ALU.subtract), "vec")
    pg.wait(S, vv)
    sm = pg.inc(S.activation(c.r_t1[:, 0:tn], c.r_var[:, 0:tn], AF.Sqrt,
                             bias=c.eps_col[0:1, :]), "act")
    pg.wait(V, sm)
    pg.inc(V.reciprocal(c.r_rstd[:, 0:tn], c.r_t1[:, 0:tn]), "vec")
    pg.wait(V, pg.at("vec"))
    if save_r:
        pg.inc(V.tensor_copy(c.rrow[0:1, t0:t0 + tn], c.r_rstd[:, 0:tn]), "vec")
    mr = pg.inc(V.tensor_tensor(c.r_t1[:, 0:tn], c.r_mean[:, 0:tn], c.r_rstd[:, 0:tn], ALU.mult), "vec")
    if save_r:
        pg.wait(V, mr)
        pg.inc(V.tensor_copy(c.mrow[0:1, t0:t0 + tn], c.r_t1[:, 0:tn]), "vec")
    pg.wait(TE, mr, c.ms.get("bcast_free"), c.ms.get("scores_free"))
    nc.tensor.matmul(c.ps_s[:, 0:tn], c.ones_row_f, c.r_rstd[:, 0:tn], start=True, stop=True)
    bc = pg.inc(nc.tensor.matmul(c.ps_s[:, 512:512 + tn], c.ones_row_f, c.r_t1[:, 0:tn],
                                 start=True, stop=True), "tns")
    c.ms["rows_free"] = bc
    ov = None
    for ch in range(NCH):
        rs = c.resid[:, ch * T + t0: ch * T + t0 + tn]
        pg.wait(V, bc, resid_ms, c.ms.get("xb32_free"))
        t_ms = pg.inc(V.tensor_tensor(c.xb32[:, 0:tn], rs, c.ps_s[:, 0:tn], ALU.mult), "vec")
        pg.wait(V, t_ms)
        dsl = dst[:, ch * dstT + dst_t0: ch * dstT + dst_t0 + tn]
        ov = pg.inc(V.tensor_tensor(dsl, c.xb32[:, 0:tn], c.ps_s[:, 512:512 + tn], ALU.subtract), "vec")
        c.ms["xb32_free"] = ov
    c.ms["bcast_free"] = ov
    return ov


def load_w_group(pg, c, dram, g, kc_n, m_n, slot, buf=None, key="wslot", sem="dw"):
    nc = pg.nc
    buf = buf if buf is not None else c.wslot
    src = dram[g, :, :, :].rearrange("p kc m -> p (kc m)")
    dst = buf[slot][:, 0:kc_n * m_n]
    pg.wait(nc.gpsimd, c.ms.get((key + "_user", slot)))
    return pg.inc(nc.gpsimd.dma_start(out=dst, in_=src), "%s%d" % (sem, slot), 16)


def emit_attn(pg, c, T, resid_ms, layer):
    nc = pg.nc
    V, S, TE = nc.vector, nc.scalar, nc.tensor
    NT = min(512, T)
    nq = T // 128
    rope = (layer == "l0b")
    l1a = (layer == "l1a")
    MW = 256
    qkv_M = 3072
    wd = c.wdram[layer + "_w"]
    wod = c.wdram[layer + "_wo"]
    es = contextlib.ExitStack()
    _u = "_%d" % pg.uid()
    sb = lambda name, shape, dt: es.enter_context(nc.sbuf_tensor(name + _u, shape, dt))
    c.qbuf = sb("qbuf", [128, NCH * T], BF16)
    c.kbuf = sb("kbuf", [128, NCH * T], BF16)
    c.vbuf = sb("vbuf", [128, (T // 128) * D], BF16)
    c.vtmp = sb("vtmp", [128, NT], BF16)
    c.bias24 = sb("bias24", [128, 24], F32)
    c.bias_bo = sb("bias_bo", [128, 8], F32)
    c.qbias = sb("qbias", [128, 8], F32)
    esx = contextlib.ExitStack()
    sbx = lambda name, shape, dt: esx.enter_context(nc.sbuf_tensor(name + _u, shape, dt))
    if rope:
        c.ropet = sbx("ropet", [128, 2 * T], BF16)
        c.rp = sbx("rp", [128, 2 * NT], BF16)
        c.scb = sbx("scb", [128, NT], BF16)
        r1 = pg.inc(nc.sync.dma_start(out=c.ropet[:], in_=c.ropetd[:, :]), "dc3", 16)
        c.ms["rope_const"] = r1

    if layer in ("l0a", "l1a"):
        b1 = pg.inc(nc.sync.dma_start(out=c.bias24[:], in_=c.wdram[layer + "_b"][:, :]), "db1", 16)
        b2 = pg.inc(nc.sync.dma_start(out=c.bias_bo[:], in_=c.wdram[layer + "_bo"][:, :]), "db2", 16)
        c.ms["bias_ready"] = [b1, b2]
    else:
        c.ms["bias_ready"] = None

    pg.wait(V, c.ms.get("bias_ready"))
    if rope:
        qbs = pg.inc(V.memset(c.qbias[:, 0:8], 0.0), "vec")
        zb = pg.inc(V.memset(c.bias24[:], 0.0), "vec")
    else:
        qbs = pg.inc(V.tensor_scalar(c.qbias[:, 0:8], c.bias24[:, 0:8], 0.125, None, ALU.mult), "vec")
        zb = qbs

    HT = min(512, T)
    ngroups = qkv_M // MW
    xnq = sbx("xnq", [128, NCH * HT], BF16)
    c.wslot = [sbx("wslot0", [128, NCH * MW], BF16), sbx("wslot1", [128, NCH * MW], BF16)]
    for half in range(T // HT):
        ht0 = half * HT
        for t in range(HT // NT):
            ln_ms = emit_ln(pg, c, T, ht0 + t * NT, NT, resid_ms, xnq, HT, dst_t0=t * NT)
        wready = {}
        wready[0] = load_w_group(pg, c, wd, 0, NCH, MW, 0)
        for g in range(ngroups):
            slot = g % 2
            if g + 1 < ngroups:
                wready[g + 1] = load_w_group(pg, c, wd, g + 1, NCH, MW, (g + 1) % 2)
            wv = c.wslot[slot][:, 0:NCH * MW].rearrange("p (kc m) -> p kc m", m=MW)
            for t in range(HT // NT):
                for mi in range(MW // 128):
                    m = g * (MW // 128) + mi
                    par = m % 2
                    psum = c.ps_a if par == 0 else c.ps_b
                    pg.wait(TE, wready[g], ln_ms, c.ms.get("pp_free%d" % par),
                            c.ms.get("rows_read"))
                    for kc in range(NCH):
                        mm = nc.tensor.matmul(psum[:, 0:NT],
                                              wv[:, kc, mi * 128:(mi + 1) * 128],
                                              xnq[:, kc * HT + t * NT: kc * HT + t * NT + NT],
                                              start=(kc == 0), stop=(kc == NCH - 1))
                    mm_ms = pg.inc(mm, "tns")
                    c.ms[("wslot_user", slot)] = mm_ms
                    col = ht0 + t * NT
                    if m < 8:
                        if rope:
                            pg.wait(S, mm_ms, qbs, c.ms.get("scb_free"))
                            ev = pg.inc(S.activation(c.scb[:, 0:NT], psum[:, 0:NT], AF.Identity,
                                                     bias=c.qbias[:, m:m + 1], scale=0.125), "act")
                            ev = emit_rope(pg, c, T, c.qbuf, m, col, NT, ev)
                            c.ms["scb_free"] = ev
                        else:
                            pg.wait(S, mm_ms, qbs)
                            ev = pg.inc(S.activation(c.qbuf[:, m * T + col: m * T + col + NT],
                                                     psum[:, 0:NT], AF.Identity,
                                                     bias=c.qbias[:, m:m + 1], scale=0.125), "act")
                    elif m < 16:
                        ch = m - 8
                        if rope:
                            pg.wait(S, mm_ms, zb, c.ms.get("scb_free"))
                            ev = pg.inc(S.activation(c.scb[:, 0:NT], psum[:, 0:NT], AF.Identity,
                                                     bias=c.bias24[:, m:m + 1]), "act")
                            ev = emit_rope(pg, c, T, c.kbuf, ch, col, NT, ev)
                            c.ms["scb_free"] = ev
                        else:
                            pg.wait(S, mm_ms, zb)
                            ev = pg.inc(S.activation(c.kbuf[:, ch * T + col: ch * T + col + NT],
                                                     psum[:, 0:NT], AF.Identity,
                                                     bias=c.bias24[:, m:m + 1]), "act")
                    else:
                        ch = m - 16
                        pg.wait(S, mm_ms, zb, c.ms.get("vtmp_free"))
                        ev = pg.inc(S.activation(c.vtmp[:, 0:NT], psum[:, 0:NT], AF.Identity,
                                                 bias=c.bias24[:, m:m + 1]), "act")
                        for j4 in range(NT // 128):
                            pg.wait(TE, ev, c.ms.get("ps_t_free"))
                            tr = pg.inc(nc.tensor.transpose(
                                c.ps_t[:], c.vtmp[:, j4 * 128:(j4 + 1) * 128], c.identb), "tns")
                            pg.wait(V, tr)
                            ktile = (col // 128) + j4
                            cp = pg.inc(V.tensor_copy(
                                c.vbuf[:, ktile * D + ch * 128: ktile * D + (ch + 1) * 128],
                                c.ps_t[:]), "vec")
                            c.ms["ps_t_free"] = cp
                        c.ms["vtmp_free"] = pg.at("tns")
                    c.ms["pp_free%d" % par] = ev
    qkv_done = [pg.at("act"), pg.at("vec"), pg.at("tns")]
    pg.barrier()
    esx.close()

    c.pbuf = sb("pbuf", [128, 2 * T], BF16)
    c.pts = [sb("pts0", [128, 128], BF16), sb("pts1", [128, 128], BF16),
             sb("pts2", [128, 128], BF16)]
    c.otbuf = sb("otbuf", [128, NCH * 512], BF16)
    c.rowmax = sb("rowmax", [128, 2], F32)
    c.rowsum = sb("rowsum", [128, 2], F32)
    c.rowrec = sb("rowrec", [128, 2], F32)
    c.wo = sb("wo", [128, NCH * 512], BF16)

    causal = not l1a

    def stage_view(s):
        if s == 0:
            return c.ps_a[:, 0:64].bitcast(BF16)
        if s == 1:
            return c.ps_b[:, 0:64].bitcast(BF16)
        return c.ps_t[:]

    def emit_pv(i, qi, hh, nk):
        par2 = hh % 2
        chq2 = hh // 2
        pb2 = (hh % 2) * 64
        PB = par2 * T
        pn = c.ms[("pn", par2)]

        def emit_tr(j):
            s = j % 3
            fk = "ptc_free%d" % s if s < 2 else "ps_t_free"
            pv = stage_view(s)
            pg.wait(TE, pn, c.ms.get(fk),
                    c.ms.get("pp_free%d" % s) if s < 2 else None)
            tr = pg.inc(nc.tensor.transpose(pv, c.pbuf[:, PB + j * 128: PB + (j + 1) * 128],
                                            c.identb), "tns")
            if s < 2:
                c.ms["pp_free%d" % s] = None
            if j % 2 == 0:
                pg.wait(V, tr, c.ms.get(("pts_user", s)))
                cp = pg.inc(V.tensor_copy(c.pts[s][:], pv), "vec")
            else:
                pg.wait(S, tr, c.ms.get(("pts_user", s)))
                cp = pg.inc(S.copy(c.pts[s][:], pv), "act")
            c.ms[fk] = cp
            c.ms[("flcp", j)] = cp

        emit_tr(0)
        if nk > 1:
            emit_tr(1)
        mo = None
        for j in range(nk):
            if j + 2 < nk:
                emit_tr(j + 2)
            s = j % 3
            pg.wait(TE, c.ms.pop(("flcp", j)), None if j > 0 else c.ms.get("ps_o_free"))
            mo = nc.tensor.matmul(c.ps_o[0:64, 0:128],
                                  c.vbuf[:, j * D + hh * 64: j * D + (hh + 1) * 64],
                                  c.pts[s][:],
                                  start=(j == 0), stop=(j == nk - 1))
            c.ms[("pts_user", s)] = pg.at("tns")
        c.ms["pbf%d" % par2] = pg.at("tns")
        mo_ms = pg.inc(mo, "tns")
        pg.wait(V, mo_ms, c.ms.get("ot_user"))
        oe = pg.inc(V.tensor_copy(
            c.otbuf[pb2:pb2 + 64, chq2 * 512 + qi * 128: chq2 * 512 + qi * 128 + 128],
            c.ps_o[0:64, 0:128]), "vec")
        c.ms["ps_o_free"] = oe

    for qg in range(nq // 4):
        for qi in range(4):
            i = qg * 4 + qi
            nk = (i + 1) if causal else nq
            L = nk * 128
            split = (L <= 1024)
            for h in range(HEADS):
                par = h % 2
                chq = h // 2
                pb = (h % 2) * 64
                RB = par * 1024 if split else 0
                PB = par * T
                qsl = c.qbuf[pb:pb + 64, chq * T + i * 128: chq * T + (i + 1) * 128]
                if split:
                    pg.wait(TE, qkv_done, c.ms.get("sfree%d" % par), c.ms.get("bcast_free"))
                else:
                    pg.wait(TE, qkv_done, c.ms.get("sfree0"), c.ms.get("sfree1"),
                            c.ms.get("bcast_free"))
                ngf = L // 512
                for kg in range(ngf):
                    mm = nc.tensor.matmul(
                        c.ps_s[:, RB + kg * 512: RB + (kg + 1) * 512], qsl,
                        c.kbuf[pb:pb + 64, chq * T + kg * 512: chq * T + (kg + 1) * 512],
                        start=True, stop=True)
                for j in range(ngf * 4, nk):
                    mm = nc.tensor.matmul(
                        c.ps_s[:, RB + j * 128: RB + (j + 1) * 128], qsl,
                        c.kbuf[pb:pb + 64, chq * T + j * 128: chq * T + (j + 1) * 128],
                        start=True, stop=True)
                mm_ms = pg.inc(mm, "tns")
                if h > 0:
                    emit_pv(i, qi, h - 1, nk)
                if causal:
                    pg.wait(V, mm_ms)
                    mm_ms = pg.inc(V.tensor_tensor(
                        c.ps_s[:, RB + (nk - 1) * 128: RB + nk * 128],
                        c.ps_s[:, RB + (nk - 1) * 128: RB + nk * 128],
                        c.maskf[:], ALU.add), "vec")
                pg.wait(V, mm_ms, c.ms.get(("rmf", par)))
                rm = pg.inc(V.tensor_reduce(c.rowmax[:, par:par + 1], c.ps_s[:, RB:RB + L],
                                            AX.X, ALU.max, negate=True), "vec")
                pg.wait(S, rm, c.ms.get("pbf%d" % par), c.ms.get(("rsf", par)))
                ex = pg.inc(S.activation(c.pbuf[:, PB:PB + L], c.ps_s[:, RB:RB + L], AF.Exp,
                                         bias=c.rowmax[:, par:par + 1],
                                         accum_out=c.rowsum[:, par:par + 1]), "act")
                if split:
                    c.ms["sfree%d" % par] = ex
                else:
                    c.ms["sfree0"] = ex
                    c.ms["sfree1"] = ex
                c.ms[("rmf", par)] = ex
                pg.wait(V, ex, c.ms.get(("rrf", par)))
                rc = pg.inc(V.reciprocal(c.rowrec[:, par:par + 1], c.rowsum[:, par:par + 1]), "vec")
                c.ms[("rsf", par)] = rc
                pg.wait(V, rc)
                pn = pg.inc(V.tensor_scalar(c.pbuf[:, PB:PB + L], c.pbuf[:, PB:PB + L],
                                            c.rowrec[:, par:par + 1], None, ALU.mult), "vec")
                c.ms[("rrf", par)] = pn
                c.ms[("pn", par)] = pn
            emit_pv(i, qi, HEADS - 1, nk)
        ot_ms = pg.at("vec")
        t0 = qg * 512
        wvv = c.wo[:, :].rearrange("p (kc m) -> p kc m", m=512)
        for m in range(NCH):
            par = m % 2
            psum = c.ps_a if par == 0 else c.ps_b
            if m % 4 == 0:
                pg.wait(nc.gpsimd, c.ms.get("wo_user"))
                wo_ms = pg.inc(nc.gpsimd.dma_start(
                    out=c.wo[:, :].rearrange("p (kc m) -> p kc m", m=512),
                    in_=wod[0, :, :, (m // 4) * 512:(m // 4) * 512 + 512]), "dwo", 16)
            pg.wait(TE, ot_ms, wo_ms, c.ms.get("pp_free%d" % par), c.ms.get("ptc_free%d" % par))
            for kc in range(NCH):
                mm = nc.tensor.matmul(psum[:, 0:512], wvv[:, kc, (m % 4) * 128:(m % 4) * 128 + 128],
                                      c.otbuf[:, kc * 512:(kc + 1) * 512],
                                      start=(kc == 0), stop=(kc == NCH - 1))
            mm_ms = pg.inc(mm, "tns")
            c.ms["wo_user"] = mm_ms
            bia = c.zero_col[:, 0:1] if rope else c.bias_bo[:, m:m + 1]
            pg.wait(S, mm_ms, c.ms.get("bias_ready"), c.ms.get("sc_free"))
            ad = pg.inc(S.activation(c.sc[:, 0:512], psum[:, 0:512], AF.Identity, bias=bia), "act")
            c.ms["pp_free%d" % par] = ad
            c.ms["ptc_free%d" % par] = ad
            pg.wait(V, ad, resid_ms)
            ra = pg.inc(V.tensor_tensor(c.resid[:, m * T + t0: m * T + t0 + 512],
                                        c.resid[:, m * T + t0: m * T + t0 + 512],
                                        c.sc[:, 0:512], ALU.add), "vec")
            c.ms["sc_free"] = ra
        c.ms["ot_user"] = pg.at("tns")
    pg.barrier()
    es.close()
    return pg.at("vec")


def emit_rope(pg, c, T, buf, ch, col, NT, ev):
    nc = pg.nc
    V = nc.vector
    last = ev
    for hh in range(2):
        be, bo = hh * 64, hh * 64 + 32
        es = c.scb[be:be + 32, 0:NT]
        os_ = c.scb[bo:bo + 32, 0:NT]
        ed = buf[be:be + 32, ch * T + col: ch * T + col + NT]
        od = buf[bo:bo + 32, ch * T + col: ch * T + col + NT]
        cos_e = c.ropet[be:be + 32, col:col + NT]
        sin_e = c.ropet[be:be + 32, T + col: T + col + NT]
        cos_o = c.ropet[bo:bo + 32, col:col + NT]
        sin_o = c.ropet[bo:bo + 32, T + col: T + col + NT]
        sA = c.rp[:, 0:NT]
        sB = c.rp[:, NT:2 * NT]
        pg.wait(V, ev, c.ms.get("rope_scr_free"), c.ms.get("rope_const"))
        pg.inc(V.tensor_tensor(sA[be:be + 32, :], es, cos_e, ALU.mult), "vec")
        pg.inc(V.tensor_tensor(sB[be:be + 32, :], os_, sin_o, ALU.mult), "vec")
        pg.inc(V.tensor_tensor(sA[bo:bo + 32, :], os_, cos_o, ALU.mult), "vec")
        t4 = pg.inc(V.tensor_tensor(sB[bo:bo + 32, :], es, sin_e, ALU.mult), "vec")
        pg.wait(V, t4)
        pg.inc(V.tensor_tensor(ed, sA[be:be + 32, :], sB[be:be + 32, :], ALU.subtract), "vec")
        last = pg.inc(V.tensor_tensor(od, sA[bo:bo + 32, :], sB[bo:bo + 32, :], ALU.add), "vec")
        c.ms["rope_scr_free"] = last
    return last



WSPECS = {
    "l0a_w": [12, 128, 8, 256], "l0a_b": [128, 24], "l0a_wo": [1, 128, 8, 1024], "l0a_bo": [128, 8],
    "l0b_w": [12, 128, 8, 256], "l0b_wo": [1, 128, 8, 1024],
    "l1a_w": [12, 128, 8, 256], "l1a_b": [128, 24], "l1a_wo": [1, 128, 8, 1024], "l1a_bo": [128, 8],
    "moe_r": [D, 16], "moe_gb": [128, 128], "moe_db": [16, D],
    "moe_gu": [16, 128, 8, 1024], "moe_dn": [16, 128, 4, 1024],
    "l1f_w1": [8, 128, 8, 512], "l1f_w2": [8, 128, 8, 512],
    "l1f_h1": [32, 128, 32, 128], "l1f_h2": [32, 128, 32, 128],
    "l1f_w3": [8, 128, 32, 128], "l1f_s": [128, 136],
    "l1b_w": [8, 128, 8, 512], "l1b_ba": [1, 128, 8, 64],
    "l1b_wo": [2, 128, 8, 512], "l1b_dtb": [16, 1], "l1b_alog": [16, 1],
    "l1b_nw": [128, 1],
}
CF2_SHAPE = [128, 192]
SL_WEIGHTS = {
    "l0a": ["l0a_w", "l0a_b", "l0a_wo", "l0a_bo"],
    "l0b": ["l0b_w", "l0b_wo"],
    "l1a": ["l1a_w", "l1a_b", "l1a_wo", "l1a_bo"],
    "l0m": ["moe_r", "moe_gb", "moe_db", "moe_gu", "moe_dn"],
    "l1f": ["l1f_w1", "l1f_w2", "l1f_h1", "l1f_h2", "l1f_w3", "l1f_s"],
    "l1b": ["l1b_w", "l1b_ba", "l1b_wo", "l1b_dtb", "l1b_alog", "l1b_nw"],
}
ATTN_SET = ("l0a", "l0b", "l1a")


def build_program(T, sublayers, n_seq=2):
    nc = bass.Bass("TRN2", target_bir_lowering=False)
    stack = contextlib.ExitStack()
    pg = Prog(nc, stack)
    c = Ctx()
    c.ms = {}

    xin = nc.dram_tensor("xin", [n_seq, D, T], F32, kind="ExternalInput").ap()
    yout = nc.dram_tensor("yout", [n_seq, D, T], F32, kind="ExternalOutput").ap()
    cb = nc.dram_tensor("cb", [128, 256], BF16, kind="ExternalInput").ap()
    cf = nc.dram_tensor("cf", [128, 128], F32, kind="ExternalInput").ap()
    if "l0b" in sublayers:
        c.ropetd = nc.dram_tensor("ropet", [128, 2 * T], BF16, kind="ExternalInput").ap()
    if "l0m" in sublayers:
        c.cindd = nc.dram_tensor("cind", [16, 16 * 128], BF16, kind="ExternalInput").ap()
    if "l1b" in sublayers:
        c.cf2d = nc.dram_tensor("cf2d", [128, 192], F32, kind="ExternalInput").ap()
        c.cfid = nc.dram_tensor("cfi", [128, 128], F32, kind="ExternalInput").ap()
        c.g_scr = nc.dram_tensor("g_scr", [16, 256], F32).ap()
    c.wdram = {}
    need = []
    for sl in sublayers:
        for wn in SL_WEIGHTS[sl]:
            if wn not in need:
                need.append(wn)
    for name in need:
        c.wdram[name] = nc.dram_tensor(name, WSPECS[name], F32, kind="ExternalInput").ap()

    sb = lambda name, shape, dt: stack.enter_context(nc.sbuf_tensor(name, shape, dt))
    ps = lambda name, shape, dt: stack.enter_context(nc.psum_tensor(name, shape, dt))

    NT = min(512, T)
    c.resid = sb("resid", [128, NCH * T], F32)
    c.constb = sb("constb", [128, 256], BF16)
    c.maskf = sb("maskf", [128, 128], F32)
    c.identb = c.constb[:, 0:128]
    c.ones_col = c.constb[:, 128:129]
    c.onesf = sb("onesf", [1, 128], F32)
    c.eps_col = sb("epsc", [1, 1], F32)
    c.zero_col = sb("zeroc", [128, 1], F32)
    c.xb = sb("xb", [128, 2 * NT], BF16)
    c.xb32 = sb("xb32", [128, NT], F32)
    c.sq = sb("sq", [128, 2 * NT], BF16)
    c.r4 = sb("r4", [1, 4 * NT], F32)
    c.r_mean = c.r4[0:1, 0:NT]
    c.r_var = c.r4[0:1, NT:2 * NT]
    c.r_t1 = c.r4[0:1, 2 * NT:3 * NT]
    c.r_rstd = c.r4[0:1, 3 * NT:4 * NT]
    c.sc = sb("sc", [128, 512], F32)

    c.ps_s = ps("ps_s", [128, 2048], F32)
    c.ps_a = ps("ps_a", [128, 512], F32)
    c.ps_b = ps("ps_b", [128, 512], F32)
    c.ps_o = ps("ps_o", [128, 128], F32)
    c.ps_t = ps("ps_t", [128, 128], BF16)

    c.ones_row_f = c.onesf[0:1, :]

    ld1 = pg.inc(nc.sync.dma_start(out=c.constb[:], in_=cb[:, :]), "dc1", 16)
    ld2 = pg.inc(nc.sync.dma_start(out=c.maskf[:], in_=cf[:, :]), "dc2", 16)
    pg.wait(nc.vector, ld1, ld2)
    pg.inc(nc.vector.memset(c.onesf[:], 1.0), "vec")
    pg.inc(nc.vector.memset(c.eps_col[:], 1e-5), "vec")
    pg.inc(nc.vector.memset(c.zero_col[:], 0.0), "vec")
    init_v = pg.at("vec")
    for eng in (nc.tensor, nc.scalar):
        pg.wait(eng, init_v, ld1, ld2)

    for s in range(n_seq):
        pg.wait(nc.sync, c.ms.get("resid_stored"))
        rl = pg.inc(nc.sync.dma_start(
            out=c.resid[:].rearrange("p (ch t) -> p ch t", ch=NCH),
            in_=xin[s, :, :].rearrange("(ch p) t -> p ch t", p=128)), "dx", 16)
        resid_ms = rl
        for sl in sublayers:
            pg.barrier()
            if sl == "l0m":
                emit_moe(pg, c, T, resid_ms)
            elif sl == "l1f":
                emit_mlp(pg, c, T, resid_ms)
            elif sl == "l1b":
                emit_dnet(pg, c, T, resid_ms)
            else:
                emit_attn(pg, c, T, resid_ms, layer=sl)
            pg.barrier()
        done = pg.at("vec")
        pg.wait(nc.sync, done)
        st = pg.inc(nc.sync.dma_start(
            out=yout[s, :, :].rearrange("(ch p) t -> p ch t", p=128),
            in_=c.resid[:].rearrange("p (ch t) -> p ch t", ch=NCH)), "dy", 16)
        c.ms["resid_stored"] = st
        pg.barrier()

    pg.finish()
    stack.close()
    return nc



def const_inputs(sublayers, consts):
    m = {"cb": consts["cb"], "cf": consts["cf"]}
    if "l0b" in sublayers:
        m["ropet"] = consts["ropet"]
    if "l0m" in sublayers:
        m["cind"] = consts["cind"]
    if "l1b" in sublayers:
        m["cf2d"] = consts["cf2"]
        m["cfi"] = consts["cfi"]
    return m


def kernel(**inputs):
    T = 2048
    x = np.ascontiguousarray(np.transpose(inputs["L0_attn_a_hidden_states"], (0, 2, 1)))
    w = prep_weights(inputs)
    consts = make_consts(T)
    groups = [["l0a", "l0m", "l0b", "l0m", "l1a", "l1f", "l1b", "l1f"]]
    cur = x
    for sublayers in groups:
        nc = build_program(T, sublayers)
        need = []
        for sl in sublayers:
            for wn in SL_WEIGHTS[sl]:
                if wn not in need:
                    need.append(wn)
        in_maps = []
        for core in range(8):
            m = {"xin": cur[core * 2:(core + 1) * 2]}
            m.update(const_inputs(sublayers, consts))
            for name in need:
                m[name] = w[name]
            in_maps.append(m)
        res = run_bass_kernel_spmd(nc, in_maps, core_ids=list(range(8)))
        cur = np.concatenate([r["yout"] for r in res.results], axis=0)
    return np.ascontiguousarray(np.transpose(cur, (0, 2, 1)))


# version 14
# speedup vs baseline: 1.0305x; 1.0305x over previous
import contextlib
import numpy as np
import ml_dtypes
import concourse.bass as bass
import concourse.mybir as mybir
from concourse.bass_utils import run_bass_kernel_spmd

F32 = mybir.dt.float32
BF16 = mybir.dt.bfloat16
AF = mybir.ActivationFunctionType
ALU = mybir.AluOpType
AX = mybir.AxisListType
BF = ml_dtypes.bfloat16

D = 1024
NCH = 8
HEADS = 16
HD = 64
NEG = -1.0e30



def _pack_bias(b):
    return np.ascontiguousarray(b.reshape(-1, 128).T).astype(np.float32)


def _rope_tables(T, d=HD, theta=10000.0):
    inv = 1.0 / theta ** (np.arange(0, d, 2, dtype=np.float64) / d)
    t = np.arange(T, dtype=np.float64)
    fr = np.outer(inv, t)
    return np.cos(fr).astype(np.float32), np.sin(fr).astype(np.float32)


def _rope_perm():
    p = []
    for h in range(HEADS):
        base = h * HD
        p += [base + i for i in range(0, HD, 2)]
        p += [base + i for i in range(1, HD, 2)]
    return np.array(p)


def tile_w(W, kcn, mw):
    K, M = W.shape
    assert K == kcn * 128 and M % mw == 0
    G = M // mw
    return np.ascontiguousarray(
        W.reshape(kcn, 128, G, mw).transpose(2, 1, 0, 3))


def prep_weights(iv):
    w = {}
    w["l0a_w"] = np.ascontiguousarray(iv["L0_attn_a_c_attn_kernel"].T)
    w["l0a_b"] = _pack_bias(iv["L0_attn_a_c_attn_bias"])
    w["l0a_wo"] = np.ascontiguousarray(iv["L0_attn_a_c_proj_kernel"].T)
    w["l0a_bo"] = _pack_bias(iv["L0_attn_a_c_proj_bias"])
    wq = iv["L0_attn_b_w_qkv"]
    perm = _rope_perm()
    cols = np.concatenate([perm, 1024 + perm, 2048 + np.arange(1024)])
    w["l0b_w"] = np.ascontiguousarray(wq[:, cols])
    w["l0b_wo"] = np.ascontiguousarray(iv["L0_attn_b_w_out"])
    w["l1a_w"] = np.ascontiguousarray(np.concatenate(
        [iv["L1_attn_a_Wq"].T, iv["L1_attn_a_Wk"].T, iv["L1_attn_a_Wv"].T], axis=1))
    w["l1a_b"] = _pack_bias(np.concatenate(
        [iv["L1_attn_a_bq"], iv["L1_attn_a_bk"], iv["L1_attn_a_bv"]]))
    w["l1a_wo"] = np.ascontiguousarray(iv["L1_attn_a_Wo"].T)
    w["l1a_bo"] = _pack_bias(iv["L1_attn_a_bo"])
    deint = np.concatenate([np.arange(0, 1024, 2), np.arange(1, 1024, 2)])
    w["moe_r"] = np.ascontiguousarray(iv["L0_ffn_router_weight"])
    w["moe_gu"] = np.ascontiguousarray(iv["L0_ffn_gate_up_proj"][:, :, deint])
    gbd = iv["L0_ffn_gate_up_proj_bias"][:, deint]
    w["moe_gb"] = np.ascontiguousarray(gbd.reshape(16, 8, 128).transpose(2, 0, 1).reshape(128, 128)).astype(np.float32)
    w["moe_dn"] = np.ascontiguousarray(iv["L0_ffn_down_proj"])
    w["moe_db"] = np.ascontiguousarray(iv["L0_ffn_down_proj_bias"])
    w["l1f_w1"] = tile_w(iv["L1_ffn_w1_w"].T, 8, 512)
    w["l1f_w2"] = tile_w(iv["L1_ffn_w2_w"].T, 8, 512)
    w["l1f_h1"] = tile_w(iv["L1_ffn_h1_w"].T, 32, 128)
    w["l1f_h2"] = tile_w(iv["L1_ffn_h2_w"].T, 32, 128)
    w["l1f_w3"] = tile_w(iv["L1_ffn_w3_w"].T, 32, 128)
    w["l1f_s"] = np.concatenate([
        _pack_bias(iv["L1_ffn_w1_s"]), _pack_bias(iv["L1_ffn_w2_s"]),
        _pack_bias(iv["L1_ffn_h1_s"]), _pack_bias(iv["L1_ffn_h2_s"]),
        _pack_bias(iv["L1_ffn_w3_s"])], axis=1)
    w["l1b_w"] = tile_w(np.ascontiguousarray(np.concatenate(
        [iv["L1_attn_b_Wq"], iv["L1_attn_b_Wk"], iv["L1_attn_b_Wv"], iv["L1_attn_b_Wg"]],
        axis=1)), 8, 512)
    ba = np.zeros((1024, 64), dtype=np.float32)
    ba[:, 0:16] = iv["L1_attn_b_Wb"]
    ba[:, 32:48] = iv["L1_attn_b_Wa"]
    w["l1b_ba"] = tile_w(ba, 8, 64)
    w["l1b_wo"] = tile_w(np.ascontiguousarray(iv["L1_attn_b_Wo"]), 8, 512)
    w["l1b_dtb"] = np.ascontiguousarray(iv["L1_attn_b_dt_bias"][:, None]).astype(np.float32)
    w["l1b_alog"] = np.ascontiguousarray(iv["L1_attn_b_A_log"][:, None]).astype(np.float32)
    w["l1b_nw"] = np.ascontiguousarray(np.tile(iv["L1_attn_b_norm_w"], 2)[:, None]).astype(np.float32)
    for nm in ("l0a_w", "l0b_w", "l1a_w"):
        w[nm] = tile_w(w[nm], 8, 256)
    for nm in ("l0a_wo", "l0b_wo", "l1a_wo"):
        w[nm] = tile_w(w[nm], 8, 1024)
    w["moe_gu"] = np.ascontiguousarray(np.stack([tile_w(e, 8, 1024) for e in w["moe_gu"]])[:, 0])
    w["moe_dn"] = np.ascontiguousarray(np.stack(
        [tile_w(np.ascontiguousarray(e), 4, 1024) for e in w["moe_dn"]])[:, 0])
    return w


def make_consts(T):
    cb = np.zeros((128, 256), dtype=BF)
    cb[:, 0:128] = np.eye(128)
    cb[:, 128:256] = 1.0
    q = np.arange(128)[:, None]
    k = np.arange(128)[None, :]
    cf = np.where(k <= q, 0.0, NEG).astype(np.float32)
    cos1, sin1 = _rope_tables(T)
    ropet = np.zeros((128, 2 * T), dtype=BF)
    for qd in range(4):
        ropet[qd * 32:(qd + 1) * 32, 0:T] = cos1
        ropet[qd * 32:(qd + 1) * 32, T:2 * T] = sin1
    cind = np.zeros((16, 16 * 128), dtype=BF)
    for e in range(16):
        cind[e, e * 128:(e + 1) * 128] = 1.0
    cf2 = np.zeros((128, 128), dtype=np.float32)
    j = np.arange(64)[:, None]
    cm = np.arange(64)[None, :]
    cf2[0:64, 0:64] = (j <= cm).astype(np.float32)
    cf2[64:128, 0:64] = cf2[0:64, 0:64]
    cf2[0:64, 64:128] = (j < cm).astype(np.float32)
    cf2[64:128, 64:128] = cf2[0:64, 64:128]
    cf2i = np.zeros((128, 64), dtype=np.float32)
    cf2i[0:64] = np.eye(64, dtype=np.float32)
    cf2i[64:128] = np.eye(64, dtype=np.float32)
    cf2 = np.concatenate([cf2, cf2i], axis=1)
    cfi = np.eye(128, dtype=np.float32)
    return {"cb": cb, "cf": cf, "ropet": ropet, "cind": cind,
            "cf2": cf2, "cfi": cfi}



class Prog:
    def __init__(self, nc, stack):
        self.nc = nc
        self.stack = stack
        self._sems = {}
        self.cnt = {}
        self._waited = {}
        self._uid = 0

    def uid(self):
        self._uid += 1
        return self._uid

    def sem(self, name):
        if name not in self._sems:
            self._sems[name] = self.stack.enter_context(self.nc.semaphore(name))
            self.cnt[name] = 0
        return self._sems[name]

    def inc(self, inst, name, n=1):
        inst.then_inc(self.sem(name), n)
        self.cnt[name] += n
        return (name, self.cnt[name])

    def at(self, name):
        return (name, self.cnt.get(name, 0))

    def _flat(self, mss):
        for ms in mss:
            if ms is None:
                continue
            if isinstance(ms, list):
                yield from self._flat(ms)
            elif isinstance(ms[0], tuple):
                yield from self._flat(ms)
            else:
                yield ms

    def wait(self, eng, *mss):
        key = id(eng)
        w = self._waited.setdefault(key, {})
        for name, v in self._flat(mss):
            if v <= 0 or w.get(name, 0) >= v:
                continue
            eng.wait_ge(self.sem(name), v)
            w[name] = v

    def finish(self):
        for name, v in sorted(self.cnt.items()):
            if name.startswith("d") and v > 0:
                self.nc.sync.wait_ge(self.sem(name), v)

    def barrier(self):
        nc = self.nc
        nc.all_engine_barrier()
        for e in (nc.tensor, nc.vector, nc.scalar, nc.gpsimd, nc.sync):
            w = self._waited.setdefault(id(e), {})
            for s in ("tns", "vec", "act", "pool", "sp"):
                if s in self.cnt:
                    w[s] = max(w.get(s, 0), self.cnt[s])



class Ctx:
    pass


def seg3(buf_slice, d=64):
    return buf_slice.rearrange("p (h d) -> p h d", d=d)


def segb(col_slice, n, d=64):
    p = col_slice.shape[0]
    return col_slice.unsqueeze(2).broadcast_to((p, n, d))


def emit_moe(pg, c, T, resid_ms):
    nc = pg.nc
    V, S, TE = nc.vector, nc.scalar, nc.tensor
    NT = 512
    ntiles = T // NT
    nblk = T // 128
    E = 16
    es = contextlib.ExitStack()
    _u = "_%d" % pg.uid()
    sb = lambda name, shape, dt: es.enter_context(nc.sbuf_tensor(name + _u, shape, dt))
    c.xnfull = sb("xnfull", [128, NCH * T], BF16)
    c.rtr32 = sb("rtr32", [128, NCH * 16], F32)
    c.gub = sb("gub", [128, 128], F32)
    c.dnb = sb("dnb", [16, D], BF16)
    c.indm = sb("indm", [16, 16 * 128], BF16)
    c.scoresT = sb("scoresT", [16, T], BF16)
    c.scbf = sb("scbf", [128, 16], BF16)
    c.col1 = sb("col1", [128, 1], F32)
    c.col2 = sb("col2", [128, 1], F32)
    c.col3 = sb("col3", [128, 1], F32)
    c.col4 = sb("col4", [128, 1], F32)
    c.rrow = sb("rrow", [1, T], BF16)
    c.mrow = sb("mrow", [1, T], BF16)
    c.crow = sb("crow", [1, 16], F32)
    c.crb = sb("crb", [128, 16], F32)
    c.onescf = sb("onescf", [128, 1], F32)
    c.wgu = [sb("wgu0", [128, NCH * 1024], BF16), sb("wgu1", [128, NCH * 1024], BF16)]
    c.wdn = [sb("wdn0", [128, 4 * 1024], BF16), sb("wdn1", [128, 4 * 1024], BF16)]
    c.actbuf = sb("actbuf", [128, 4 * 512], BF16)
    c.msc = sb("msc", [128, 3 * 512], F32)
    ldi = pg.inc(nc.sync.dma_start(out=c.indm[:], in_=c.cindd[:, :]), "dc5", 16)
    c.ms["indm_ms"] = ldi

    rt = pg.inc(nc.sync.dma_start(
        out=c.rtr32[:].rearrange("p (kc e) -> p kc e", e=16),
        in_=c.wdram["moe_r"].rearrange("(kc p) e -> p kc e", p=128)), "drt", 16)
    gb = pg.inc(nc.sync.dma_start(out=c.gub[:], in_=c.wdram["moe_gb"][:, :]), "dgb", 16)
    db = pg.inc(nc.gpsimd.dma_start(out=c.dnb[:], in_=c.wdram["moe_db"][:, :]), "ddb", 16)

    for t in range(ntiles):
        ln_ms = emit_ln(pg, c, T, t * NT, NT, resid_ms, c.xnfull, T, save_r=True)

    oc = pg.inc(nc.vector.memset(c.onescf[:], 1.0), "vec")
    pg.wait(TE, rt, oc, c.ms.get("pp_free0"), c.ms.get("rows_read"))
    for kc in range(NCH):
        mmc = nc.tensor.matmul(c.ps_a[0:1, 0:16], c.onescf[:],
                               c.rtr32[:].rearrange("p (kc e) -> p kc e", e=16)[:, kc, :],
                               start=(kc == 0), stop=(kc == NCH - 1))
    mmc_ms = pg.inc(mmc, "tns")
    pg.wait(V, mmc_ms)
    crc = pg.inc(V.tensor_copy(c.crow[:], c.ps_a[0:1, 0:16]), "vec")
    pg.wait(TE, crc)
    crb_mm = pg.inc(nc.tensor.matmul(c.ps_a[:, 16:32], c.ones_row_f, c.crow[0:1, :],
                                     start=True, stop=True), "tns")
    pg.wait(V, crb_mm)
    crb = pg.inc(V.tensor_copy(c.crb[:], c.ps_a[:, 16:32]), "vec")
    c.ms["pp_free0"] = crb

    for blk in range(nblk):
        t0 = blk * 128
        pg.wait(TE, rt, resid_ms, c.ms.get("ps_o_free"), c.ms.get("rows_free"),
                c.ms.get("rows_read"), c.ms.get("rcol_free"))
        for kc in range(NCH):
            g_mm = nc.tensor.matmul(c.ps_o[:, 0:16],
                                    c.resid[:, kc * T + t0: kc * T + t0 + 128],
                                    c.rtr32[:].rearrange("p (kc e) -> p kc e", e=16)[:, kc, :],
                                    start=(kc == 0), stop=(kc == NCH - 1))
        nc.tensor.matmul(c.ps_b[:, 0:1], c.rrow[0:1, t0:t0 + 128],
                         c.constb[0:1, 128:129], start=True, stop=True)
        rmm = pg.inc(nc.tensor.matmul(c.ps_b[:, 1:2], c.mrow[0:1, t0:t0 + 128],
                                      c.constb[0:1, 128:129], start=True, stop=True), "tns")
        pg.wait(V, rmm, c.ms.get("msc_free"))
        cp = pg.inc(V.tensor_copy(c.col1[:], c.ps_b[:, 0:1]), "vec")
        cp2 = pg.inc(V.tensor_copy(c.col2[:], c.ps_b[:, 1:2]), "vec")
        c.ms["rcol_free"] = cp2
        pg.wait(V, cp, cp2, crb)
        pg.inc(V.tensor_scalar(c.msc[:, 0:16], c.ps_o[:, 0:16], c.col1[:], None, ALU.mult), "vec")
        lg = pg.inc(V.tensor_scalar(c.msc[:, 16:32], c.crb[:], c.col2[:], None, ALU.mult), "vec")
        c.ms["ps_o_free"] = lg
        pg.wait(V, lg)
        lg = pg.inc(V.tensor_tensor(c.msc[:, 0:16], c.msc[:, 0:16], c.msc[:, 16:32],
                                    ALU.subtract), "vec")
        pg.wait(V, lg)
        pg.inc(V.tensor_reduce(c.col1[:], c.msc[:, 0:16], AX.X, ALU.max), "vec")
        pg.wait(V, pg.at("vec"))
        pg.inc(V.tensor_scalar(c.msc[:, 16:32], c.msc[:, 0:16], c.col1[:], None, ALU.is_equal), "vec")
        pg.wait(V, pg.at("vec"))
        pg.inc(V.tensor_scalar(c.msc[:, 32:48], c.msc[:, 16:32], NEG, None, ALU.mult), "vec")
        pg.wait(V, pg.at("vec"))
        pg.inc(V.tensor_tensor(c.msc[:, 32:48], c.msc[:, 0:16], c.msc[:, 32:48], ALU.add), "vec")
        pg.wait(V, pg.at("vec"))
        pg.inc(V.tensor_reduce(c.col2[:], c.msc[:, 32:48], AX.X, ALU.max), "vec")
        pg.wait(V, pg.at("vec"))
        d12 = pg.inc(V.tensor_tensor(c.col3[:], c.col1[:], c.col2[:], ALU.subtract), "vec")
        pg.wait(S, d12)
        sg = pg.inc(S.activation(c.col3[:], c.col3[:], AF.Sigmoid), "act")
        pg.wait(V, sg)
        pg.inc(V.tensor_scalar(c.col4[:], c.col3[:], -1.0, 1.0, ALU.mult, ALU.add), "vec")
        pg.inc(V.tensor_scalar(c.msc[:, 48:64], c.msc[:, 32:48], c.col2[:], None, ALU.is_equal), "vec")
        pg.wait(V, pg.at("vec"))
        pg.inc(V.tensor_scalar(c.msc[:, 16:32], c.msc[:, 16:32], c.col3[:], None, ALU.mult), "vec")
        pg.inc(V.tensor_scalar(c.msc[:, 48:64], c.msc[:, 48:64], c.col4[:], None, ALU.mult), "vec")
        pg.wait(V, pg.at("vec"))
        pg.inc(V.tensor_tensor(c.msc[:, 16:32], c.msc[:, 16:32], c.msc[:, 48:64], ALU.add), "vec")
        pg.wait(V, pg.at("vec"))
        sb_ms = pg.inc(V.tensor_copy(c.scbf[:], c.msc[:, 16:32]), "vec")
        c.ms["msc_free"] = sb_ms
        pg.wait(TE, sb_ms, c.ms.get("ps_t_free"))
        tr = pg.inc(nc.tensor.transpose(c.ps_t[0:16, :], c.scbf[:], c.identb), "tns")
        pg.wait(V, tr)
        stc = pg.inc(V.tensor_copy(c.scoresT[:, blk * 128:(blk + 1) * 128], c.ps_t[0:16, :]), "vec")
        c.ms["ps_t_free"] = stc
    route_ms = pg.at("vec")

    def load_gu(e, slot):
        src = c.wdram["moe_gu"][e, :, :, :].rearrange("p kc m -> p (kc m)")
        pg.wait(nc.gpsimd, c.ms.get(("wgu_user", slot)))
        return pg.inc(nc.gpsimd.dma_start(out=c.wgu[slot][:], in_=src), "dgu%d" % slot, 16)

    def load_dn(e, slot):
        src = c.wdram["moe_dn"][e, :, :, :].rearrange("p kc m -> p (kc m)")
        pg.wait(nc.gpsimd, c.ms.get(("wdn_user", slot)))
        return pg.inc(nc.gpsimd.dma_start(out=c.wdn[slot][:], in_=src), "ddn%d" % slot, 16)

    gu_ms = {0: load_gu(0, 0)}
    dn_ms = {0: load_dn(0, 0)}
    for e in range(E):
        slot = e % 2
        if e + 1 < E:
            gu_ms[e + 1] = load_gu(e + 1, (e + 1) % 2)
            dn_ms[e + 1] = load_dn(e + 1, (e + 1) % 2)
        wguv = c.wgu[slot][:].rearrange("p (kc m) -> p kc m", m=1024)
        wdnv = c.wdn[slot][:].rearrange("p (kc m) -> p kc m", m=1024)
        for t in range(ntiles):
            t0 = t * NT
            pg.wait(TE, route_ms, c.ms.get("indm_ms"), c.ms.get("bcast_read"),
                    c.ms.get("scores_free"))
            bcm = pg.inc(nc.tensor.matmul(c.ps_s[:, 0:NT],
                                          c.indm[:, e * 128:(e + 1) * 128],
                                          c.scoresT[:, t0:t0 + NT],
                                          start=True, stop=True), "tns")
            for pr in range(4):
                pg.wait(TE, gu_ms[e], ln_ms, c.ms.get("pp_free0"), c.ms.get("pp_free1"),
                        c.ms.get("rows_read"))
                for kc in range(NCH):
                    nc.tensor.matmul(c.ps_a[:, 0:NT], wguv[:, kc, pr * 128:(pr + 1) * 128],
                                     c.xnfull[:, kc * T + t0: kc * T + t0 + NT],
                                     start=(kc == 0), stop=(kc == NCH - 1))
                for kc in range(NCH):
                    mm = nc.tensor.matmul(c.ps_b[:, 0:NT],
                                          wguv[:, kc, (pr + 4) * 128:(pr + 5) * 128],
                                          c.xnfull[:, kc * T + t0: kc * T + t0 + NT],
                                          start=(kc == 0), stop=(kc == NCH - 1))
                mm_ms = pg.inc(mm, "tns")
                c.ms[("wgu_user", slot)] = mm_ms
                pg.wait(S, mm_ms, gb, c.ms.get("msc01_free"))
                pg.inc(S.activation(c.msc[:, 0:NT], c.ps_a[:, 0:NT], AF.Identity,
                                    bias=c.gub[:, e * 8 + pr: e * 8 + pr + 1]), "act")
                g_ms = pg.at("act")
                pg.inc(S.activation(c.msc[:, NT:2 * NT], c.ps_b[:, 0:NT], AF.Identity,
                                    bias=c.gub[:, e * 8 + pr + 4: e * 8 + pr + 5]), "act")
                u_ms = pg.at("act")
                c.ms["pp_free0"] = g_ms
                c.ms["pp_free1"] = u_ms
                pg.wait(V, g_ms)
                gm = pg.inc(V.tensor_scalar(c.msc[:, 0:NT], c.msc[:, 0:NT], 7.0, None, ALU.min), "vec")
                pg.wait(V, u_ms)
                um = pg.inc(V.tensor_scalar(c.msc[:, NT:2 * NT], c.msc[:, NT:2 * NT],
                                            7.0, -7.0, ALU.min, ALU.max), "vec")
                pg.wait(S, gm)
                sgl = pg.inc(S.activation(c.msc[:, 2 * NT:3 * NT], c.msc[:, 0:NT],
                                          AF.Sigmoid, scale=1.702), "act")
                pg.wait(V, sgl, um)
                pg.inc(V.tensor_tensor(c.msc[:, 2 * NT:3 * NT], c.msc[:, 0:NT],
                                       c.msc[:, 2 * NT:3 * NT], ALU.mult), "vec")
                pg.inc(V.tensor_scalar(c.msc[:, NT:2 * NT], c.msc[:, NT:2 * NT],
                                       1.0, None, ALU.add), "vec")
                pg.wait(V, pg.at("vec"))
                a0 = pg.inc(V.tensor_tensor(c.msc[:, NT:2 * NT], c.msc[:, NT:2 * NT],
                                            c.msc[:, 2 * NT:3 * NT], ALU.mult), "vec")
                pg.wait(V, a0, bcm, c.ms.get("act_user"))
                av = pg.inc(V.tensor_tensor(c.actbuf[:, pr * NT:(pr + 1) * NT],
                                            c.msc[:, NT:2 * NT], c.ps_s[:, 0:NT], ALU.mult), "vec")
                c.ms["msc01_free"] = av
                c.ms["bcast_read"] = av
            act_ms = pg.at("vec")
            for m in range(NCH):
                pg.wait(TE, act_ms, dn_ms[e], db, c.ms.get("ps_d_free"))
                first = True
                if e == 0:
                    nc.tensor.matmul(c.ps_s[:, 512:512 + NT],
                                     c.dnb[:, m * 128:(m + 1) * 128],
                                     c.scoresT[:, t0:t0 + NT], start=True, stop=False)
                    first = False
                for kc in range(4):
                    mm = nc.tensor.matmul(c.ps_s[:, 512:512 + NT],
                                          wdnv[:, kc, m * 128:(m + 1) * 128],
                                          c.actbuf[:, kc * NT:(kc + 1) * NT],
                                          start=first and (kc == 0), stop=(kc == 3))
                dmm = pg.inc(mm, "tns")
                c.ms[("wdn_user", slot)] = dmm
                pg.wait(V, dmm, resid_ms)
                ra = pg.inc(V.tensor_tensor(c.resid[:, m * T + t0: m * T + t0 + NT],
                                            c.resid[:, m * T + t0: m * T + t0 + NT],
                                            c.ps_s[:, 512:512 + NT], ALU.add), "vec")
                c.ms["ps_d_free"] = ra
            c.ms["act_user"] = pg.at("tns")
    pg.barrier()
    es.close()
    return pg.at("vec")


def emit_dnet(pg, c, T, resid_ms):
    nc = pg.nc
    V, S, TE = nc.vector, nc.scalar, nc.tensor
    LP = 256
    npass = T // LP
    nstr = LP // 128
    H, K = HEADS, HD
    es = contextlib.ExitStack()
    _u = "_%d" % pg.uid()
    sb = lambda name, shape, dt: es.enter_context(nc.sbuf_tensor(name + _u, shape, dt))
    LPB = LP
    c.wbas = sb("wbas", [128, 512], BF16)
    c.dtb = sb("dtb", [16, 1], F32)
    c.alog = sb("alog", [16, 1], F32)
    c.nega = sb("nega", [16, 1], F32)
    c.nwcol = sb("nwcol", [128, 1], F32)
    c.cf2 = sb("cf2", [128, 192], F32)
    c.identf = sb("identf", [128, 128], F32)
    c.one16 = sb("one16", [16, 1], F32)
    c.eps6 = sb("eps6", [128, 1], F32)
    c.eps5 = sb("eps5", [128, 1], F32)
    c.q_tok = sb("q_tok", [128, 2 * 1024], BF16)
    c.k_tok = sb("k_tok", [128, 2 * 1024], BF16)
    c.v_tok = sb("v_tok", [128, 2 * 1024], BF16)
    c.g_tok = sb("g_tok", [128, 2 * 1024], BF16)
    c.vtmp2 = sb("vtmp2", [128, LPB], BF16)
    c.dbrows = sb("dbrows", [64, LPB], F32)
    c.dsc = sb("dsc", [16, 2 * LPB], F32)
    c.dbcols = sb("dbcols", [128, 128], F32)
    c.gcols = sb("gcols", [128, 32], F32)
    c.egcols = sb("egcols", [128, 32], F32)
    c.grows = sb("grows", [16, LPB], F32)
    c.colsb = sb("colsb", [128, 96], BF16)
    c.gflat = sb("gflat", [1, 2048], F32)
    c.ecols = sb("ecols", [64, 32], F32)
    c.ecolsb = sb("ecolsb", [128, 32], BF16)
    c.l2c = sb("l2c", [128, 32], F32)
    c.l2cb = sb("l2cb", [128, 16], BF16)
    c.mscb = sb("mscb", [128, 1024], BF16)
    c.cscr = sb("cscr", [64, 4096], BF16)
    c.kT = sb("kT", [64, 1024], BF16)
    c.qT = sb("qT", [64, 1024], BF16)
    c.qeT = sb("qeT", [64, 1024], BF16)
    c.kbT = sb("kbT", [64, 1024], BF16)
    c.ltbuf = sb("ltbuf", [64, 1024], F32)
    c.ltbuf2 = sb("ltbuf2", [64, 1024], F32)
    c.pt_all = sb("pt_all", [64, 1024], BF16)
    c.pkT = sb("pkT", [64, 1024], BF16)
    c.solvp = sb("solvp", [64, 1024], BF16)
    c.ybuf = sb("ybuf", [64, 2048], BF16)
    c.kcdT = sb("kcdT", [64, 1024], BF16)
    c.vnew = sb("vnew", [64, 1024], BF16)
    c.at_all = sb("at_all", [64, 1024], BF16)
    c.s_all = sb("s_all", [64, 1024], F32)
    c.s_b = sb("s_b", [64, 1024], BF16)
    c.och = sb("och", [64, 1024], BF16)
    c.otb = sb("otb", [128, NCH * LPB], BF16)
    c.bslot = [sb("bslot0", [128, 4096], BF16), sb("bslot1", [128, 4096], BF16)]
    c.msc = sb("msc", [128, 3 * 512], F32)
    xnq = sb("xnq_d", [128, NCH * LP], BF16)
    ldf = pg.inc(nc.sync.dma_start(out=c.identf[:], in_=c.cfid[:, :]), "dc7", 16)
    c.ms["identf_ms"] = ldf

    def t64(src_ap, dst_ap, idx, extra=None, sbase=0):
        s = idx % 3
        stg = (c.ps_t[0:64, 0:64],
               c.ps_s[0:64, 1024:1056].bitcast(BF16),
               c.ps_s[0:64, 1536:1568].bitcast(BF16))[s]
        pg.wait(TE, c.ms.get(("t64", s)), extra,
                c.ms.get("ps_t_free") if s == 0 else None)
        tr = pg.inc(nc.tensor.transpose(
            stg, src_ap, c.identb[sbase:sbase + 64, sbase:sbase + 64]), "tns")
        eng = V if idx % 2 == 0 else S
        pg.wait(eng, tr)
        cp = pg.inc((eng.tensor_copy if eng is V else eng.copy)(dst_ap, stg),
                    "vec" if eng is V else "act")
        c.ms[("t64", s)] = cp
        if s == 0:
            c.ms["ps_t_free"] = cp
        return cp

    wba = pg.inc(nc.gpsimd.dma_start(
        out=c.wbas[:], in_=c.wdram["l1b_ba"][0].rearrange("p kc m -> p (kc m)")), "dba", 16)
    dtb = pg.inc(nc.sync.dma_start(out=c.dtb[:], in_=c.wdram["l1b_dtb"][:, :]), "ddt", 16)
    alg = pg.inc(nc.sync.dma_start(out=c.alog[:], in_=c.wdram["l1b_alog"][:, :]), "dal", 16)
    nwl = pg.inc(nc.sync.dma_start(out=c.nwcol[:], in_=c.wdram["l1b_nw"][:, :]), "dnw", 16)
    c2 = pg.inc(nc.sync.dma_start(out=c.cf2[:], in_=c.cf2d[:, :]), "dc6", 16)
    pg.wait(S, alg)
    ea = pg.inc(S.activation(c.nega[:], c.alog[:], AF.Exp), "act")
    pg.wait(V, ea)
    nega = pg.inc(V.tensor_scalar(c.nega[:], c.nega[:], -1.0, None, ALU.mult), "vec")
    pg.wait(V, c.ms.get("onecol16_init"))
    one16 = pg.inc(V.memset(c.one16[:], 1.0), "vec")
    c.ms["onecol16_init"] = one16
    eps6 = pg.inc(V.memset(c.eps6[:], 1e-6), "vec")
    eps5 = pg.inc(V.memset(c.eps5[:], 1e-5), "vec")
    pg.wait(V, c.ms.get("s_user"))
    pg.inc(V.memset(c.s_all[:], 0.0), "vec")
    sb0 = pg.inc(V.memset(c.s_b[:], 0.0), "vec")
    s_ready = sb0

    for q in range(npass):
        t0 = q * LP
        ln_ms = emit_ln(pg, c, T, t0, LP, resid_ms, xnq, LP, dst_t0=0)
        wd = c.wdram["l1b_w"]
        wr = {0: load_w_group(pg, c, wd, 0, NCH, 512, 0, buf=c.bslot, key="bs", sem="dq")}
        for g in range(8):
            sl = g % 2
            if g + 1 < 8:
                wr[g + 1] = load_w_group(pg, c, wd, g + 1, NCH, 512, (g + 1) % 2,
                                         buf=c.bslot, key="bs", sem="dq")
            wv = c.bslot[sl][:, 0:4096].rearrange("p (kc m) -> p kc m", m=512)
            for mi in range(4):
                m = g * 4 + mi
                par = m % 2
                psum = c.ps_a if par == 0 else c.ps_b
                pg.wait(TE, wr[g], ln_ms, c.ms.get("pp_free%d" % par))
                for kc in range(NCH):
                    mm = nc.tensor.matmul(psum[:, 0:LP], wv[:, kc, mi * 128:(mi + 1) * 128],
                                          xnq[:, kc * LP:(kc + 1) * LP],
                                          start=(kc == 0), stop=(kc == NCH - 1))
                mm_ms = pg.inc(mm, "tns")
                c.ms[("bs_user", sl)] = mm_ms
                tensor_i, ch = divmod(m, 8)
                dstbuf = (c.q_tok, c.k_tok, c.v_tok, c.g_tok)[tensor_i]
                pg.wait(S, mm_ms, c.ms.get("vt_free"))
                ev = pg.inc(S.activation(c.vtmp2[:, 0:LP], psum[:, 0:LP], AF.Identity), "act")
                c.ms["pp_free%d" % par] = ev
                for st in range(nstr):
                    pg.wait(TE, ev, c.ms.get("ps_t_free"))
                    tr = pg.inc(nc.tensor.transpose(
                        c.ps_t[:], c.vtmp2[:, st * 128:(st + 1) * 128], c.identb), "tns")
                    pg.wait(V, tr)
                    cp = pg.inc(V.tensor_copy(
                        dstbuf[:, st * 1024 + ch * 128: st * 1024 + (ch + 1) * 128],
                        c.ps_t[:]), "vec")
                    c.ms["ps_t_free"] = cp
                c.ms["vt_free"] = pg.at("tns")
        pg.wait(TE, wba, ln_ms, c.ms.get("dbps_free"))
        for kc in range(NCH):
            mm = nc.tensor.matmul(c.ps_s[0:64, 0:LP],
                                  c.wbas[:, 0:512].rearrange("p (kc m) -> p kc m", m=64)[:, kc, :],
                                  xnq[:, kc * LP:(kc + 1) * LP],
                                  start=(kc == 0), stop=(kc == NCH - 1))
        ba_ms = pg.inc(mm, "tns")
        pg.wait(S, ba_ms, dtb, c.ms.get("dbrows_user"))
        bet = pg.inc(S.activation(c.dbrows[32:48, 0:LP], c.ps_s[0:16, 0:LP], AF.Sigmoid), "act")
        aid = pg.inc(S.activation(c.dsc[0:16, 0:LP], c.ps_s[32:48, 0:LP], AF.Identity,
                                  bias=c.dtb[:, 0:1]), "act")
        c.ms["dbps_free"] = aid
        c.ms[("scanps", 0)] = [c.ms.get(("scanps", 0)), bet, aid]
        pg.wait(S, aid)
        ab = pg.inc(S.activation(c.dsc[0:16, LP:2 * LP], c.dsc[0:16, 0:LP], AF.Abs), "act")
        pg.wait(S, ab)
        ex = pg.inc(S.activation(c.dsc[0:16, LP:2 * LP], c.dsc[0:16, LP:2 * LP], AF.Exp,
                                 scale=-1.0), "act")
        pg.wait(S, ex, one16)
        ln1 = pg.inc(S.activation(c.dsc[0:16, LP:2 * LP], c.dsc[0:16, LP:2 * LP], AF.Ln,
                                  bias=c.one16[:, 0:1]), "act")
        rl = pg.inc(S.activation(c.dsc[0:16, 0:LP], c.dsc[0:16, 0:LP], AF.Relu), "act")
        pg.wait(V, ln1, rl, nega)
        sp = pg.inc(V.tensor_tensor(c.dsc[0:16, 0:LP], c.dsc[0:16, 0:LP],
                                    c.dsc[0:16, LP:2 * LP], ALU.add), "vec")
        pg.wait(V, sp)
        dec = pg.inc(V.tensor_scalar(c.dbrows[0:16, 0:LP], c.dsc[0:16, 0:LP],
                                     c.nega[:, 0:1], None, ALU.mult), "vec")
        pg.wait(V, bet, dec)
        bd_ms = pg.at("vec")
        for st in range(nstr):
            pg.wait(TE, bd_ms, c2, c.ms.get("ps_t_free"), c.ms.get("dbcols_user"),
                    c.ms.get("identf_ms"))
            tr = pg.inc(nc.tensor.transpose(c.ps_o[:, 0:64], c.dbrows[:, st * 128:(st + 1) * 128],
                                            c.identf[0:64, 0:64]), "tns")
            pg.wait(V, tr)
            dbc = pg.inc(V.tensor_copy(c.dbcols[:, st * 64:(st + 1) * 64], c.ps_o[:, 0:64]), "vec")
            c.ms["ps_t_free"] = dbc
            for half in range(2):
                ch = st * 2 + half
                pg.wait(TE, dbc, c.ms.get("gc_ps_free"), c.ms.get("pp_free1"))
                gmm = pg.inc(nc.tensor.matmul(
                    c.ps_b[0:64, 0:16], c.cf2[half * 64:(half + 1) * 64 - 64 + 64, 0:64]
                    if False else c.cf2[half * 64: half * 64 + 64, 0:64],
                    c.dbcols[half * 64:(half + 1) * 64, st * 64: st * 64 + 16],
                    start=True, stop=True), "tns")
                pg.wait(V, gmm)
                gcp = pg.inc(V.tensor_copy(
                    c.gcols[half * 64:(half + 1) * 64, st * 16:(st + 1) * 16],
                    c.ps_b[0:64, 0:16]), "vec")
                c.ms["gc_ps_free"] = gcp
        gc_ms = pg.at("vec")
        c.ms["gl_ps_free"] = [c.ms.get("gl_ps_free"), gc_ms]
        pg.wait(S, gc_ms)
        ega = pg.inc(S.activation(c.egcols[:, 0:nstr * 16], c.gcols[:, 0:nstr * 16], AF.Exp), "act")
        for st in range(nstr):
            pg.wait(TE, gc_ms, c.ms.get("gr_ps_free"))
            tr = pg.inc(nc.tensor.transpose(c.ps_o[0:16, 0:128],
                                            c.gcols[:, st * 16:(st + 1) * 16], c.identf[:, :]), "tns")
            pg.wait(V, tr)
            grc = pg.inc(V.tensor_copy(c.grows[:, st * 128:(st + 1) * 128], c.ps_o[0:16, 0:128]), "vec")
            c.ms["gr_ps_free"] = grc
        gr_ms = pg.at("vec")
        pg.wait(nc.sync, gr_ms, c.ms.get("gscr_read"))
        gw = pg.inc(nc.sync.dma_start(out=c.g_scr[:, :], in_=c.grows[0:16, 0:LP]), "dgw", 16)

        pg.wait(V, ega, gc_ms)
        bcast_cols = pg.inc(V.tensor_copy(
            c.colsb[:, 0:nstr * 16].rearrange("p (s x) -> p s x", x=16),
            c.dbcols[:, 0:nstr * 64].rearrange("p (s x) -> p s x", x=64)[:, :, 32:48]), "vec")
        egb = pg.inc(V.tensor_copy(c.colsb[:, 64:64 + nstr * 16], c.egcols[:, 0:nstr * 16]), "vec")
        pg.wait(V, bcast_cols, egb)
        beg = pg.inc(V.tensor_tensor(c.colsb[:, 32:32 + nstr * 16],
                                     c.colsb[:, 0:nstr * 16],
                                     c.colsb[:, 64:64 + nstr * 16], ALU.mult), "vec")
        pg.wait(V, beg)
        for st in range(nstr):
            for which, buf in (("q", c.q_tok), ("k", c.k_tok)):
                sl = buf[:, st * 1024:(st + 1) * 1024]
                pg.wait(S, pg.at("vec"), c.ms.get("msc_l2_free"))
                sq = pg.inc(S.activation(c.mscb[:, 0:1024], sl, AF.Square), "act")
                pg.wait(V, sq)
                rd = pg.inc(V.tensor_reduce(c.l2c[:, 0:16], seg3(c.mscb[:, 0:1024]),
                                            AX.X, ALU.add), "vec")
                c.ms["msc_l2_free"] = rd
                pg.wait(S, rd, eps6)
                sr = pg.inc(S.activation(c.l2c[:, 16:32], c.l2c[:, 0:16], AF.Sqrt,
                                         bias=c.eps6[:, 0:1]), "act")
                pg.wait(V, sr)
                rc = pg.inc(V.reciprocal(c.l2c[:, 0:16], c.l2c[:, 16:32]), "vec")
                pg.wait(V, rc)
                if which == "q":
                    rc = pg.inc(V.tensor_scalar(c.l2c[:, 0:16], c.l2c[:, 0:16], 0.125,
                                                None, ALU.mult), "vec")
                    pg.wait(V, rc)
                cb_ = pg.inc(V.tensor_copy(c.l2cb[:, 0:16], c.l2c[:, 0:16]), "vec")
                pg.wait(V, cb_)
                nm = pg.inc(V.tensor_tensor(seg3(sl), seg3(sl), segb(c.l2cb[:, 0:16], 16),
                                            ALU.mult), "vec")
            vsl = c.v_tok[:, st * 1024:(st + 1) * 1024]
            pg.wait(V, pg.at("vec"))
            pg.inc(V.tensor_tensor(seg3(vsl), seg3(vsl),
                                   segb(c.colsb[:, st * 16:(st + 1) * 16], 16), ALU.mult), "vec")
        phn_ms = pg.at("vec")

        def issue_gfr(j):
            sl = j % 2
            pg.wait(nc.sync, gw, c.ms.get(("gflatu", sl)))
            ms = pg.inc(nc.sync.dma_start(
                out=c.gflat[0:1, sl * 1024:(sl + 1) * 1024].rearrange("p (h t) -> p h t", t=64),
                in_=c.g_scr.rearrange("h (cc t) -> cc h t", t=64)[j:j + 1]), "dgr%d" % sl, 16)
            c.ms[("gfr", j)] = ms
            return ms

        issue_gfr(0)
        for cc in range(LP // 64):
            st = cc // 2
            half = cc % 2
            pb = half * 64
            gsl = cc % 2
            gview = c.gflat[0:1, gsl * 1024:(gsl + 1) * 1024]
            ctok = lambda buf: buf[:, st * 1024:(st + 1) * 1024]
            if cc + 1 < LP // 64:
                issue_gfr(cc + 1)
            gfr = c.ms.pop(("gfr", cc))
            c.ms["gscr_read"] = [c.ms.get("gscr_read") if cc == 0 else None, gfr]
            pg.wait(TE, gfr, c.ms.get("lt_ps_free"), c.ms.get(("t64", 1)),
                    c.ms.get(("t64", 2)), c.ms.get(("pktst", 0)), c.ms.get(("pktst", 1)))
            nc.tensor.matmul(c.ps_s[0:64, 1024:1536], c.onesf[0:1, 0:64],
                             gview[:, 0:512], start=True, stop=True)
            lmm = pg.inc(nc.tensor.matmul(c.ps_s[0:64, 1536:2048], c.onesf[0:1, 0:64],
                                          gview[:, 512:1024], start=True, stop=True), "tns")
            gseg = segb(c.gcols[pb:pb + 64, st * 16:(st + 1) * 16], 16)
            stseg = c.cf2[0:64, 64:128].unsqueeze(1).broadcast_to((64, 16, 64))
            pg.wait(V, lmm, c.ms.get("msc_lt_free"))
            d1 = pg.inc(V.tensor_tensor(seg3(c.msc[0:64, 0:1024]),
                                        seg3(c.ps_s[0:64, 1024:2048]),
                                        gseg, ALU.subtract), "vec")
            pg.wait(V, d1)
            d2 = pg.inc(V.tensor_tensor(seg3(c.msc[0:64, 0:1024]), seg3(c.msc[0:64, 0:1024]),
                                        stseg, ALU.mult), "vec")
            pg.wait(S, d2)
            d3 = pg.inc(S.activation(c.msc[0:64, 0:1024], c.msc[0:64, 0:1024], AF.Exp), "act")
            pg.wait(V, d3)
            lt_ms = pg.inc(V.tensor_tensor(seg3(c.ltbuf[0:64, 0:1024]), seg3(c.msc[0:64, 0:1024]),
                                           stseg, ALU.mult), "vec")
            c.ms["msc_lt_free"] = lt_ms
            eyeseg = c.cf2[0:64, 128:192].unsqueeze(1).broadcast_to((64, 16, 64))
            pg.wait(V, lt_ms, c.ms.get("lt2_user"))
            lt2_ms = pg.inc(V.tensor_tensor(seg3(c.ltbuf2[0:64, 0:1024]),
                                            seg3(c.ltbuf[0:64, 0:1024]),
                                            eyeseg, ALU.add), "vec")
            pg.wait(TE, gfr, c.ms.get("gl_ps_free"), c.ms.get("pp_free1"))
            glmm = pg.inc(nc.tensor.matmul(
                c.ps_b[0:64, 16:32], c.onesf[0:1, 0:64],
                gview.rearrange("p (h t) -> p h t", t=64)[:, :, 63:64].squeeze(2),
                start=True, stop=True), "tns")
            pg.wait(V, glmm, c.ms.get("ecols_user"))
            ei = pg.inc(V.tensor_tensor(c.ecols[0:64, 0:16],
                                        c.ps_b[0:64, 16:32],
                                        c.gcols[pb:pb + 64, st * 16:(st + 1) * 16],
                                        ALU.subtract), "vec")
            pg.wait(S, ei)
            ei2 = pg.inc(S.activation(c.ecols[0:64, 0:16], c.ecols[0:64, 0:16], AF.Exp), "act")
            egl = pg.inc(S.activation(c.ecols[0:64, 16:32], c.ps_b[0:64, 16:32], AF.Exp), "act")
            c.ms["gl_ps_free"] = egl
            c.ms["sv_ps_free1"] = [c.ms.get("sv_ps_free1"), egl]
            c.ms["sv_ps_free0"] = [c.ms.get("sv_ps_free0"), c.ms.get("pp_free0")]
            c.ms["pt_ps_free0"] = [c.ms.get("pt_ps_free0"), c.ms.get("pp_free0")]
            c.ms["pt_ps_free1"] = [c.ms.get("pt_ps_free1"), c.ms.get("pp_free1"), egl]
            c.ms["at_ps_free0"] = [c.ms.get("at_ps_free0"), c.ms.get("pp_free0")]
            c.ms["at_ps_free1"] = [c.ms.get("at_ps_free1"), c.ms.get("pp_free1"), egl]
            bseg = segb(c.colsb[pb:pb + 64, st * 16:(st + 1) * 16], 16)
            egseg = segb(c.colsb[pb:pb + 64, 64 + st * 16: 64 + (st + 1) * 16], 16)
            begseg = segb(c.colsb[pb:pb + 64, 32 + st * 16: 32 + (st + 1) * 16], 16)
            pg.wait(V, phn_ms, ei2, egl, c.ms.get("cscr_user"))
            eib = pg.inc(V.tensor_copy(c.ecolsb[pb:pb + 64, 0:16], c.ecols[0:64, 0:16]), "vec")
            eglb = pg.inc(V.tensor_copy(c.ecolsb[0:64, 16:32], c.ecols[0:64, 16:32]), "vec")
            kb = pg.inc(V.tensor_tensor(seg3(c.cscr[0:64, 0:1024]),
                                        seg3(ctok(c.k_tok)[pb:pb + 64, :]),
                                        bseg, ALU.mult), "vec")
            kbe = pg.inc(V.tensor_tensor(seg3(c.cscr[0:64, 1024:2048]),
                                         seg3(ctok(c.k_tok)[pb:pb + 64, :]),
                                         begseg, ALU.mult), "vec")
            eiseg = segb(c.ecolsb[pb:pb + 64, 0:16], 16)
            pg.wait(V, eib)
            ke1 = pg.inc(V.tensor_tensor(seg3(c.cscr[0:64, 2048:3072]),
                                         seg3(ctok(c.k_tok)[pb:pb + 64, :]),
                                         eiseg, ALU.mult), "vec")
            qe = pg.inc(V.tensor_tensor(seg3(c.cscr[0:64, 3072:4096]),
                                        seg3(ctok(c.q_tok)[pb:pb + 64, :]),
                                        egseg, ALU.mult), "vec")
            pg.wait(TE, kb, qe, phn_ms, lt_ms, c.ms.get("fm_user"))
            ii = 0
            for h in range(H):
                for srcb, dstc, sbase in (
                        (ctok(c.k_tok)[pb:pb + 64, h * 64:(h + 1) * 64], c.kT, pb),
                        (ctok(c.q_tok)[pb:pb + 64, h * 64:(h + 1) * 64], c.qT, pb),
                        (c.cscr[0:64, 3072 + h * 64: 3072 + (h + 1) * 64], c.qeT, 0),
                        (c.cscr[0:64, h * 64:(h + 1) * 64], c.kbT, 0)):
                    t64(srcb, dstc[0:64, h * 64:(h + 1) * 64], ii, sbase=sbase)
                    ii += 1
            fm_ms = [pg.at("vec"), pg.at("act")]
            for hg in range(4):
                pg.wait(TE, fm_ms, lt_ms, c.ms.get("pt_ps_free%d" % (hg % 2)))
                for hh in range(4):
                    h = hg * 4 + hh
                    mmp = nc.tensor.matmul(
                        (c.ps_a if hg % 2 == 0 else c.ps_b)[0:64, hh * 64:(hh + 1) * 64],
                        c.kT[0:64, h * 64:(h + 1) * 64],
                        c.kbT[0:64, h * 64:(h + 1) * 64], start=True, stop=True)
                mmp_ms = pg.inc(mmp, "tns")
                pg.wait(V, mmp_ms, c.ms.get("ptall_user"))
                ptc = pg.inc(V.tensor_tensor(
                    c.pt_all[0:64, hg * 256:(hg + 1) * 256],
                    (c.ps_a if hg % 2 == 0 else c.ps_b)[0:64, 0:256],
                    c.ltbuf[0:64, hg * 256:(hg + 1) * 256], ALU.mult), "vec")
                c.ms["pt_ps_free%d" % (hg % 2)] = ptc
            pt_ms = pg.at("vec")
            pg.wait(V, pt_ms, c.ms.get("solv_user"))
            yview = c.ybuf[0:64, :].rearrange("p (h two d) -> p h two d", two=2, d=64)
            yi1 = pg.inc(V.tensor_copy(yview[:, :, 0:1, :].squeeze(2),
                                       seg3(ctok(c.v_tok)[pb:pb + 64, :])), "vec")
            yi2 = pg.inc(V.tensor_copy(yview[:, :, 1:2, :].squeeze(2),
                                       seg3(c.cscr[0:64, 1024:2048])), "vec")
            for h in range(H):
                t64(c.pt_all[0:64, h * 64:(h + 1) * 64],
                    c.solvp[0:64, h * 64:(h + 1) * 64], h, extra=pt_ms)
            pg.wait(V, pg.at("act"))
            pg.inc(V.tensor_copy(c.pkT[0:64, :], c.pt_all[0:64, :]), "vec")
            sv_ms = pg.at("vec")
            svreg = [c.ps_a[0:64, 0:192], c.ps_b[0:64, 0:192],
                     c.ps_s[0:64, 0:192], c.ps_s[0:64, 512:704]]
            for it in range(6):
                lastiter = (it == 5)
                for h in range(H):
                    par = h % 4
                    psum = svreg[par]
                    pg.wait(TE, sv_ms, yi1, yi2, c.ms.get("sv_ps_free%d" % par),
                            c.ms.get(("scanps", par - 2)) if par >= 2 else None)
                    if not lastiter:
                        nc.tensor.matmul(psum[:, 0:64], c.pkT[0:64, h * 64:(h + 1) * 64],
                                         c.solvp[0:64, h * 64:(h + 1) * 64],
                                         start=True, stop=True)
                    smm = pg.inc(nc.tensor.matmul(
                        psum[:, 64:192], c.pkT[0:64, h * 64:(h + 1) * 64],
                        c.ybuf[0:64, h * 128:(h + 1) * 128], start=True, stop=True), "tns")
                    pg.wait(V, smm)
                    yup = pg.inc(V.tensor_tensor(c.ybuf[0:64, h * 128:(h + 1) * 128],
                                                 c.ybuf[0:64, h * 128:(h + 1) * 128],
                                                 psum[:, 64:192], ALU.add), "vec")
                    if not lastiter:
                        pk = pg.inc(V.tensor_copy(c.solvp[0:64, h * 64:(h + 1) * 64],
                                                  psum[:, 0:64]), "vec")
                        c.ms["sv_ps_free%d" % par] = pk
                        c.ms[("pkready", h)] = pk
                    else:
                        c.ms["sv_ps_free%d" % par] = yup
                if not lastiter:
                    pkstg = [c.ps_s[0:64, 1024:1056].bitcast(BF16),
                             c.ps_s[0:64, 1536:1568].bitcast(BF16)]
                    for h in range(H):
                        ss = h % 2
                        pg.wait(TE, c.ms.get(("pkready", h)), c.ms.get(("pktst", ss)), lt_ms)
                        tr = pg.inc(nc.tensor.transpose(pkstg[ss],
                                                        c.solvp[0:64, h * 64:(h + 1) * 64],
                                                        c.identb[0:64, 0:64]), "tns")
                        eng = V if h % 2 == 0 else S
                        pg.wait(eng, tr)
                        cp = pg.inc((eng.tensor_copy if eng is V else eng.copy)(
                            c.pkT[0:64, h * 64:(h + 1) * 64], pkstg[ss]),
                            "vec" if eng is V else "act")
                        c.ms[("pktst", ss)] = cp
                    sv_ms = [pg.at("vec"), pg.at("act")]
            y_ms = pg.at("vec")
            for h in range(H):
                t64(c.ybuf[0:64, h * 128 + 64: (h + 1) * 128],
                    c.kcdT[0:64, h * 64:(h + 1) * 64], h, extra=y_ms)
            kcd_ms = [pg.at("vec"), pg.at("act")]
            for hg in range(4):
                pg.wait(TE, kcd_ms, s_ready, c.ms.get(("scanps", hg)))
                for hh in range(4):
                    h = hg * 4 + hh
                    mmw = nc.tensor.matmul(
                        c.ps_s[0:64, hg * 512 + hh * 64: hg * 512 + (hh + 1) * 64],
                        c.kcdT[0:64, h * 64:(h + 1) * 64],
                        c.s_b[0:64, h * 64:(h + 1) * 64],
                        start=True, stop=True)
                w_ms = pg.inc(mmw, "tns")
                pg.wait(V, w_ms, y_ms, c.ms.get("vnew_user"))
                vn = pg.inc(V.tensor_tensor(
                    seg3(c.vnew[0:64, hg * 256:(hg + 1) * 256]),
                    c.ybuf[0:64, :].rearrange("p (h two d) -> p h two d", two=2, d=64)[
                        :, hg * 4:(hg + 1) * 4, 0:1, :].squeeze(2),
                    seg3(c.ps_s[0:64, hg * 512:hg * 512 + 256]), ALU.subtract), "vec")
            vn_ms = pg.at("vec")
            for hg in range(4):
                par = hg % 2
                pg.wait(TE, fm_ms, c.ms.get("at_ps_free%d" % par))
                for hh in range(4):
                    h = hg * 4 + hh
                    mma = nc.tensor.matmul((c.ps_a if par == 0 else c.ps_b)[0:64, hh * 64:(hh + 1) * 64],
                                           c.kT[0:64, h * 64:(h + 1) * 64],
                                           c.qT[0:64, h * 64:(h + 1) * 64],
                                           start=True, stop=True)
                at_ms = pg.inc(mma, "tns")
                pg.wait(V, at_ms, lt2_ms, c.ms.get("atall_user"))
                atc = pg.inc(V.tensor_tensor(
                    c.at_all[0:64, hg * 256:(hg + 1) * 256],
                    (c.ps_a if par == 0 else c.ps_b)[0:64, 0:256],
                    c.ltbuf2[0:64, hg * 256:(hg + 1) * 256], ALU.mult), "vec")
                c.ms["at_ps_free%d" % par] = atc
            at_all_ms = pg.at("vec")
            for hg in range(4):
                pg.wait(TE, at_all_ms, vn_ms, s_ready)
                for hh in range(4):
                    h = hg * 4 + hh
                    off = hg * 512 + hh * 64
                    nc.tensor.matmul(c.ps_s[0:64, off:off + 64],
                                     c.qeT[0:64, h * 64:(h + 1) * 64],
                                     c.s_b[0:64, h * 64:(h + 1) * 64],
                                     start=True, stop=False)
                    mo = nc.tensor.matmul(c.ps_s[0:64, off:off + 64],
                                          c.at_all[0:64, h * 64:(h + 1) * 64],
                                          c.vnew[0:64, h * 64:(h + 1) * 64],
                                          start=False, stop=True)
                mo_ms = pg.inc(mo, "tns")
                pg.wait(V, mo_ms, c.ms.get("och_user"))
                oc = pg.inc(V.tensor_copy(c.och[0:64, hg * 256:(hg + 1) * 256],
                                          c.ps_s[0:64, hg * 512:hg * 512 + 256]), "vec")
                c.ms[("scanps", hg)] = oc
            och_ms = pg.at("vec")
            pg.wait(TE, vn_ms, ke1, och_ms, c.ms.get("supd_ps_free"),
                    c.ms.get(("t64", 1)), c.ms.get(("t64", 2)),
                    c.ms.get(("pktst", 0)), c.ms.get(("pktst", 1)))
            for h in range(H):
                msu = nc.tensor.matmul(c.ps_s[0:64, 1024 + h * 64: 1024 + (h + 1) * 64],
                                       c.cscr[0:64, 2048 + h * 64: 2048 + (h + 1) * 64],
                                       c.vnew[0:64, h * 64:(h + 1) * 64],
                                       start=True, stop=True)
            su_ms = pg.inc(msu, "tns")
            eglseg = segb(c.ecolsb[0:64, 16:32], 16)
            pg.wait(V, su_ms, eglb, och_ms)
            s1 = pg.inc(V.tensor_tensor(seg3(c.s_all[0:64, :]), seg3(c.s_all[0:64, :]),
                                        eglseg, ALU.mult), "vec")
            pg.wait(V, s1)
            s2 = pg.inc(V.tensor_tensor(c.s_all[0:64, :], c.s_all[0:64, :],
                                        c.ps_s[0:64, 1024:2048], ALU.add), "vec")
            c.ms["supd_ps_free"] = s2
            c.ms["lt_ps_free"] = s2
            pg.wait(V, s2)
            s_ready = pg.inc(V.tensor_copy(c.s_b[0:64, :], c.s_all[0:64, :]), "vec")
            for _hg in range(4):
                c.ms[("scanps", _hg)] = [c.ms.get(("scanps", _hg)), s2]
            c.ms["lt2_user"] = pg.at("tns")
            c.ms["ecols_user"] = s_ready
            c.ms["pp_free0"] = s_ready
            c.ms["pp_free1"] = s_ready
            c.ms["bcast_free"] = s_ready
            c.ms["cscr_user"] = pg.at("tns")
            c.ms["vnew_user"] = pg.at("tns")
            c.ms["fm_user"] = pg.at("tns")
            c.ms["solv_user"] = pg.at("tns")
            c.ms["ptall_user"] = pg.at("tns")
            c.ms["atall_user"] = pg.at("tns")
            c.ms[("gflatu", gsl)] = pg.at("tns")
            pg.wait(S, och_ms, c.ms.get("msc_l2_free"))
            sqo = pg.inc(S.activation(c.mscb[0:64, 0:1024], c.och[0:64, :], AF.Square), "act")
            pg.wait(V, sqo)
            rdo = pg.inc(V.tensor_reduce(c.l2c[0:64, 0:16], seg3(c.mscb[0:64, 0:1024]),
                                         AX.X, ALU.add), "vec")
            pg.wait(V, rdo)
            sc1 = pg.inc(V.tensor_scalar(c.l2c[0:64, 0:16], c.l2c[0:64, 0:16], 1.0 / 64,
                                         None, ALU.mult), "vec")
            pg.wait(S, sc1, eps5)
            sr = pg.inc(S.activation(c.l2c[0:64, 16:32], c.l2c[0:64, 0:16], AF.Sqrt,
                                     bias=c.eps5[0:64, 0:1]), "act")
            pg.wait(V, sr)
            rco = pg.inc(V.reciprocal(c.l2c[0:64, 0:16], c.l2c[0:64, 16:32]), "vec")
            pg.wait(V, rco)
            rcb = pg.inc(V.tensor_copy(c.l2cb[0:64, 0:16], c.l2c[0:64, 0:16]), "vec")
            rseg = segb(c.l2cb[0:64, 0:16], 16)
            pg.wait(S, phn_ms, rdo)
            sgt = pg.inc(S.activation(c.cscr[0:64, 0:1024], ctok(c.g_tok)[pb:pb + 64, :],
                                      AF.Silu), "act")
            c.ms["msc_l2_free"] = sgt
            pg.wait(V, rcb, sgt)
            o1 = pg.inc(V.tensor_tensor(seg3(c.och[0:64, :]), seg3(c.och[0:64, :]),
                                        rseg, ALU.mult), "vec")
            pg.wait(V, o1)
            o2 = pg.inc(V.tensor_tensor(c.och[0:64, :], c.och[0:64, :],
                                        c.cscr[0:64, 0:1024], ALU.mult), "vec")
            c.ms["cscr_user"] = [c.ms["cscr_user"], o2]
            for h in range(H):
                t64(c.och[0:64, h * 64:(h + 1) * 64],
                    c.otb[(h % 2) * 64:(h % 2) * 64 + 64,
                          (h // 2) * LP + cc * 64: (h // 2) * LP + (cc + 1) * 64],
                    h, extra=[o2, c.ms.get("ot_user")])
            c.ms["och_user"] = pg.at("tns")
        ot_ms = [pg.at("vec"), pg.at("act")]
        wod = c.wdram["l1b_wo"]
        for g2 in range(2):
            wms = load_w_group(pg, c, wod, g2, NCH, 512, g2 % 2, buf=c.bslot, key="bs", sem="dq")
            pg.wait(V, wms, nwl)
            wsc = pg.inc(V.tensor_scalar(c.bslot[g2 % 2][:, 0:4096], c.bslot[g2 % 2][:, 0:4096],
                                         c.nwcol[:, 0:1], None, ALU.mult), "vec")
            wv = c.bslot[g2 % 2][:, 0:4096].rearrange("p (kc m) -> p kc m", m=512)
            for mi in range(4):
                m = g2 * 4 + mi
                par = m % 2
                psum = c.ps_a if par == 0 else c.ps_b
                pg.wait(TE, wsc, ot_ms, c.ms.get("pp_free%d" % par),
                        c.ms.get("sv_ps_free0"), c.ms.get("sv_ps_free1"),
                        c.ms.get("at_ps_free0"), c.ms.get("at_ps_free1"))
                for kc in range(NCH):
                    mm = nc.tensor.matmul(psum[:, 0:LP], wv[:, kc, mi * 128:(mi + 1) * 128],
                                          c.otb[:, kc * LP: (kc + 1) * LP],
                                          start=(kc == 0), stop=(kc == NCH - 1))
                mm_ms = pg.inc(mm, "tns")
                c.ms[("bs_user", g2 % 2)] = mm_ms
                pg.wait(V, mm_ms, resid_ms, c.ms.get("sc_free"))
                sv = pg.inc(V.tensor_copy(c.sc[:, 0:LP], psum[:, 0:LP]), "vec")
                c.ms["pp_free%d" % par] = sv
                pg.wait(V, sv)
                ra = pg.inc(V.tensor_tensor(c.resid[:, m * T + t0: m * T + t0 + LP],
                                            c.resid[:, m * T + t0: m * T + t0 + LP],
                                            c.sc[:, 0:LP], ALU.add), "vec")
                c.ms["sc_free"] = ra
        c.ms["ot_user"] = pg.at("tns")
    c.ms["s_user"] = pg.at("tns")
    pg.barrier()
    es.close()
    return pg.at("vec")


def emit_mlp(pg, c, T, resid_ms):
    nc = pg.nc
    V, S, TE = nc.vector, nc.scalar, nc.tensor
    NT = 512
    ntiles = T // NT
    es = contextlib.ExitStack()
    _u = "_%d" % pg.uid()
    sb = lambda name, shape, dt: es.enter_context(nc.sbuf_tensor(name + _u, shape, dt))
    c.mslot = [sb("mslot%d" % i, [128, 4096], BF16) for i in range(4)]
    c.hbuf1 = sb("hbuf1", [128, 32 * 512], BF16)
    c.hbuf2 = sb("hbuf2", [128, 32 * 512], BF16)
    c.l1fs = sb("l1fs", [128, 136], F32)
    c.msc = sb("msc", [128, 3 * 512], F32)
    xnq = sb("xnq", [128, NCH * NT], BF16)
    sload = pg.inc(nc.sync.dma_start(out=c.l1fs[:], in_=c.wdram["l1f_s"][:, :]), "dfs", 16)

    def scol(which, m):
        off = {"w1": 0, "w2": 32, "h1": 64, "h2": 96, "w3": 128}[which]
        return c.l1fs[:, off + m: off + m + 1]

    for q in range(ntiles):
        t0 = q * NT
        ln_ms = emit_ln(pg, c, T, t0, NT, resid_ms, xnq, NT, dst_t0=0)
        w1d, w2d = c.wdram["l1f_w1"], c.wdram["l1f_w2"]
        wr = {0: (load_w_group(pg, c, w1d, 0, 8, 512, 0, buf=c.mslot, key="ms", sem="dm"),
                  load_w_group(pg, c, w2d, 0, 8, 512, 1, buf=c.mslot, key="ms", sem="dm"))}
        for g in range(8):
            sl = (g % 2) * 2
            if g + 1 < 8:
                nsl = ((g + 1) % 2) * 2
                wr[g + 1] = (
                    load_w_group(pg, c, w1d, g + 1, 8, 512, nsl, buf=c.mslot, key="ms", sem="dm"),
                    load_w_group(pg, c, w2d, g + 1, 8, 512, nsl + 1, buf=c.mslot, key="ms", sem="dm"))
            v1 = c.mslot[sl][:, 0:4096].rearrange("p (kc m) -> p kc m", m=512)
            v2 = c.mslot[sl + 1][:, 0:4096].rearrange("p (kc m) -> p kc m", m=512)
            for mi in range(4):
                m = g * 4 + mi
                pg.wait(TE, wr[g][0], ln_ms, c.ms.get("pp_free0"), c.ms.get("rows_read"))
                for kc in range(NCH):
                    mm1 = nc.tensor.matmul(c.ps_a[:, 0:NT], v1[:, kc, mi * 128:(mi + 1) * 128],
                                           xnq[:, kc * NT: (kc + 1) * NT],
                                           start=(kc == 0), stop=(kc == NCH - 1))
                pg.wait(TE, wr[g][1], c.ms.get("pp_free1"))
                for kc in range(NCH):
                    mm2 = nc.tensor.matmul(c.ps_b[:, 0:NT], v2[:, kc, mi * 128:(mi + 1) * 128],
                                           xnq[:, kc * NT: (kc + 1) * NT],
                                           start=(kc == 0), stop=(kc == NCH - 1))
                mm_ms = pg.inc(mm2, "tns")
                c.ms[("ms_user", sl)] = mm_ms
                c.ms[("ms_user", sl + 1)] = mm_ms
                pg.wait(S, mm_ms, sload, c.ms.get("msc0_free"))
                a1 = pg.inc(S.activation(c.msc[:, 0:NT], c.ps_a[:, 0:NT], AF.Silu,
                                         scale=scol("w1", m)), "act")
                c.ms["pp_free0"] = a1
                pg.wait(V, mm_ms, sload, c.ms.get("msc1_free"))
                a2 = pg.inc(V.tensor_scalar(c.msc[:, NT:2 * NT], c.ps_b[:, 0:NT],
                                            scol("w2", m), None, ALU.mult), "vec")
                c.ms["pp_free1"] = a2
                pg.wait(V, a1, a2)
                hv = pg.inc(V.tensor_tensor(c.hbuf1[:, m * NT:(m + 1) * NT],
                                            c.msc[:, 0:NT], c.msc[:, NT:2 * NT], ALU.mult), "vec")
                c.ms["msc0_free"] = hv
                c.ms["msc1_free"] = hv
        h_done = pg.at("vec")

        def stage(wname, src, dst, sname, src_ready):
            wd = c.wdram[wname]
            rr = {}
            for k in range(3):
                rr[k] = load_w_group(pg, c, wd, k, 32, 128, k % 4, buf=c.mslot,
                                     key="ms", sem="dm")
            last = None
            for m in range(32):
                sl2 = m % 4
                if m + 3 < 32:
                    rr[m + 3] = load_w_group(pg, c, wd, m + 3, 32, 128, (m + 3) % 4,
                                             buf=c.mslot, key="ms", sem="dm")
                wv = c.mslot[sl2][:, 0:4096].rearrange("p (kc m) -> p kc m", m=128)
                par = m % 2
                psum = c.ps_a if par == 0 else c.ps_b
                pg.wait(TE, rr[m], src_ready, c.ms.get("pp_free%d" % par))
                for kc in range(32):
                    mm = nc.tensor.matmul(psum[:, 0:NT], wv[:, kc, :],
                                          src[:, kc * NT:(kc + 1) * NT],
                                          start=(kc == 0), stop=(kc == 31))
                mm_ms = pg.inc(mm, "tns")
                c.ms[("ms_user", sl2)] = mm_ms
                pg.wait(S, mm_ms, c.ms.get("dst_war"))
                last = pg.inc(S.activation(dst[:, m * NT:(m + 1) * NT], psum[:, 0:NT],
                                           AF.Gelu, scale=scol(sname, m)), "act")
                c.ms["pp_free%d" % par] = last
            return pg.at("act")

        g1_done = stage("l1f_h1", c.hbuf1, c.hbuf2, "h1", h_done)
        c.ms["dst_war"] = pg.at("tns")
        g2_done = stage("l1f_h2", c.hbuf2, c.hbuf1, "h2", g1_done)
        c.ms["dst_war"] = None

        w3d = c.wdram["l1f_w3"]
        rr = {0: load_w_group(pg, c, w3d, 0, 32, 128, 0, buf=c.mslot, key="ms", sem="dm")}
        for m in range(NCH):
            sl2 = m % 2
            if m + 1 < NCH:
                rr[m + 1] = load_w_group(pg, c, w3d, m + 1, 32, 128, (m + 1) % 2,
                                         buf=c.mslot, key="ms", sem="dm")
            wv = c.mslot[sl2][:, 0:4096].rearrange("p (kc m) -> p kc m", m=128)
            par = m % 2
            psum = c.ps_a if par == 0 else c.ps_b
            pg.wait(TE, rr[m], g2_done, c.ms.get("pp_free%d" % par))
            for kc in range(32):
                mm = nc.tensor.matmul(psum[:, 0:NT], wv[:, kc, :],
                                      c.hbuf1[:, kc * NT:(kc + 1) * NT],
                                      start=(kc == 0), stop=(kc == 31))
            mm_ms = pg.inc(mm, "tns")
            c.ms[("ms_user", sl2)] = mm_ms
            pg.wait(V, mm_ms, sload, c.ms.get("sc_free"))
            sv = pg.inc(V.tensor_scalar(c.sc[:, 0:NT], psum[:, 0:NT],
                                        scol("w3", m), None, ALU.mult), "vec")
            c.ms["pp_free%d" % par] = sv
            pg.wait(V, sv, resid_ms)
            ra = pg.inc(V.tensor_tensor(c.resid[:, m * T + t0: m * T + t0 + NT],
                                        c.resid[:, m * T + t0: m * T + t0 + NT],
                                        c.sc[:, 0:NT], ALU.add), "vec")
            c.ms["sc_free"] = ra
    pg.barrier()
    es.close()
    return pg.at("vec")


def emit_ln(pg, c, T, t0, tn, resid_ms, dst, dstT, save_r=False, dst_t0=None):
    nc = pg.nc
    V, S, TE = nc.vector, nc.scalar, nc.tensor
    if dst_t0 is None:
        dst_t0 = t0
    mm = None
    for ch in range(NCH):
        sl = ch % 2
        rs = c.resid[:, ch * T + t0: ch * T + t0 + tn]
        pg.wait(V, resid_ms, c.ms.get(("xbslot", sl)))
        cp = pg.inc(V.tensor_copy(c.xb[:, sl * tn:(sl + 1) * tn], rs), "vec")
        pg.wait(S, cp)
        sq = pg.inc(S.activation(c.sq[:, sl * tn:(sl + 1) * tn],
                                 c.xb[:, sl * tn:(sl + 1) * tn], AF.Square), "act")
        pg.wait(TE, cp, sq)
        if ch == 0:
            pg.wait(TE, c.ms.get("pp_free0"), c.ms.get("pp_free1"), c.ms.get("rows_read"))
        nc.tensor.matmul(c.ps_a[0:1, 0:tn], c.ones_col, c.xb[:, sl * tn:(sl + 1) * tn],
                         start=(ch == 0), stop=(ch == NCH - 1), skip_group_check=True)
        mm = nc.tensor.matmul(c.ps_b[0:1, 0:tn], c.ones_col, c.sq[:, sl * tn:(sl + 1) * tn],
                              start=(ch == 0), stop=(ch == NCH - 1), skip_group_check=True)
        c.ms[("xbslot", sl)] = pg.inc(mm, "tns")
    mm_ms = pg.at("tns")
    pg.wait(V, mm_ms, c.ms.get("rows_free"))
    pg.inc(V.tensor_scalar(c.r_mean[:, 0:tn], c.ps_a[0:1, 0:tn], 1.0 / D, None, ALU.mult), "vec")
    rr = pg.inc(V.tensor_scalar(c.r_var[:, 0:tn], c.ps_b[0:1, 0:tn], 1.0 / D, None, ALU.mult), "vec")
    c.ms["rows_read"] = rr
    pg.wait(V, rr)
    pg.inc(V.tensor_tensor(c.r_t1[:, 0:tn], c.r_mean[:, 0:tn], c.r_mean[:, 0:tn], ALU.mult), "vec")
    pg.wait(V, pg.at("vec"))
    vv = pg.inc(V.tensor_tensor(c.r_var[:, 0:tn], c.r_var[:, 0:tn], c.r_t1[:, 0:tn], ALU.subtract), "vec")
    pg.wait(S, vv)
    sm = pg.inc(S.activation(c.r_t1[:, 0:tn], c.r_var[:, 0:tn], AF.Sqrt,
                             bias=c.eps_col[0:1, :]), "act")
    pg.wait(V, sm)
    pg.inc(V.reciprocal(c.r_rstd[:, 0:tn], c.r_t1[:, 0:tn]), "vec")
    pg.wait(V, pg.at("vec"))
    if save_r:
        pg.inc(V.tensor_copy(c.rrow[0:1, t0:t0 + tn], c.r_rstd[:, 0:tn]), "vec")
    mr = pg.inc(V.tensor_tensor(c.r_t1[:, 0:tn], c.r_mean[:, 0:tn], c.r_rstd[:, 0:tn], ALU.mult), "vec")
    if save_r:
        pg.wait(V, mr)
        pg.inc(V.tensor_copy(c.mrow[0:1, t0:t0 + tn], c.r_t1[:, 0:tn]), "vec")
    pg.wait(TE, mr, c.ms.get("bcast_free"), c.ms.get("scores_free"))
    nc.tensor.matmul(c.ps_s[:, 0:tn], c.ones_row_f, c.r_rstd[:, 0:tn], start=True, stop=True)
    bc = pg.inc(nc.tensor.matmul(c.ps_s[:, 512:512 + tn], c.ones_row_f, c.r_t1[:, 0:tn],
                                 start=True, stop=True), "tns")
    c.ms["rows_free"] = bc
    ov = None
    for ch in range(NCH):
        rs = c.resid[:, ch * T + t0: ch * T + t0 + tn]
        pg.wait(V, bc, resid_ms, c.ms.get("xb32_free"))
        t_ms = pg.inc(V.tensor_tensor(c.xb32[:, 0:tn], rs, c.ps_s[:, 0:tn], ALU.mult), "vec")
        pg.wait(V, t_ms)
        dsl = dst[:, ch * dstT + dst_t0: ch * dstT + dst_t0 + tn]
        ov = pg.inc(V.tensor_tensor(dsl, c.xb32[:, 0:tn], c.ps_s[:, 512:512 + tn], ALU.subtract), "vec")
        c.ms["xb32_free"] = ov
    c.ms["bcast_free"] = ov
    return ov


def load_w_group(pg, c, dram, g, kc_n, m_n, slot, buf=None, key="wslot", sem="dw"):
    nc = pg.nc
    buf = buf if buf is not None else c.wslot
    src = dram[g, :, :, :].rearrange("p kc m -> p (kc m)")
    dst = buf[slot][:, 0:kc_n * m_n]
    pg.wait(nc.gpsimd, c.ms.get((key + "_user", slot)))
    return pg.inc(nc.gpsimd.dma_start(out=dst, in_=src), "%s%d" % (sem, slot), 16)


def emit_attn(pg, c, T, resid_ms, layer):
    nc = pg.nc
    V, S, TE = nc.vector, nc.scalar, nc.tensor
    NT = min(512, T)
    nq = T // 128
    rope = (layer == "l0b")
    l1a = (layer == "l1a")
    MW = 256
    qkv_M = 3072
    wd = c.wdram[layer + "_w"]
    wod = c.wdram[layer + "_wo"]
    es = contextlib.ExitStack()
    _u = "_%d" % pg.uid()
    sb = lambda name, shape, dt: es.enter_context(nc.sbuf_tensor(name + _u, shape, dt))
    c.qbuf = sb("qbuf", [128, NCH * T], BF16)
    c.kbuf = sb("kbuf", [128, NCH * T], BF16)
    c.vbuf = sb("vbuf", [128, (T // 128) * D], BF16)
    c.vtmp = sb("vtmp", [128, NT], BF16)
    c.bias24 = sb("bias24", [128, 24], F32)
    c.bias_bo = sb("bias_bo", [128, 8], F32)
    c.qbias = sb("qbias", [128, 8], F32)
    esx = contextlib.ExitStack()
    sbx = lambda name, shape, dt: esx.enter_context(nc.sbuf_tensor(name + _u, shape, dt))
    if rope:
        c.ropet = sbx("ropet", [128, 2 * T], BF16)
        c.rp = sbx("rp", [128, 2 * NT], BF16)
        c.scb = sbx("scb", [128, NT], BF16)
        r1 = pg.inc(nc.sync.dma_start(out=c.ropet[:], in_=c.ropetd[:, :]), "dc3", 16)
        c.ms["rope_const"] = r1

    if layer in ("l0a", "l1a"):
        b1 = pg.inc(nc.sync.dma_start(out=c.bias24[:], in_=c.wdram[layer + "_b"][:, :]), "db1", 16)
        b2 = pg.inc(nc.sync.dma_start(out=c.bias_bo[:], in_=c.wdram[layer + "_bo"][:, :]), "db2", 16)
        c.ms["bias_ready"] = [b1, b2]
    else:
        c.ms["bias_ready"] = None

    pg.wait(V, c.ms.get("bias_ready"))
    if rope:
        qbs = pg.inc(V.memset(c.qbias[:, 0:8], 0.0), "vec")
        zb = pg.inc(V.memset(c.bias24[:], 0.0), "vec")
    else:
        qbs = pg.inc(V.tensor_scalar(c.qbias[:, 0:8], c.bias24[:, 0:8], 0.125, None, ALU.mult), "vec")
        zb = qbs

    HT = min(512, T)
    ngroups = qkv_M // MW
    xnq = sbx("xnq", [128, NCH * HT], BF16)
    c.wslot = [sbx("wslot0", [128, NCH * MW], BF16), sbx("wslot1", [128, NCH * MW], BF16)]
    for half in range(T // HT):
        ht0 = half * HT
        for t in range(HT // NT):
            ln_ms = emit_ln(pg, c, T, ht0 + t * NT, NT, resid_ms, xnq, HT, dst_t0=t * NT)
        wready = {}
        wready[0] = load_w_group(pg, c, wd, 0, NCH, MW, 0)
        for g in range(ngroups):
            slot = g % 2
            if g + 1 < ngroups:
                wready[g + 1] = load_w_group(pg, c, wd, g + 1, NCH, MW, (g + 1) % 2)
            wv = c.wslot[slot][:, 0:NCH * MW].rearrange("p (kc m) -> p kc m", m=MW)
            for t in range(HT // NT):
                for mi in range(MW // 128):
                    m = g * (MW // 128) + mi
                    par = m % 2
                    psum = c.ps_a if par == 0 else c.ps_b
                    pg.wait(TE, wready[g], ln_ms, c.ms.get("pp_free%d" % par),
                            c.ms.get("rows_read"))
                    for kc in range(NCH):
                        mm = nc.tensor.matmul(psum[:, 0:NT],
                                              wv[:, kc, mi * 128:(mi + 1) * 128],
                                              xnq[:, kc * HT + t * NT: kc * HT + t * NT + NT],
                                              start=(kc == 0), stop=(kc == NCH - 1))
                    mm_ms = pg.inc(mm, "tns")
                    c.ms[("wslot_user", slot)] = mm_ms
                    col = ht0 + t * NT
                    if m < 8:
                        if rope:
                            pg.wait(S, mm_ms, qbs, c.ms.get("scb_free"))
                            ev = pg.inc(S.activation(c.scb[:, 0:NT], psum[:, 0:NT], AF.Identity,
                                                     bias=c.qbias[:, m:m + 1], scale=0.125), "act")
                            ev = emit_rope(pg, c, T, c.qbuf, m, col, NT, ev)
                            c.ms["scb_free"] = ev
                        else:
                            pg.wait(S, mm_ms, qbs)
                            ev = pg.inc(S.activation(c.qbuf[:, m * T + col: m * T + col + NT],
                                                     psum[:, 0:NT], AF.Identity,
                                                     bias=c.qbias[:, m:m + 1], scale=0.125), "act")
                    elif m < 16:
                        ch = m - 8
                        if rope:
                            pg.wait(S, mm_ms, zb, c.ms.get("scb_free"))
                            ev = pg.inc(S.activation(c.scb[:, 0:NT], psum[:, 0:NT], AF.Identity,
                                                     bias=c.bias24[:, m:m + 1]), "act")
                            ev = emit_rope(pg, c, T, c.kbuf, ch, col, NT, ev)
                            c.ms["scb_free"] = ev
                        else:
                            pg.wait(S, mm_ms, zb)
                            ev = pg.inc(S.activation(c.kbuf[:, ch * T + col: ch * T + col + NT],
                                                     psum[:, 0:NT], AF.Identity,
                                                     bias=c.bias24[:, m:m + 1]), "act")
                    else:
                        ch = m - 16
                        pg.wait(S, mm_ms, zb, c.ms.get("vtmp_free"))
                        ev = pg.inc(S.activation(c.vtmp[:, 0:NT], psum[:, 0:NT], AF.Identity,
                                                 bias=c.bias24[:, m:m + 1]), "act")
                        for j4 in range(NT // 128):
                            pg.wait(TE, ev, c.ms.get("ps_t_free"))
                            tr = pg.inc(nc.tensor.transpose(
                                c.ps_t[:], c.vtmp[:, j4 * 128:(j4 + 1) * 128], c.identb), "tns")
                            pg.wait(V, tr)
                            ktile = (col // 128) + j4
                            cp = pg.inc(V.tensor_copy(
                                c.vbuf[:, ktile * D + ch * 128: ktile * D + (ch + 1) * 128],
                                c.ps_t[:]), "vec")
                            c.ms["ps_t_free"] = cp
                        c.ms["vtmp_free"] = pg.at("tns")
                    c.ms["pp_free%d" % par] = ev
    qkv_done = [pg.at("act"), pg.at("vec"), pg.at("tns")]
    pg.barrier()
    esx.close()

    c.pbuf = sb("pbuf", [128, 2 * T], BF16)
    c.pts = [sb("pts0", [128, 128], BF16), sb("pts1", [128, 128], BF16),
             sb("pts2", [128, 128], BF16)]
    c.otbuf = sb("otbuf", [128, NCH * 512], BF16)
    c.rowmax = sb("rowmax", [128, 2], F32)
    c.rowsum = sb("rowsum", [128, 2], F32)
    c.rowrec = sb("rowrec", [128, 2], F32)
    c.gmax = sb("gmax", [128, 10], F32)
    c.wo = sb("wo", [128, NCH * 512], BF16)

    causal = not l1a

    def stage_view(s):
        if s == 0:
            return c.ps_a[:, 0:64].bitcast(BF16)
        if s == 1:
            return c.ps_b[:, 0:64].bitcast(BF16)
        return c.ps_t[:]

    def emit_pv(i, qi, hh, nk):
        par2 = hh % 2
        chq2 = hh // 2
        pb2 = (hh % 2) * 64
        PB = par2 * T
        pn = c.ms[("pn", par2)]

        def emit_tr(j):
            s = j % 3
            fk = "ptc_free%d" % s if s < 2 else "ps_t_free"
            pv = stage_view(s)
            pg.wait(TE, pn, c.ms.get(fk),
                    c.ms.get("pp_free%d" % s) if s < 2 else None)
            tr = pg.inc(nc.tensor.transpose(pv, c.pbuf[:, PB + j * 128: PB + (j + 1) * 128],
                                            c.identb), "tns")
            if s < 2:
                c.ms["pp_free%d" % s] = None
            if j % 2 == 0:
                pg.wait(V, tr, c.ms.get(("pts_user", s)))
                cp = pg.inc(V.tensor_copy(c.pts[s][:], pv), "vec")
            else:
                pg.wait(S, tr, c.ms.get(("pts_user", s)))
                cp = pg.inc(S.copy(c.pts[s][:], pv), "act")
            c.ms[fk] = cp
            c.ms[("flcp", j)] = cp

        emit_tr(0)
        if nk > 1:
            emit_tr(1)
        mo = None
        for j in range(nk):
            if j + 2 < nk:
                emit_tr(j + 2)
            s = j % 3
            pg.wait(TE, c.ms.pop(("flcp", j)), None if j > 0 else c.ms.get("ps_o_free"))
            mo = nc.tensor.matmul(c.ps_o[0:64, 0:128],
                                  c.vbuf[:, j * D + hh * 64: j * D + (hh + 1) * 64],
                                  c.pts[s][:],
                                  start=(j == 0), stop=(j == nk - 1))
            c.ms[("pts_user", s)] = pg.at("tns")
        c.ms["pbf%d" % par2] = pg.at("tns")
        mo_ms = pg.inc(mo, "tns")
        pg.wait(V, mo_ms, c.ms.get("ot_user"))
        oe = pg.inc(V.tensor_copy(
            c.otbuf[pb2:pb2 + 64, chq2 * 512 + qi * 128: chq2 * 512 + qi * 128 + 128],
            c.ps_o[0:64, 0:128]), "vec")
        c.ms["ps_o_free"] = oe

    for qg in range(nq // 4):
        for qi in range(4):
            i = qg * 4 + qi
            nk = (i + 1) if causal else nq
            L = nk * 128
            split = (L <= 1024)
            for h in range(HEADS):
                par = h % 2
                chq = h // 2
                pb = (h % 2) * 64
                RB = par * 1024 if split else 0
                PB = par * T
                qsl = c.qbuf[pb:pb + 64, chq * T + i * 128: chq * T + (i + 1) * 128]
                if split:
                    pg.wait(TE, qkv_done, c.ms.get("sfree%d" % par), c.ms.get("bcast_free"))
                else:
                    pg.wait(TE, qkv_done, c.ms.get("sfree0"), c.ms.get("sfree1"),
                            c.ms.get("bcast_free"))
                ngf = L // 512
                nrem = nk - ngf * 4
                glast = ngf - 1 if nrem == 0 else ngf
                gmms = {}
                for kg in range(ngf):
                    mm = nc.tensor.matmul(
                        c.ps_s[:, RB + kg * 512: RB + (kg + 1) * 512], qsl,
                        c.kbuf[pb:pb + 64, chq * T + kg * 512: chq * T + (kg + 1) * 512],
                        start=True, stop=True)
                    gmms[kg] = pg.inc(mm, "tns")
                for j in range(ngf * 4, nk):
                    mm = nc.tensor.matmul(
                        c.ps_s[:, RB + j * 128: RB + (j + 1) * 128], qsl,
                        c.kbuf[pb:pb + 64, chq * T + j * 128: chq * T + (j + 1) * 128],
                        start=True, stop=True)
                if nrem:
                    gmms[ngf] = pg.inc(mm, "tns")
                mm_ms = gmms[glast]
                if h > 0:
                    emit_pv(i, qi, h - 1, nk)
                if causal:
                    pg.wait(V, mm_ms)
                    mm_ms = pg.inc(V.tensor_tensor(
                        c.ps_s[:, RB + (nk - 1) * 128: RB + nk * 128],
                        c.ps_s[:, RB + (nk - 1) * 128: RB + nk * 128],
                        c.maskf[:], ALU.add), "vec")
                    gmms[glast] = mm_ms
                for kg in sorted(gmms):
                    wdt = 512 if kg < ngf else nrem * 128
                    pg.wait(V, gmms[kg], c.ms.get(("gmf", par)))
                    pg.inc(V.tensor_reduce(c.gmax[:, par * 5 + kg: par * 5 + kg + 1],
                                           c.ps_s[:, RB + kg * 512: RB + kg * 512 + wdt],
                                           AX.X, ALU.max), "vec")
                pg.wait(V, pg.at("vec"), c.ms.get(("rmf", par)))
                rm = pg.inc(V.tensor_reduce(c.rowmax[:, par:par + 1],
                                            c.gmax[:, par * 5: par * 5 + len(gmms)],
                                            AX.X, ALU.max, negate=True), "vec")
                c.ms[("gmf", par)] = rm
                pg.wait(S, rm, c.ms.get("pbf%d" % par), c.ms.get(("rsf", par)))
                ex = pg.inc(S.activation(c.pbuf[:, PB:PB + L], c.ps_s[:, RB:RB + L], AF.Exp,
                                         bias=c.rowmax[:, par:par + 1],
                                         accum_out=c.rowsum[:, par:par + 1]), "act")
                if split:
                    c.ms["sfree%d" % par] = ex
                else:
                    c.ms["sfree0"] = ex
                    c.ms["sfree1"] = ex
                c.ms[("rmf", par)] = ex
                pg.wait(V, ex, c.ms.get(("rrf", par)))
                rc = pg.inc(V.reciprocal(c.rowrec[:, par:par + 1], c.rowsum[:, par:par + 1]), "vec")
                c.ms[("rsf", par)] = rc
                pg.wait(V, rc)
                pn = pg.inc(V.tensor_scalar(c.pbuf[:, PB:PB + L], c.pbuf[:, PB:PB + L],
                                            c.rowrec[:, par:par + 1], None, ALU.mult), "vec")
                c.ms[("rrf", par)] = pn
                c.ms[("pn", par)] = pn
            emit_pv(i, qi, HEADS - 1, nk)
        ot_ms = pg.at("vec")
        t0 = qg * 512
        wvv = c.wo[:, :].rearrange("p (kc m) -> p kc m", m=512)
        for m in range(NCH):
            par = m % 2
            psum = c.ps_a if par == 0 else c.ps_b
            if m % 4 == 0:
                pg.wait(nc.gpsimd, c.ms.get("wo_user"))
                wo_ms = pg.inc(nc.gpsimd.dma_start(
                    out=c.wo[:, :].rearrange("p (kc m) -> p kc m", m=512),
                    in_=wod[0, :, :, (m // 4) * 512:(m // 4) * 512 + 512]), "dwo", 16)
            pg.wait(TE, ot_ms, wo_ms, c.ms.get("pp_free%d" % par), c.ms.get("ptc_free%d" % par))
            for kc in range(NCH):
                mm = nc.tensor.matmul(psum[:, 0:512], wvv[:, kc, (m % 4) * 128:(m % 4) * 128 + 128],
                                      c.otbuf[:, kc * 512:(kc + 1) * 512],
                                      start=(kc == 0), stop=(kc == NCH - 1))
            mm_ms = pg.inc(mm, "tns")
            c.ms["wo_user"] = mm_ms
            bia = c.zero_col[:, 0:1] if rope else c.bias_bo[:, m:m + 1]
            pg.wait(S, mm_ms, c.ms.get("bias_ready"), c.ms.get("sc_free"))
            ad = pg.inc(S.activation(c.sc[:, 0:512], psum[:, 0:512], AF.Identity, bias=bia), "act")
            c.ms["pp_free%d" % par] = ad
            c.ms["ptc_free%d" % par] = ad
            pg.wait(V, ad, resid_ms)
            ra = pg.inc(V.tensor_tensor(c.resid[:, m * T + t0: m * T + t0 + 512],
                                        c.resid[:, m * T + t0: m * T + t0 + 512],
                                        c.sc[:, 0:512], ALU.add), "vec")
            c.ms["sc_free"] = ra
        c.ms["ot_user"] = pg.at("tns")
    pg.barrier()
    es.close()
    return pg.at("vec")


def emit_rope(pg, c, T, buf, ch, col, NT, ev):
    nc = pg.nc
    V = nc.vector
    last = ev
    for hh in range(2):
        be, bo = hh * 64, hh * 64 + 32
        es = c.scb[be:be + 32, 0:NT]
        os_ = c.scb[bo:bo + 32, 0:NT]
        ed = buf[be:be + 32, ch * T + col: ch * T + col + NT]
        od = buf[bo:bo + 32, ch * T + col: ch * T + col + NT]
        cos_e = c.ropet[be:be + 32, col:col + NT]
        sin_e = c.ropet[be:be + 32, T + col: T + col + NT]
        cos_o = c.ropet[bo:bo + 32, col:col + NT]
        sin_o = c.ropet[bo:bo + 32, T + col: T + col + NT]
        sA = c.rp[:, 0:NT]
        sB = c.rp[:, NT:2 * NT]
        pg.wait(V, ev, c.ms.get("rope_scr_free"), c.ms.get("rope_const"))
        pg.inc(V.tensor_tensor(sA[be:be + 32, :], es, cos_e, ALU.mult), "vec")
        pg.inc(V.tensor_tensor(sB[be:be + 32, :], os_, sin_o, ALU.mult), "vec")
        pg.inc(V.tensor_tensor(sA[bo:bo + 32, :], os_, cos_o, ALU.mult), "vec")
        t4 = pg.inc(V.tensor_tensor(sB[bo:bo + 32, :], es, sin_e, ALU.mult), "vec")
        pg.wait(V, t4)
        pg.inc(V.tensor_tensor(ed, sA[be:be + 32, :], sB[be:be + 32, :], ALU.subtract), "vec")
        last = pg.inc(V.tensor_tensor(od, sA[bo:bo + 32, :], sB[bo:bo + 32, :], ALU.add), "vec")
        c.ms["rope_scr_free"] = last
    return last



WSPECS = {
    "l0a_w": [12, 128, 8, 256], "l0a_b": [128, 24], "l0a_wo": [1, 128, 8, 1024], "l0a_bo": [128, 8],
    "l0b_w": [12, 128, 8, 256], "l0b_wo": [1, 128, 8, 1024],
    "l1a_w": [12, 128, 8, 256], "l1a_b": [128, 24], "l1a_wo": [1, 128, 8, 1024], "l1a_bo": [128, 8],
    "moe_r": [D, 16], "moe_gb": [128, 128], "moe_db": [16, D],
    "moe_gu": [16, 128, 8, 1024], "moe_dn": [16, 128, 4, 1024],
    "l1f_w1": [8, 128, 8, 512], "l1f_w2": [8, 128, 8, 512],
    "l1f_h1": [32, 128, 32, 128], "l1f_h2": [32, 128, 32, 128],
    "l1f_w3": [8, 128, 32, 128], "l1f_s": [128, 136],
    "l1b_w": [8, 128, 8, 512], "l1b_ba": [1, 128, 8, 64],
    "l1b_wo": [2, 128, 8, 512], "l1b_dtb": [16, 1], "l1b_alog": [16, 1],
    "l1b_nw": [128, 1],
}
CF2_SHAPE = [128, 192]
SL_WEIGHTS = {
    "l0a": ["l0a_w", "l0a_b", "l0a_wo", "l0a_bo"],
    "l0b": ["l0b_w", "l0b_wo"],
    "l1a": ["l1a_w", "l1a_b", "l1a_wo", "l1a_bo"],
    "l0m": ["moe_r", "moe_gb", "moe_db", "moe_gu", "moe_dn"],
    "l1f": ["l1f_w1", "l1f_w2", "l1f_h1", "l1f_h2", "l1f_w3", "l1f_s"],
    "l1b": ["l1b_w", "l1b_ba", "l1b_wo", "l1b_dtb", "l1b_alog", "l1b_nw"],
}
ATTN_SET = ("l0a", "l0b", "l1a")


def build_program(T, sublayers, n_seq=2):
    nc = bass.Bass("TRN2", target_bir_lowering=False)
    stack = contextlib.ExitStack()
    pg = Prog(nc, stack)
    c = Ctx()
    c.ms = {}

    xin = nc.dram_tensor("xin", [n_seq, D, T], F32, kind="ExternalInput").ap()
    yout = nc.dram_tensor("yout", [n_seq, D, T], F32, kind="ExternalOutput").ap()
    cb = nc.dram_tensor("cb", [128, 256], BF16, kind="ExternalInput").ap()
    cf = nc.dram_tensor("cf", [128, 128], F32, kind="ExternalInput").ap()
    if "l0b" in sublayers:
        c.ropetd = nc.dram_tensor("ropet", [128, 2 * T], BF16, kind="ExternalInput").ap()
    if "l0m" in sublayers:
        c.cindd = nc.dram_tensor("cind", [16, 16 * 128], BF16, kind="ExternalInput").ap()
    if "l1b" in sublayers:
        c.cf2d = nc.dram_tensor("cf2d", [128, 192], F32, kind="ExternalInput").ap()
        c.cfid = nc.dram_tensor("cfi", [128, 128], F32, kind="ExternalInput").ap()
        c.g_scr = nc.dram_tensor("g_scr", [16, 256], F32).ap()
    c.wdram = {}
    need = []
    for sl in sublayers:
        for wn in SL_WEIGHTS[sl]:
            if wn not in need:
                need.append(wn)
    for name in need:
        c.wdram[name] = nc.dram_tensor(name, WSPECS[name], F32, kind="ExternalInput").ap()

    sb = lambda name, shape, dt: stack.enter_context(nc.sbuf_tensor(name, shape, dt))
    ps = lambda name, shape, dt: stack.enter_context(nc.psum_tensor(name, shape, dt))

    NT = min(512, T)
    c.resid = sb("resid", [128, NCH * T], F32)
    c.constb = sb("constb", [128, 256], BF16)
    c.maskf = sb("maskf", [128, 128], F32)
    c.identb = c.constb[:, 0:128]
    c.ones_col = c.constb[:, 128:129]
    c.onesf = sb("onesf", [1, 128], F32)
    c.eps_col = sb("epsc", [1, 1], F32)
    c.zero_col = sb("zeroc", [128, 1], F32)
    c.xb = sb("xb", [128, 2 * NT], BF16)
    c.xb32 = sb("xb32", [128, NT], F32)
    c.sq = sb("sq", [128, 2 * NT], BF16)
    c.r4 = sb("r4", [1, 4 * NT], F32)
    c.r_mean = c.r4[0:1, 0:NT]
    c.r_var = c.r4[0:1, NT:2 * NT]
    c.r_t1 = c.r4[0:1, 2 * NT:3 * NT]
    c.r_rstd = c.r4[0:1, 3 * NT:4 * NT]
    c.sc = sb("sc", [128, 512], F32)

    c.ps_s = ps("ps_s", [128, 2048], F32)
    c.ps_a = ps("ps_a", [128, 512], F32)
    c.ps_b = ps("ps_b", [128, 512], F32)
    c.ps_o = ps("ps_o", [128, 128], F32)
    c.ps_t = ps("ps_t", [128, 128], BF16)

    c.ones_row_f = c.onesf[0:1, :]

    ld1 = pg.inc(nc.sync.dma_start(out=c.constb[:], in_=cb[:, :]), "dc1", 16)
    ld2 = pg.inc(nc.sync.dma_start(out=c.maskf[:], in_=cf[:, :]), "dc2", 16)
    pg.wait(nc.vector, ld1, ld2)
    pg.inc(nc.vector.memset(c.onesf[:], 1.0), "vec")
    pg.inc(nc.vector.memset(c.eps_col[:], 1e-5), "vec")
    pg.inc(nc.vector.memset(c.zero_col[:], 0.0), "vec")
    init_v = pg.at("vec")
    for eng in (nc.tensor, nc.scalar):
        pg.wait(eng, init_v, ld1, ld2)

    for s in range(n_seq):
        pg.wait(nc.sync, c.ms.get("resid_stored"))
        rl = pg.inc(nc.sync.dma_start(
            out=c.resid[:].rearrange("p (ch t) -> p ch t", ch=NCH),
            in_=xin[s, :, :].rearrange("(ch p) t -> p ch t", p=128)), "dx", 16)
        resid_ms = rl
        for sl in sublayers:
            pg.barrier()
            if sl == "l0m":
                emit_moe(pg, c, T, resid_ms)
            elif sl == "l1f":
                emit_mlp(pg, c, T, resid_ms)
            elif sl == "l1b":
                emit_dnet(pg, c, T, resid_ms)
            else:
                emit_attn(pg, c, T, resid_ms, layer=sl)
            pg.barrier()
        done = pg.at("vec")
        pg.wait(nc.sync, done)
        st = pg.inc(nc.sync.dma_start(
            out=yout[s, :, :].rearrange("(ch p) t -> p ch t", p=128),
            in_=c.resid[:].rearrange("p (ch t) -> p ch t", ch=NCH)), "dy", 16)
        c.ms["resid_stored"] = st
        pg.barrier()

    pg.finish()
    stack.close()
    return nc



def const_inputs(sublayers, consts):
    m = {"cb": consts["cb"], "cf": consts["cf"]}
    if "l0b" in sublayers:
        m["ropet"] = consts["ropet"]
    if "l0m" in sublayers:
        m["cind"] = consts["cind"]
    if "l1b" in sublayers:
        m["cf2d"] = consts["cf2"]
        m["cfi"] = consts["cfi"]
    return m


def kernel(**inputs):
    T = 2048
    x = np.ascontiguousarray(np.transpose(inputs["L0_attn_a_hidden_states"], (0, 2, 1)))
    w = prep_weights(inputs)
    consts = make_consts(T)
    groups = [["l0a", "l0m", "l0b", "l0m", "l1a", "l1f", "l1b", "l1f"]]
    cur = x
    for sublayers in groups:
        nc = build_program(T, sublayers)
        need = []
        for sl in sublayers:
            for wn in SL_WEIGHTS[sl]:
                if wn not in need:
                    need.append(wn)
        in_maps = []
        for core in range(8):
            m = {"xin": cur[core * 2:(core + 1) * 2]}
            m.update(const_inputs(sublayers, consts))
            for name in need:
                m[name] = w[name]
            in_maps.append(m)
        res = run_bass_kernel_spmd(nc, in_maps, core_ids=list(range(8)))
        cur = np.concatenate([r["yout"] for r in res.results], axis=0)
    return np.ascontiguousarray(np.transpose(cur, (0, 2, 1)))


# version 16
# speedup vs baseline: 1.1331x; 1.0995x over previous
import contextlib
import numpy as np
import ml_dtypes
import concourse.bass as bass
import concourse.mybir as mybir
from concourse.bass_utils import run_bass_kernel_spmd

F32 = mybir.dt.float32
BF16 = mybir.dt.bfloat16
AF = mybir.ActivationFunctionType
ALU = mybir.AluOpType
AX = mybir.AxisListType
BF = ml_dtypes.bfloat16

D = 1024
NCH = 8
HEADS = 16
HD = 64
NEG = -1.0e30



def _pack_bias(b):
    return np.ascontiguousarray(b.reshape(-1, 128).T).astype(np.float32)


def _rope_tables(T, d=HD, theta=10000.0):
    inv = 1.0 / theta ** (np.arange(0, d, 2, dtype=np.float64) / d)
    t = np.arange(T, dtype=np.float64)
    fr = np.outer(inv, t)
    return np.cos(fr).astype(np.float32), np.sin(fr).astype(np.float32)


def _rope_perm():
    p = []
    for h in range(HEADS):
        base = h * HD
        p += [base + i for i in range(0, HD, 2)]
        p += [base + i for i in range(1, HD, 2)]
    return np.array(p)


def tile_w(W, kcn, mw):
    K, M = W.shape
    assert K == kcn * 128 and M % mw == 0
    G = M // mw
    return np.ascontiguousarray(
        W.reshape(kcn, 128, G, mw).transpose(2, 1, 0, 3))


def prep_weights(iv):
    w = {}
    w["l0a_w"] = np.ascontiguousarray(iv["L0_attn_a_c_attn_kernel"].T)
    w["l0a_b"] = _pack_bias(iv["L0_attn_a_c_attn_bias"])
    w["l0a_wo"] = np.ascontiguousarray(iv["L0_attn_a_c_proj_kernel"].T)
    w["l0a_bo"] = _pack_bias(iv["L0_attn_a_c_proj_bias"])
    wq = iv["L0_attn_b_w_qkv"]
    perm = _rope_perm()
    cols = np.concatenate([perm, 1024 + perm, 2048 + np.arange(1024)])
    w["l0b_w"] = np.ascontiguousarray(wq[:, cols])
    w["l0b_wo"] = np.ascontiguousarray(iv["L0_attn_b_w_out"])
    w["l1a_w"] = np.ascontiguousarray(np.concatenate(
        [iv["L1_attn_a_Wq"].T, iv["L1_attn_a_Wk"].T, iv["L1_attn_a_Wv"].T], axis=1))
    w["l1a_b"] = _pack_bias(np.concatenate(
        [iv["L1_attn_a_bq"], iv["L1_attn_a_bk"], iv["L1_attn_a_bv"]]))
    w["l1a_wo"] = np.ascontiguousarray(iv["L1_attn_a_Wo"].T)
    w["l1a_bo"] = _pack_bias(iv["L1_attn_a_bo"])
    deint = np.concatenate([np.arange(0, 1024, 2), np.arange(1, 1024, 2)])
    w["moe_r"] = np.ascontiguousarray(iv["L0_ffn_router_weight"])
    w["moe_gu"] = np.ascontiguousarray(iv["L0_ffn_gate_up_proj"][:, :, deint])
    gbd = iv["L0_ffn_gate_up_proj_bias"][:, deint]
    w["moe_gb"] = np.ascontiguousarray(gbd.reshape(16, 8, 128).transpose(2, 0, 1).reshape(128, 128)).astype(np.float32)
    w["moe_dn"] = np.ascontiguousarray(iv["L0_ffn_down_proj"])
    w["moe_db"] = np.ascontiguousarray(iv["L0_ffn_down_proj_bias"])
    w["l1f_w1"] = tile_w(iv["L1_ffn_w1_w"].T, 8, 512)
    w["l1f_w2"] = tile_w(iv["L1_ffn_w2_w"].T, 8, 512)
    w["l1f_h1"] = tile_w(iv["L1_ffn_h1_w"].T, 32, 128)
    w["l1f_h2"] = tile_w(iv["L1_ffn_h2_w"].T, 32, 128)
    w["l1f_w3"] = tile_w(iv["L1_ffn_w3_w"].T, 32, 128)
    w["l1f_s"] = np.concatenate([
        _pack_bias(iv["L1_ffn_w1_s"]), _pack_bias(iv["L1_ffn_w2_s"]),
        _pack_bias(iv["L1_ffn_h1_s"]), _pack_bias(iv["L1_ffn_h2_s"]),
        _pack_bias(iv["L1_ffn_w3_s"])], axis=1)
    w["l1b_w"] = tile_w(np.ascontiguousarray(np.concatenate(
        [iv["L1_attn_b_Wq"], iv["L1_attn_b_Wk"], iv["L1_attn_b_Wv"], iv["L1_attn_b_Wg"]],
        axis=1)), 8, 512)
    ba = np.zeros((1024, 64), dtype=np.float32)
    ba[:, 0:16] = iv["L1_attn_b_Wb"]
    ba[:, 32:48] = iv["L1_attn_b_Wa"]
    w["l1b_ba"] = tile_w(ba, 8, 64)
    w["l1b_wo"] = tile_w(np.ascontiguousarray(iv["L1_attn_b_Wo"]), 8, 512)
    w["l1b_dtb"] = np.ascontiguousarray(iv["L1_attn_b_dt_bias"][:, None]).astype(np.float32)
    w["l1b_alog"] = np.ascontiguousarray(iv["L1_attn_b_A_log"][:, None]).astype(np.float32)
    w["l1b_nw"] = np.ascontiguousarray(np.tile(iv["L1_attn_b_norm_w"], 2)[:, None]).astype(np.float32)
    for nm in ("l0a_w", "l0b_w", "l1a_w"):
        w[nm] = tile_w(w[nm], 8, 256)
    for nm in ("l0a_wo", "l0b_wo", "l1a_wo"):
        w[nm] = tile_w(w[nm], 8, 1024)
    w["moe_gu"] = np.ascontiguousarray(np.stack([tile_w(e, 8, 1024) for e in w["moe_gu"]])[:, 0])
    w["moe_dn"] = np.ascontiguousarray(np.stack(
        [tile_w(np.ascontiguousarray(e), 4, 1024) for e in w["moe_dn"]])[:, 0])
    return w


def make_consts(T):
    cb = np.zeros((128, 256), dtype=BF)
    cb[:, 0:128] = np.eye(128)
    cb[:, 128:256] = 1.0
    q = np.arange(128)[:, None]
    k = np.arange(128)[None, :]
    cf = np.where(k <= q, 0.0, NEG).astype(np.float32)
    cos1, sin1 = _rope_tables(T)
    ropet = np.zeros((128, 2 * T), dtype=BF)
    for qd in range(4):
        ropet[qd * 32:(qd + 1) * 32, 0:T] = cos1
        ropet[qd * 32:(qd + 1) * 32, T:2 * T] = sin1
    cind = np.zeros((16, 16 * 128), dtype=BF)
    for e in range(16):
        cind[e, e * 128:(e + 1) * 128] = 1.0
    cf2 = np.zeros((128, 128), dtype=np.float32)
    j = np.arange(64)[:, None]
    cm = np.arange(64)[None, :]
    cf2[0:64, 0:64] = (j <= cm).astype(np.float32)
    cf2[64:128, 0:64] = cf2[0:64, 0:64]
    cf2[0:64, 64:128] = (j < cm).astype(np.float32)
    cf2[64:128, 64:128] = cf2[0:64, 64:128]
    cf2i = np.zeros((128, 64), dtype=np.float32)
    cf2i[0:64] = np.eye(64, dtype=np.float32)
    cf2i[64:128] = np.eye(64, dtype=np.float32)
    cf2 = np.concatenate([cf2, cf2i], axis=1)
    cfi = np.eye(128, dtype=np.float32)
    return {"cb": cb, "cf": cf, "ropet": ropet, "cind": cind,
            "cf2": cf2, "cfi": cfi}



class Prog:
    def __init__(self, nc, stack):
        self.nc = nc
        self.stack = stack
        self._sems = {}
        self.cnt = {}
        self._waited = {}
        self._uid = 0

    def uid(self):
        self._uid += 1
        return self._uid

    def sem(self, name):
        if name not in self._sems:
            self._sems[name] = self.stack.enter_context(self.nc.semaphore(name))
            self.cnt[name] = 0
        return self._sems[name]

    def inc(self, inst, name, n=1):
        inst.then_inc(self.sem(name), n)
        self.cnt[name] += n
        return (name, self.cnt[name])

    def at(self, name):
        return (name, self.cnt.get(name, 0))

    def _flat(self, mss):
        for ms in mss:
            if ms is None:
                continue
            if isinstance(ms, list):
                yield from self._flat(ms)
            elif isinstance(ms[0], tuple):
                yield from self._flat(ms)
            else:
                yield ms

    def wait(self, eng, *mss):
        key = id(eng)
        w = self._waited.setdefault(key, {})
        for name, v in self._flat(mss):
            if v <= 0 or w.get(name, 0) >= v:
                continue
            eng.wait_ge(self.sem(name), v)
            w[name] = v

    def finish(self):
        for name, v in sorted(self.cnt.items()):
            if name.startswith("d") and v > 0:
                self.nc.sync.wait_ge(self.sem(name), v)

    def barrier(self):
        nc = self.nc
        nc.all_engine_barrier()
        for e in (nc.tensor, nc.vector, nc.scalar, nc.gpsimd, nc.sync):
            w = self._waited.setdefault(id(e), {})
            for s in ("tns", "vec", "act", "pool", "sp"):
                if s in self.cnt:
                    w[s] = max(w.get(s, 0), self.cnt[s])



class Ctx:
    pass


def seg3(buf_slice, d=64):
    return buf_slice.rearrange("p (h d) -> p h d", d=d)


def segb(col_slice, n, d=64):
    p = col_slice.shape[0]
    return col_slice.unsqueeze(2).broadcast_to((p, n, d))


def emit_moe(pg, c, T, resid_ms):
    nc = pg.nc
    V, S, TE = nc.vector, nc.scalar, nc.tensor
    NT = 512
    ntiles = T // NT
    nblk = T // 128
    E = 16
    es = contextlib.ExitStack()
    _u = "_%d" % pg.uid()
    sb = lambda name, shape, dt: es.enter_context(nc.sbuf_tensor(name + _u, shape, dt))
    c.xnfull = sb("xnfull", [128, NCH * T], BF16)
    c.rtr32 = sb("rtr32", [128, NCH * 16], F32)
    c.gub = sb("gub", [128, 128], F32)
    c.dnb = sb("dnb", [16, D], BF16)
    c.indm = sb("indm", [16, 16 * 128], BF16)
    c.scoresT = sb("scoresT", [16, T], BF16)
    c.scbf = sb("scbf", [128, 16], BF16)
    c.col1 = sb("col1", [128, 1], F32)
    c.col2 = sb("col2", [128, 1], F32)
    c.col3 = sb("col3", [128, 1], F32)
    c.col4 = sb("col4", [128, 1], F32)
    c.rrow = sb("rrow", [1, T], BF16)
    c.mrow = sb("mrow", [1, T], BF16)
    c.crow = sb("crow", [1, 16], F32)
    c.crb = sb("crb", [128, 16], F32)
    c.onescf = sb("onescf", [128, 1], F32)
    c.wgu = [sb("wgu0", [128, NCH * 1024], BF16), sb("wgu1", [128, NCH * 1024], BF16)]
    c.wdn = [sb("wdn0", [128, 4 * 1024], BF16), sb("wdn1", [128, 4 * 1024], BF16)]
    c.actbuf = sb("actbuf", [128, 8 * 512], BF16)
    c.msc = sb("msc", [128, 3 * 512], F32)
    ldi = pg.inc(nc.sync.dma_start(out=c.indm[:], in_=c.cindd[:, :]), "dc5", 16)
    c.ms["indm_ms"] = ldi

    rt = pg.inc(nc.sync.dma_start(
        out=c.rtr32[:].rearrange("p (kc e) -> p kc e", e=16),
        in_=c.wdram["moe_r"].rearrange("(kc p) e -> p kc e", p=128)), "drt", 16)
    gb = pg.inc(nc.sync.dma_start(out=c.gub[:], in_=c.wdram["moe_gb"][:, :]), "dgb", 16)
    db = pg.inc(nc.gpsimd.dma_start(out=c.dnb[:], in_=c.wdram["moe_db"][:, :]), "ddb", 16)

    for t in range(ntiles):
        ln_ms = emit_ln(pg, c, T, t * NT, NT, resid_ms, c.xnfull, T, save_r=True)

    oc = pg.inc(nc.vector.memset(c.onescf[:], 1.0), "vec")
    pg.wait(TE, rt, oc, c.ms.get("pp_free0"), c.ms.get("rows_read"))
    for kc in range(NCH):
        mmc = nc.tensor.matmul(c.ps_a[0:1, 0:16], c.onescf[:],
                               c.rtr32[:].rearrange("p (kc e) -> p kc e", e=16)[:, kc, :],
                               start=(kc == 0), stop=(kc == NCH - 1))
    mmc_ms = pg.inc(mmc, "tns")
    pg.wait(V, mmc_ms)
    crc = pg.inc(V.tensor_copy(c.crow[:], c.ps_a[0:1, 0:16]), "vec")
    pg.wait(TE, crc)
    crb_mm = pg.inc(nc.tensor.matmul(c.ps_a[:, 16:32], c.ones_row_f, c.crow[0:1, :],
                                     start=True, stop=True), "tns")
    pg.wait(V, crb_mm)
    crb = pg.inc(V.tensor_copy(c.crb[:], c.ps_a[:, 16:32]), "vec")
    c.ms["pp_free0"] = crb

    for blk in range(nblk):
        t0 = blk * 128
        pg.wait(TE, rt, resid_ms, c.ms.get("ps_o_free"), c.ms.get("rows_free"),
                c.ms.get("rows_read"), c.ms.get("rcol_free"))
        for kc in range(NCH):
            g_mm = nc.tensor.matmul(c.ps_o[:, 0:16],
                                    c.resid[:, kc * T + t0: kc * T + t0 + 128],
                                    c.rtr32[:].rearrange("p (kc e) -> p kc e", e=16)[:, kc, :],
                                    start=(kc == 0), stop=(kc == NCH - 1))
        nc.tensor.matmul(c.ps_b[:, 0:1], c.rrow[0:1, t0:t0 + 128],
                         c.constb[0:1, 128:129], start=True, stop=True)
        rmm = pg.inc(nc.tensor.matmul(c.ps_b[:, 1:2], c.mrow[0:1, t0:t0 + 128],
                                      c.constb[0:1, 128:129], start=True, stop=True), "tns")
        pg.wait(V, rmm, c.ms.get("msc_free"))
        cp = pg.inc(V.tensor_copy(c.col1[:], c.ps_b[:, 0:1]), "vec")
        cp2 = pg.inc(V.tensor_copy(c.col2[:], c.ps_b[:, 1:2]), "vec")
        c.ms["rcol_free"] = cp2
        pg.wait(V, cp, cp2, crb)
        pg.inc(V.tensor_scalar(c.msc[:, 0:16], c.ps_o[:, 0:16], c.col1[:], None, ALU.mult), "vec")
        lg = pg.inc(V.tensor_scalar(c.msc[:, 16:32], c.crb[:], c.col2[:], None, ALU.mult), "vec")
        c.ms["ps_o_free"] = lg
        pg.wait(V, lg)
        lg = pg.inc(V.tensor_tensor(c.msc[:, 0:16], c.msc[:, 0:16], c.msc[:, 16:32],
                                    ALU.subtract), "vec")
        pg.wait(V, lg)
        pg.inc(V.tensor_reduce(c.col1[:], c.msc[:, 0:16], AX.X, ALU.max), "vec")
        pg.wait(V, pg.at("vec"))
        pg.inc(V.tensor_scalar(c.msc[:, 16:32], c.msc[:, 0:16], c.col1[:], None, ALU.is_equal), "vec")
        pg.wait(V, pg.at("vec"))
        pg.inc(V.tensor_scalar(c.msc[:, 32:48], c.msc[:, 16:32], NEG, None, ALU.mult), "vec")
        pg.wait(V, pg.at("vec"))
        pg.inc(V.tensor_tensor(c.msc[:, 32:48], c.msc[:, 0:16], c.msc[:, 32:48], ALU.add), "vec")
        pg.wait(V, pg.at("vec"))
        pg.inc(V.tensor_reduce(c.col2[:], c.msc[:, 32:48], AX.X, ALU.max), "vec")
        pg.wait(V, pg.at("vec"))
        d12 = pg.inc(V.tensor_tensor(c.col3[:], c.col1[:], c.col2[:], ALU.subtract), "vec")
        pg.wait(S, d12)
        sg = pg.inc(S.activation(c.col3[:], c.col3[:], AF.Sigmoid), "act")
        pg.wait(V, sg)
        pg.inc(V.tensor_scalar(c.col4[:], c.col3[:], -1.0, 1.0, ALU.mult, ALU.add), "vec")
        pg.inc(V.tensor_scalar(c.msc[:, 48:64], c.msc[:, 32:48], c.col2[:], None, ALU.is_equal), "vec")
        pg.wait(V, pg.at("vec"))
        pg.inc(V.tensor_scalar(c.msc[:, 16:32], c.msc[:, 16:32], c.col3[:], None, ALU.mult), "vec")
        pg.inc(V.tensor_scalar(c.msc[:, 48:64], c.msc[:, 48:64], c.col4[:], None, ALU.mult), "vec")
        pg.wait(V, pg.at("vec"))
        pg.inc(V.tensor_tensor(c.msc[:, 16:32], c.msc[:, 16:32], c.msc[:, 48:64], ALU.add), "vec")
        pg.wait(V, pg.at("vec"))
        sb_ms = pg.inc(V.tensor_copy(c.scbf[:], c.msc[:, 16:32]), "vec")
        c.ms["msc_free"] = sb_ms
        pg.wait(TE, sb_ms, c.ms.get("ps_t_free"))
        tr = pg.inc(nc.tensor.transpose(c.ps_t[0:16, :], c.scbf[:], c.identb), "tns")
        pg.wait(V, tr)
        stc = pg.inc(V.tensor_copy(c.scoresT[:, blk * 128:(blk + 1) * 128], c.ps_t[0:16, :]), "vec")
        c.ms["ps_t_free"] = stc
    route_ms = pg.at("vec")

    def load_gu(e, slot):
        src = c.wdram["moe_gu"][e, :, :, :].rearrange("p kc m -> p (kc m)")
        pg.wait(nc.gpsimd, c.ms.get(("wgu_user", slot)))
        return pg.inc(nc.gpsimd.dma_start(out=c.wgu[slot][:], in_=src), "dgu%d" % slot, 16)

    def load_dn(e, slot):
        src = c.wdram["moe_dn"][e, :, :, :].rearrange("p kc m -> p (kc m)")
        pg.wait(nc.gpsimd, c.ms.get(("wdn_user", slot)))
        return pg.inc(nc.gpsimd.dma_start(out=c.wdn[slot][:], in_=src), "ddn%d" % slot, 16)

    gu_ms = {0: load_gu(0, 0)}
    dn_ms = {0: load_dn(0, 0)}
    seqlist = [(e, t) for e in range(E) for t in range(ntiles)]
    prevdn = None

    def emit_down(info):
        idxp, ep, tp, dnm, slotp = info
        parp = idxp % 2
        t0p = tp * NT
        dreg = c.ps_s[:, 512:512 + NT] if parp == 0 else c.ps_s[:, 1536:1536 + NT]
        wdnv = c.wdn[slotp][:].rearrange("p (kc m) -> p kc m", m=1024)
        act_ms = c.ms[("actdone", parp)]
        for m in range(NCH):
            pg.wait(TE, act_ms, dnm, db, c.ms.get(("psd", parp)))
            first = True
            if ep == 0:
                nc.tensor.matmul(dreg, c.dnb[:, m * 128:(m + 1) * 128],
                                 c.scoresT[:, t0p:t0p + NT], start=True, stop=False)
                first = False
            for kc in range(4):
                mm = nc.tensor.matmul(dreg, wdnv[:, kc, m * 128:(m + 1) * 128],
                                      c.actbuf[:, parp * 4 * NT + kc * NT: parp * 4 * NT + (kc + 1) * NT],
                                      start=first and (kc == 0), stop=(kc == 3))
            dmm = pg.inc(mm, "tns")
            c.ms[("wdn_user", slotp)] = dmm
            pg.wait(V, dmm, resid_ms)
            ra = pg.inc(V.tensor_tensor(c.resid[:, m * T + t0p: m * T + t0p + NT],
                                        c.resid[:, m * T + t0p: m * T + t0p + NT],
                                        dreg, ALU.add), "vec")
            c.ms[("psd", parp)] = ra
        c.ms[("actu", parp)] = pg.at("tns")

    for idx, (e, t) in enumerate(seqlist):
        slot = e % 2
        par = idx % 2

        wguv = c.wgu[slot][:].rearrange("p (kc m) -> p kc m", m=1024)
        t0 = t * NT
        breg = c.ps_s[:, 0:NT] if par == 0 else c.ps_s[:, 1024:1024 + NT]
        pg.wait(TE, route_ms, c.ms.get("indm_ms"), c.ms.get(("bcr", par)),
                c.ms.get("scores_free"))
        bcm = pg.inc(nc.tensor.matmul(breg, c.indm[:, e * 128:(e + 1) * 128],
                                      c.scoresT[:, t0:t0 + NT],
                                      start=True, stop=True), "tns")
        for pr in range(4):
            pg.wait(TE, gu_ms[e], ln_ms, c.ms.get("pp_free0"), c.ms.get("pp_free1"),
                    c.ms.get("rows_read"))
            for kc in range(NCH):
                nc.tensor.matmul(c.ps_a[:, 0:NT], wguv[:, kc, pr * 128:(pr + 1) * 128],
                                 c.xnfull[:, kc * T + t0: kc * T + t0 + NT],
                                 start=(kc == 0), stop=(kc == NCH - 1))
            for kc in range(NCH):
                mm = nc.tensor.matmul(c.ps_b[:, 0:NT],
                                      wguv[:, kc, (pr + 4) * 128:(pr + 5) * 128],
                                      c.xnfull[:, kc * T + t0: kc * T + t0 + NT],
                                      start=(kc == 0), stop=(kc == NCH - 1))
            mm_ms = pg.inc(mm, "tns")
            c.ms[("wgu_user", slot)] = mm_ms
            if pr == 3:
                if prevdn is not None:
                    emit_down(prevdn)
                    prevdn = None
                if t == 0 and e + 1 < E:
                    gu_ms[e + 1] = load_gu(e + 1, (e + 1) % 2)
                    dn_ms[e + 1] = load_dn(e + 1, (e + 1) % 2)
            pg.wait(S, mm_ms, gb, c.ms.get("msc01_free"))
            pg.inc(S.activation(c.msc[:, 0:NT], c.ps_a[:, 0:NT], AF.Identity,
                                bias=c.gub[:, e * 8 + pr: e * 8 + pr + 1]), "act")
            g_ms = pg.at("act")
            pg.inc(S.activation(c.msc[:, NT:2 * NT], c.ps_b[:, 0:NT], AF.Identity,
                                bias=c.gub[:, e * 8 + pr + 4: e * 8 + pr + 5]), "act")
            u_ms = pg.at("act")
            c.ms["pp_free0"] = g_ms
            c.ms["pp_free1"] = u_ms
            pg.wait(V, g_ms)
            gm = pg.inc(V.tensor_scalar(c.msc[:, 0:NT], c.msc[:, 0:NT], 7.0, None, ALU.min), "vec")
            pg.wait(V, u_ms)
            um = pg.inc(V.tensor_scalar(c.msc[:, NT:2 * NT], c.msc[:, NT:2 * NT],
                                        7.0, -7.0, ALU.min, ALU.max), "vec")
            pg.wait(S, gm)
            sgl = pg.inc(S.activation(c.msc[:, 2 * NT:3 * NT], c.msc[:, 0:NT],
                                      AF.Sigmoid, scale=1.702), "act")
            pg.wait(V, sgl, um)
            pg.inc(V.tensor_tensor(c.msc[:, 2 * NT:3 * NT], c.msc[:, 0:NT],
                                   c.msc[:, 2 * NT:3 * NT], ALU.mult), "vec")
            pg.inc(V.tensor_scalar(c.msc[:, NT:2 * NT], c.msc[:, NT:2 * NT],
                                   1.0, None, ALU.add), "vec")
            pg.wait(V, pg.at("vec"))
            a0 = pg.inc(V.tensor_tensor(c.msc[:, NT:2 * NT], c.msc[:, NT:2 * NT],
                                        c.msc[:, 2 * NT:3 * NT], ALU.mult), "vec")
            pg.wait(V, a0, bcm, c.ms.get(("actu", par)))
            av = pg.inc(V.tensor_tensor(
                c.actbuf[:, par * 4 * NT + pr * NT: par * 4 * NT + (pr + 1) * NT],
                c.msc[:, NT:2 * NT], breg, ALU.mult), "vec")
            c.ms["msc01_free"] = av
            c.ms[("bcr", par)] = av
        c.ms[("actdone", par)] = pg.at("vec")
        prevdn = (idx, e, t, dn_ms[e], slot)
    emit_down(prevdn)
    pg.barrier()
    es.close()
    return pg.at("vec")


def emit_dnet(pg, c, T, resid_ms):
    nc = pg.nc
    V, S, TE = nc.vector, nc.scalar, nc.tensor
    LP = 256
    npass = T // LP
    nstr = LP // 128
    H, K = HEADS, HD
    es = contextlib.ExitStack()
    _u = "_%d" % pg.uid()
    sb = lambda name, shape, dt: es.enter_context(nc.sbuf_tensor(name + _u, shape, dt))
    LPB = LP
    c.wbas = sb("wbas", [128, 512], BF16)
    c.dtb = sb("dtb", [16, 1], F32)
    c.alog = sb("alog", [16, 1], F32)
    c.nega = sb("nega", [16, 1], F32)
    c.nwcol = sb("nwcol", [128, 1], F32)
    c.cf2 = sb("cf2", [128, 192], F32)
    c.identf = sb("identf", [128, 128], F32)
    c.one16 = sb("one16", [16, 1], F32)
    c.eps6 = sb("eps6", [128, 1], F32)
    c.eps5 = sb("eps5", [128, 1], F32)
    c.q_tok = sb("q_tok", [128, 2 * 1024], BF16)
    c.k_tok = sb("k_tok", [128, 2 * 1024], BF16)
    c.v_tok = sb("v_tok", [128, 2 * 1024], BF16)
    c.g_tok = sb("g_tok", [128, 2 * 1024], BF16)
    c.vtmp2 = sb("vtmp2", [128, LPB], BF16)
    c.dbrows = sb("dbrows", [64, LPB], F32)
    c.dsc = sb("dsc", [16, 2 * LPB], F32)
    c.dbcols = sb("dbcols", [128, 128], F32)
    c.gcols = sb("gcols", [128, 32], F32)
    c.egcols = sb("egcols", [128, 32], F32)
    c.grows = sb("grows", [16, LPB], F32)
    c.colsb = sb("colsb", [128, 96], BF16)
    c.gflat = sb("gflat", [1, 2048], F32)
    c.ecols = sb("ecols", [64, 32], F32)
    c.ecolsb = sb("ecolsb", [128, 32], BF16)
    c.l2c = sb("l2c", [128, 32], F32)
    c.l2cb = sb("l2cb", [128, 16], BF16)
    c.mscb = sb("mscb", [128, 1024], BF16)
    c.cscr = sb("cscr", [64, 4096], BF16)
    c.kT = sb("kT", [64, 1024], BF16)
    c.qT = sb("qT", [64, 1024], BF16)
    c.qeT = sb("qeT", [64, 1024], BF16)
    c.kbT = sb("kbT", [64, 1024], BF16)
    c.ltbuf = sb("ltbuf", [64, 1024], F32)
    c.ltbuf2 = sb("ltbuf2", [64, 1024], F32)
    c.pt_all = sb("pt_all", [64, 1024], BF16)
    c.pkT = sb("pkT", [64, 1024], BF16)
    c.solvp = sb("solvp", [64, 1024], BF16)
    c.ybuf = sb("ybuf", [64, 2048], BF16)
    c.kcdT = sb("kcdT", [64, 1024], BF16)
    c.vnew = sb("vnew", [64, 1024], BF16)
    c.at_all = sb("at_all", [64, 1024], BF16)
    c.s_all = sb("s_all", [64, 1024], F32)
    c.s_b = sb("s_b", [64, 1024], BF16)
    c.och = sb("och", [64, 1024], BF16)
    c.otb = sb("otb", [128, NCH * LPB], BF16)
    c.bslot = [sb("bslot0", [128, 4096], BF16), sb("bslot1", [128, 4096], BF16)]
    c.msc = sb("msc", [128, 3 * 512], F32)
    xnq = sb("xnq_d", [128, NCH * LP], BF16)
    ldf = pg.inc(nc.sync.dma_start(out=c.identf[:], in_=c.cfid[:, :]), "dc7", 16)
    c.ms["identf_ms"] = ldf

    def t64(src_ap, dst_ap, idx, extra=None, sbase=0):
        s = idx % 3
        stg = (c.ps_t[0:64, 0:64],
               c.ps_s[0:64, 1024:1056].bitcast(BF16),
               c.ps_s[0:64, 1536:1568].bitcast(BF16))[s]
        pg.wait(TE, c.ms.get(("t64", s)), extra,
                c.ms.get("ps_t_free") if s == 0 else None)
        tr = pg.inc(nc.tensor.transpose(
            stg, src_ap, c.identb[sbase:sbase + 64, sbase:sbase + 64]), "tns")
        eng = V if idx % 2 == 0 else S
        pg.wait(eng, tr)
        cp = pg.inc((eng.tensor_copy if eng is V else eng.copy)(dst_ap, stg),
                    "vec" if eng is V else "act")
        c.ms[("t64", s)] = cp
        if s == 0:
            c.ms["ps_t_free"] = cp
        return cp

    wba = pg.inc(nc.gpsimd.dma_start(
        out=c.wbas[:], in_=c.wdram["l1b_ba"][0].rearrange("p kc m -> p (kc m)")), "dba", 16)
    dtb = pg.inc(nc.sync.dma_start(out=c.dtb[:], in_=c.wdram["l1b_dtb"][:, :]), "ddt", 16)
    alg = pg.inc(nc.sync.dma_start(out=c.alog[:], in_=c.wdram["l1b_alog"][:, :]), "dal", 16)
    nwl = pg.inc(nc.sync.dma_start(out=c.nwcol[:], in_=c.wdram["l1b_nw"][:, :]), "dnw", 16)
    c2 = pg.inc(nc.sync.dma_start(out=c.cf2[:], in_=c.cf2d[:, :]), "dc6", 16)
    pg.wait(S, alg)
    ea = pg.inc(S.activation(c.nega[:], c.alog[:], AF.Exp), "act")
    pg.wait(V, ea)
    nega = pg.inc(V.tensor_scalar(c.nega[:], c.nega[:], -1.0, None, ALU.mult), "vec")
    pg.wait(V, c.ms.get("onecol16_init"))
    one16 = pg.inc(V.memset(c.one16[:], 1.0), "vec")
    c.ms["onecol16_init"] = one16
    eps6 = pg.inc(V.memset(c.eps6[:], 1e-6), "vec")
    eps5 = pg.inc(V.memset(c.eps5[:], 1e-5), "vec")
    pg.wait(V, c.ms.get("s_user"))
    pg.inc(V.memset(c.s_all[:], 0.0), "vec")
    sb0 = pg.inc(V.memset(c.s_b[:], 0.0), "vec")
    s_ready = sb0

    for q in range(npass):
        t0 = q * LP
        ln_ms = emit_ln(pg, c, T, t0, LP, resid_ms, xnq, LP, dst_t0=0)
        wd = c.wdram["l1b_w"]
        wr = {0: load_w_group(pg, c, wd, 0, NCH, 512, 0, buf=c.bslot, key="bs", sem="dq")}
        for g in range(8):
            sl = g % 2
            if g + 1 < 8:
                wr[g + 1] = load_w_group(pg, c, wd, g + 1, NCH, 512, (g + 1) % 2,
                                         buf=c.bslot, key="bs", sem="dq")
            wv = c.bslot[sl][:, 0:4096].rearrange("p (kc m) -> p kc m", m=512)
            for mi in range(4):
                m = g * 4 + mi
                par = m % 2
                psum = c.ps_a if par == 0 else c.ps_b
                pg.wait(TE, wr[g], ln_ms, c.ms.get("pp_free%d" % par))
                for kc in range(NCH):
                    mm = nc.tensor.matmul(psum[:, 0:LP], wv[:, kc, mi * 128:(mi + 1) * 128],
                                          xnq[:, kc * LP:(kc + 1) * LP],
                                          start=(kc == 0), stop=(kc == NCH - 1))
                mm_ms = pg.inc(mm, "tns")
                c.ms[("bs_user", sl)] = mm_ms
                tensor_i, ch = divmod(m, 8)
                dstbuf = (c.q_tok, c.k_tok, c.v_tok, c.g_tok)[tensor_i]
                pg.wait(S, mm_ms, c.ms.get("vt_free"))
                ev = pg.inc(S.activation(c.vtmp2[:, 0:LP], psum[:, 0:LP], AF.Identity), "act")
                c.ms["pp_free%d" % par] = ev
                for st in range(nstr):
                    pg.wait(TE, ev, c.ms.get("ps_t_free"))
                    tr = pg.inc(nc.tensor.transpose(
                        c.ps_t[:], c.vtmp2[:, st * 128:(st + 1) * 128], c.identb), "tns")
                    pg.wait(V, tr)
                    cp = pg.inc(V.tensor_copy(
                        dstbuf[:, st * 1024 + ch * 128: st * 1024 + (ch + 1) * 128],
                        c.ps_t[:]), "vec")
                    c.ms["ps_t_free"] = cp
                c.ms["vt_free"] = pg.at("tns")
        pg.wait(TE, wba, ln_ms, c.ms.get("dbps_free"))
        for kc in range(NCH):
            mm = nc.tensor.matmul(c.ps_s[0:64, 0:LP],
                                  c.wbas[:, 0:512].rearrange("p (kc m) -> p kc m", m=64)[:, kc, :],
                                  xnq[:, kc * LP:(kc + 1) * LP],
                                  start=(kc == 0), stop=(kc == NCH - 1))
        ba_ms = pg.inc(mm, "tns")
        pg.wait(S, ba_ms, dtb, c.ms.get("dbrows_user"))
        bet = pg.inc(S.activation(c.dbrows[32:48, 0:LP], c.ps_s[0:16, 0:LP], AF.Sigmoid), "act")
        aid = pg.inc(S.activation(c.dsc[0:16, 0:LP], c.ps_s[32:48, 0:LP], AF.Identity,
                                  bias=c.dtb[:, 0:1]), "act")
        c.ms["dbps_free"] = aid
        c.ms[("scanps", 0)] = [c.ms.get(("scanps", 0)), bet, aid]
        pg.wait(S, aid)
        ab = pg.inc(S.activation(c.dsc[0:16, LP:2 * LP], c.dsc[0:16, 0:LP], AF.Abs), "act")
        pg.wait(S, ab)
        ex = pg.inc(S.activation(c.dsc[0:16, LP:2 * LP], c.dsc[0:16, LP:2 * LP], AF.Exp,
                                 scale=-1.0), "act")
        pg.wait(S, ex, one16)
        ln1 = pg.inc(S.activation(c.dsc[0:16, LP:2 * LP], c.dsc[0:16, LP:2 * LP], AF.Ln,
                                  bias=c.one16[:, 0:1]), "act")
        rl = pg.inc(S.activation(c.dsc[0:16, 0:LP], c.dsc[0:16, 0:LP], AF.Relu), "act")
        pg.wait(V, ln1, rl, nega)
        sp = pg.inc(V.tensor_tensor(c.dsc[0:16, 0:LP], c.dsc[0:16, 0:LP],
                                    c.dsc[0:16, LP:2 * LP], ALU.add), "vec")
        pg.wait(V, sp)
        dec = pg.inc(V.tensor_scalar(c.dbrows[0:16, 0:LP], c.dsc[0:16, 0:LP],
                                     c.nega[:, 0:1], None, ALU.mult), "vec")
        pg.wait(V, bet, dec)
        bd_ms = pg.at("vec")
        for st in range(nstr):
            pg.wait(TE, bd_ms, c2, c.ms.get("ps_t_free"), c.ms.get("dbcols_user"),
                    c.ms.get("identf_ms"))
            tr = pg.inc(nc.tensor.transpose(c.ps_o[:, 0:64], c.dbrows[:, st * 128:(st + 1) * 128],
                                            c.identf[0:64, 0:64]), "tns")
            pg.wait(V, tr)
            dbc = pg.inc(V.tensor_copy(c.dbcols[:, st * 64:(st + 1) * 64], c.ps_o[:, 0:64]), "vec")
            c.ms["ps_t_free"] = dbc
            for half in range(2):
                ch = st * 2 + half
                pg.wait(TE, dbc, c.ms.get("gc_ps_free"), c.ms.get("pp_free1"))
                gmm = pg.inc(nc.tensor.matmul(
                    c.ps_b[0:64, 0:16], c.cf2[half * 64:(half + 1) * 64 - 64 + 64, 0:64]
                    if False else c.cf2[half * 64: half * 64 + 64, 0:64],
                    c.dbcols[half * 64:(half + 1) * 64, st * 64: st * 64 + 16],
                    start=True, stop=True), "tns")
                pg.wait(V, gmm)
                gcp = pg.inc(V.tensor_copy(
                    c.gcols[half * 64:(half + 1) * 64, st * 16:(st + 1) * 16],
                    c.ps_b[0:64, 0:16]), "vec")
                c.ms["gc_ps_free"] = gcp
        gc_ms = pg.at("vec")
        c.ms["gl_ps_free"] = [c.ms.get("gl_ps_free"), gc_ms]
        pg.wait(S, gc_ms)
        ega = pg.inc(S.activation(c.egcols[:, 0:nstr * 16], c.gcols[:, 0:nstr * 16], AF.Exp), "act")
        for st in range(nstr):
            pg.wait(TE, gc_ms, c.ms.get("gr_ps_free"))
            tr = pg.inc(nc.tensor.transpose(c.ps_o[0:16, 0:128],
                                            c.gcols[:, st * 16:(st + 1) * 16], c.identf[:, :]), "tns")
            pg.wait(V, tr)
            grc = pg.inc(V.tensor_copy(c.grows[:, st * 128:(st + 1) * 128], c.ps_o[0:16, 0:128]), "vec")
            c.ms["gr_ps_free"] = grc
        gr_ms = pg.at("vec")
        pg.wait(nc.sync, gr_ms, c.ms.get("gscr_read"))
        gw = pg.inc(nc.sync.dma_start(out=c.g_scr[:, :], in_=c.grows[0:16, 0:LP]), "dgw", 16)

        pg.wait(V, ega, gc_ms)
        bcast_cols = pg.inc(V.tensor_copy(
            c.colsb[:, 0:nstr * 16].rearrange("p (s x) -> p s x", x=16),
            c.dbcols[:, 0:nstr * 64].rearrange("p (s x) -> p s x", x=64)[:, :, 32:48]), "vec")
        egb = pg.inc(V.tensor_copy(c.colsb[:, 64:64 + nstr * 16], c.egcols[:, 0:nstr * 16]), "vec")
        pg.wait(V, bcast_cols, egb)
        beg = pg.inc(V.tensor_tensor(c.colsb[:, 32:32 + nstr * 16],
                                     c.colsb[:, 0:nstr * 16],
                                     c.colsb[:, 64:64 + nstr * 16], ALU.mult), "vec")
        pg.wait(V, beg)
        for st in range(nstr):
            for which, buf in (("q", c.q_tok), ("k", c.k_tok)):
                sl = buf[:, st * 1024:(st + 1) * 1024]
                pg.wait(S, pg.at("vec"), c.ms.get("msc_l2_free"))
                sq = pg.inc(S.activation(c.mscb[:, 0:1024], sl, AF.Square), "act")
                pg.wait(V, sq)
                rd = pg.inc(V.tensor_reduce(c.l2c[:, 0:16], seg3(c.mscb[:, 0:1024]),
                                            AX.X, ALU.add), "vec")
                c.ms["msc_l2_free"] = rd
                pg.wait(S, rd, eps6)
                sr = pg.inc(S.activation(c.l2c[:, 16:32], c.l2c[:, 0:16], AF.Sqrt,
                                         bias=c.eps6[:, 0:1]), "act")
                pg.wait(V, sr)
                rc = pg.inc(V.reciprocal(c.l2c[:, 0:16], c.l2c[:, 16:32]), "vec")
                pg.wait(V, rc)
                if which == "q":
                    rc = pg.inc(V.tensor_scalar(c.l2c[:, 0:16], c.l2c[:, 0:16], 0.125,
                                                None, ALU.mult), "vec")
                    pg.wait(V, rc)
                cb_ = pg.inc(V.tensor_copy(c.l2cb[:, 0:16], c.l2c[:, 0:16]), "vec")
                pg.wait(V, cb_)
                nm = pg.inc(V.tensor_tensor(seg3(sl), seg3(sl), segb(c.l2cb[:, 0:16], 16),
                                            ALU.mult), "vec")
            vsl = c.v_tok[:, st * 1024:(st + 1) * 1024]
            pg.wait(V, pg.at("vec"))
            pg.inc(V.tensor_tensor(seg3(vsl), seg3(vsl),
                                   segb(c.colsb[:, st * 16:(st + 1) * 16], 16), ALU.mult), "vec")
        phn_ms = pg.at("vec")

        def issue_gfr(j):
            sl = j % 2
            pg.wait(nc.sync, gw, c.ms.get(("gflatu", sl)))
            ms = pg.inc(nc.sync.dma_start(
                out=c.gflat[0:1, sl * 1024:(sl + 1) * 1024].rearrange("p (h t) -> p h t", t=64),
                in_=c.g_scr.rearrange("h (cc t) -> cc h t", t=64)[j:j + 1]), "dgr%d" % sl, 16)
            c.ms[("gfr", j)] = ms
            return ms

        issue_gfr(0)
        for cc in range(LP // 64):
            st = cc // 2
            half = cc % 2
            pb = half * 64
            gsl = cc % 2
            gview = c.gflat[0:1, gsl * 1024:(gsl + 1) * 1024]
            ctok = lambda buf: buf[:, st * 1024:(st + 1) * 1024]
            if cc + 1 < LP // 64:
                issue_gfr(cc + 1)
            gfr = c.ms.pop(("gfr", cc))
            c.ms["gscr_read"] = [c.ms.get("gscr_read") if cc == 0 else None, gfr]
            pg.wait(TE, gfr, c.ms.get("lt_ps_free"), c.ms.get(("t64", 1)),
                    c.ms.get(("t64", 2)), c.ms.get(("pktst", 0)), c.ms.get(("pktst", 1)))
            nc.tensor.matmul(c.ps_s[0:64, 1024:1536], c.onesf[0:1, 0:64],
                             gview[:, 0:512], start=True, stop=True)
            lmm = pg.inc(nc.tensor.matmul(c.ps_s[0:64, 1536:2048], c.onesf[0:1, 0:64],
                                          gview[:, 512:1024], start=True, stop=True), "tns")
            gseg = segb(c.gcols[pb:pb + 64, st * 16:(st + 1) * 16], 16)
            stseg = c.cf2[0:64, 64:128].unsqueeze(1).broadcast_to((64, 16, 64))
            pg.wait(V, lmm, c.ms.get("msc_lt_free"))
            d1 = pg.inc(V.tensor_tensor(seg3(c.msc[0:64, 0:1024]),
                                        seg3(c.ps_s[0:64, 1024:2048]),
                                        gseg, ALU.subtract), "vec")
            pg.wait(V, d1)
            d2 = pg.inc(V.tensor_tensor(seg3(c.msc[0:64, 0:1024]), seg3(c.msc[0:64, 0:1024]),
                                        stseg, ALU.mult), "vec")
            pg.wait(S, d2)
            d3 = pg.inc(S.activation(c.msc[0:64, 0:1024], c.msc[0:64, 0:1024], AF.Exp), "act")
            pg.wait(V, d3)
            lt_ms = pg.inc(V.tensor_tensor(seg3(c.ltbuf[0:64, 0:1024]), seg3(c.msc[0:64, 0:1024]),
                                           stseg, ALU.mult), "vec")
            c.ms["msc_lt_free"] = lt_ms
            eyeseg = c.cf2[0:64, 128:192].unsqueeze(1).broadcast_to((64, 16, 64))
            pg.wait(V, lt_ms, c.ms.get("lt2_user"))
            lt2_ms = pg.inc(V.tensor_tensor(seg3(c.ltbuf2[0:64, 0:1024]),
                                            seg3(c.ltbuf[0:64, 0:1024]),
                                            eyeseg, ALU.add), "vec")
            pg.wait(TE, gfr, c.ms.get("gl_ps_free"), c.ms.get("pp_free1"))
            glmm = pg.inc(nc.tensor.matmul(
                c.ps_b[0:64, 16:32], c.onesf[0:1, 0:64],
                gview.rearrange("p (h t) -> p h t", t=64)[:, :, 63:64].squeeze(2),
                start=True, stop=True), "tns")
            pg.wait(V, glmm, c.ms.get("ecols_user"))
            ei = pg.inc(V.tensor_tensor(c.ecols[0:64, 0:16],
                                        c.ps_b[0:64, 16:32],
                                        c.gcols[pb:pb + 64, st * 16:(st + 1) * 16],
                                        ALU.subtract), "vec")
            pg.wait(S, ei)
            ei2 = pg.inc(S.activation(c.ecols[0:64, 0:16], c.ecols[0:64, 0:16], AF.Exp), "act")
            egl = pg.inc(S.activation(c.ecols[0:64, 16:32], c.ps_b[0:64, 16:32], AF.Exp), "act")
            c.ms["gl_ps_free"] = egl
            c.ms["sv_ps_free1"] = [c.ms.get("sv_ps_free1"), egl]
            c.ms["sv_ps_free0"] = [c.ms.get("sv_ps_free0"), c.ms.get("pp_free0")]
            c.ms["pt_ps_free0"] = [c.ms.get("pt_ps_free0"), c.ms.get("pp_free0")]
            c.ms["pt_ps_free1"] = [c.ms.get("pt_ps_free1"), c.ms.get("pp_free1"), egl]
            c.ms["at_ps_free0"] = [c.ms.get("at_ps_free0"), c.ms.get("pp_free0")]
            c.ms["at_ps_free1"] = [c.ms.get("at_ps_free1"), c.ms.get("pp_free1"), egl]
            bseg = segb(c.colsb[pb:pb + 64, st * 16:(st + 1) * 16], 16)
            egseg = segb(c.colsb[pb:pb + 64, 64 + st * 16: 64 + (st + 1) * 16], 16)
            begseg = segb(c.colsb[pb:pb + 64, 32 + st * 16: 32 + (st + 1) * 16], 16)
            pg.wait(V, phn_ms, ei2, egl, c.ms.get("cscr_user"))
            eib = pg.inc(V.tensor_copy(c.ecolsb[pb:pb + 64, 0:16], c.ecols[0:64, 0:16]), "vec")
            eglb = pg.inc(V.tensor_copy(c.ecolsb[0:64, 16:32], c.ecols[0:64, 16:32]), "vec")
            kb = pg.inc(V.tensor_tensor(seg3(c.cscr[0:64, 0:1024]),
                                        seg3(ctok(c.k_tok)[pb:pb + 64, :]),
                                        bseg, ALU.mult), "vec")
            kbe = pg.inc(V.tensor_tensor(seg3(c.cscr[0:64, 1024:2048]),
                                         seg3(ctok(c.k_tok)[pb:pb + 64, :]),
                                         begseg, ALU.mult), "vec")
            eiseg = segb(c.ecolsb[pb:pb + 64, 0:16], 16)
            pg.wait(V, eib)
            ke1 = pg.inc(V.tensor_tensor(seg3(c.cscr[0:64, 2048:3072]),
                                         seg3(ctok(c.k_tok)[pb:pb + 64, :]),
                                         eiseg, ALU.mult), "vec")
            qe = pg.inc(V.tensor_tensor(seg3(c.cscr[0:64, 3072:4096]),
                                        seg3(ctok(c.q_tok)[pb:pb + 64, :]),
                                        egseg, ALU.mult), "vec")
            pg.wait(TE, kb, qe, phn_ms, lt_ms, c.ms.get("fm_user"))
            ii = 0
            for h in range(H):
                for srcb, dstc, sbase in (
                        (ctok(c.k_tok)[pb:pb + 64, h * 64:(h + 1) * 64], c.kT, pb),
                        (ctok(c.q_tok)[pb:pb + 64, h * 64:(h + 1) * 64], c.qT, pb),
                        (c.cscr[0:64, 3072 + h * 64: 3072 + (h + 1) * 64], c.qeT, 0),
                        (c.cscr[0:64, h * 64:(h + 1) * 64], c.kbT, 0)):
                    t64(srcb, dstc[0:64, h * 64:(h + 1) * 64], ii, sbase=sbase)
                    ii += 1
            fm_ms = [pg.at("vec"), pg.at("act")]
            for hg in range(4):
                pg.wait(TE, fm_ms, lt_ms, c.ms.get("pt_ps_free%d" % (hg % 2)))
                for hh in range(4):
                    h = hg * 4 + hh
                    mmp = nc.tensor.matmul(
                        (c.ps_a if hg % 2 == 0 else c.ps_b)[0:64, hh * 64:(hh + 1) * 64],
                        c.kT[0:64, h * 64:(h + 1) * 64],
                        c.kbT[0:64, h * 64:(h + 1) * 64], start=True, stop=True)
                mmp_ms = pg.inc(mmp, "tns")
                pg.wait(V, mmp_ms, c.ms.get("ptall_user"))
                ptc = pg.inc(V.tensor_tensor(
                    c.pt_all[0:64, hg * 256:(hg + 1) * 256],
                    (c.ps_a if hg % 2 == 0 else c.ps_b)[0:64, 0:256],
                    c.ltbuf[0:64, hg * 256:(hg + 1) * 256], ALU.mult), "vec")
                c.ms["pt_ps_free%d" % (hg % 2)] = ptc
            pt_ms = pg.at("vec")
            pg.wait(V, pt_ms, c.ms.get("solv_user"))
            yview = c.ybuf[0:64, :].rearrange("p (h two d) -> p h two d", two=2, d=64)
            yi1 = pg.inc(V.tensor_copy(yview[:, :, 0:1, :].squeeze(2),
                                       seg3(ctok(c.v_tok)[pb:pb + 64, :])), "vec")
            yi2 = pg.inc(V.tensor_copy(yview[:, :, 1:2, :].squeeze(2),
                                       seg3(c.cscr[0:64, 1024:2048])), "vec")
            for h in range(H):
                t64(c.pt_all[0:64, h * 64:(h + 1) * 64],
                    c.solvp[0:64, h * 64:(h + 1) * 64], h, extra=pt_ms)
            pg.wait(V, pg.at("act"))
            pg.inc(V.tensor_copy(c.pkT[0:64, :], c.pt_all[0:64, :]), "vec")
            sv_ms = pg.at("vec")
            svreg = [c.ps_a[0:64, 0:192], c.ps_b[0:64, 0:192],
                     c.ps_s[0:64, 0:192], c.ps_s[0:64, 512:704]]
            for it in range(6):
                lastiter = (it == 5)
                for h in range(H):
                    par = h % 4
                    psum = svreg[par]
                    pg.wait(TE, sv_ms, yi1, yi2, c.ms.get("sv_ps_free%d" % par),
                            c.ms.get(("scanps", par - 2)) if par >= 2 else None)
                    if not lastiter:
                        nc.tensor.matmul(psum[:, 0:64], c.pkT[0:64, h * 64:(h + 1) * 64],
                                         c.solvp[0:64, h * 64:(h + 1) * 64],
                                         start=True, stop=True)
                    smm = pg.inc(nc.tensor.matmul(
                        psum[:, 64:192], c.pkT[0:64, h * 64:(h + 1) * 64],
                        c.ybuf[0:64, h * 128:(h + 1) * 128], start=True, stop=True), "tns")
                    pg.wait(V, smm)
                    yup = pg.inc(V.tensor_tensor(c.ybuf[0:64, h * 128:(h + 1) * 128],
                                                 c.ybuf[0:64, h * 128:(h + 1) * 128],
                                                 psum[:, 64:192], ALU.add), "vec")
                    if not lastiter:
                        pk = pg.inc(V.tensor_copy(c.solvp[0:64, h * 64:(h + 1) * 64],
                                                  psum[:, 0:64]), "vec")
                        c.ms["sv_ps_free%d" % par] = pk
                        c.ms[("pkready", h)] = pk
                    else:
                        c.ms["sv_ps_free%d" % par] = yup
                if not lastiter:
                    pkstg = [c.ps_s[0:64, 1024:1056].bitcast(BF16),
                             c.ps_s[0:64, 1536:1568].bitcast(BF16)]
                    for h in range(H):
                        ss = h % 2
                        pg.wait(TE, c.ms.get(("pkready", h)), c.ms.get(("pktst", ss)), lt_ms)
                        tr = pg.inc(nc.tensor.transpose(pkstg[ss],
                                                        c.solvp[0:64, h * 64:(h + 1) * 64],
                                                        c.identb[0:64, 0:64]), "tns")
                        eng = V if h % 2 == 0 else S
                        pg.wait(eng, tr)
                        cp = pg.inc((eng.tensor_copy if eng is V else eng.copy)(
                            c.pkT[0:64, h * 64:(h + 1) * 64], pkstg[ss]),
                            "vec" if eng is V else "act")
                        c.ms[("pktst", ss)] = cp
                    sv_ms = [pg.at("vec"), pg.at("act")]
            y_ms = pg.at("vec")
            for h in range(H):
                t64(c.ybuf[0:64, h * 128 + 64: (h + 1) * 128],
                    c.kcdT[0:64, h * 64:(h + 1) * 64], h, extra=y_ms)
            kcd_ms = [pg.at("vec"), pg.at("act")]
            for hg in range(4):
                pg.wait(TE, kcd_ms, s_ready, c.ms.get(("scanps", hg)))
                for hh in range(4):
                    h = hg * 4 + hh
                    mmw = nc.tensor.matmul(
                        c.ps_s[0:64, hg * 512 + hh * 64: hg * 512 + (hh + 1) * 64],
                        c.kcdT[0:64, h * 64:(h + 1) * 64],
                        c.s_b[0:64, h * 64:(h + 1) * 64],
                        start=True, stop=True)
                w_ms = pg.inc(mmw, "tns")
                pg.wait(V, w_ms, y_ms, c.ms.get("vnew_user"))
                vn = pg.inc(V.tensor_tensor(
                    seg3(c.vnew[0:64, hg * 256:(hg + 1) * 256]),
                    c.ybuf[0:64, :].rearrange("p (h two d) -> p h two d", two=2, d=64)[
                        :, hg * 4:(hg + 1) * 4, 0:1, :].squeeze(2),
                    seg3(c.ps_s[0:64, hg * 512:hg * 512 + 256]), ALU.subtract), "vec")
            vn_ms = pg.at("vec")
            for hg in range(4):
                par = hg % 2
                pg.wait(TE, fm_ms, c.ms.get("at_ps_free%d" % par))
                for hh in range(4):
                    h = hg * 4 + hh
                    mma = nc.tensor.matmul((c.ps_a if par == 0 else c.ps_b)[0:64, hh * 64:(hh + 1) * 64],
                                           c.kT[0:64, h * 64:(h + 1) * 64],
                                           c.qT[0:64, h * 64:(h + 1) * 64],
                                           start=True, stop=True)
                at_ms = pg.inc(mma, "tns")
                pg.wait(V, at_ms, lt2_ms, c.ms.get("atall_user"))
                atc = pg.inc(V.tensor_tensor(
                    c.at_all[0:64, hg * 256:(hg + 1) * 256],
                    (c.ps_a if par == 0 else c.ps_b)[0:64, 0:256],
                    c.ltbuf2[0:64, hg * 256:(hg + 1) * 256], ALU.mult), "vec")
                c.ms["at_ps_free%d" % par] = atc
            at_all_ms = pg.at("vec")
            for hg in range(4):
                pg.wait(TE, at_all_ms, vn_ms, s_ready)
                for hh in range(4):
                    h = hg * 4 + hh
                    off = hg * 512 + hh * 64
                    nc.tensor.matmul(c.ps_s[0:64, off:off + 64],
                                     c.qeT[0:64, h * 64:(h + 1) * 64],
                                     c.s_b[0:64, h * 64:(h + 1) * 64],
                                     start=True, stop=False)
                    mo = nc.tensor.matmul(c.ps_s[0:64, off:off + 64],
                                          c.at_all[0:64, h * 64:(h + 1) * 64],
                                          c.vnew[0:64, h * 64:(h + 1) * 64],
                                          start=False, stop=True)
                mo_ms = pg.inc(mo, "tns")
                pg.wait(V, mo_ms, c.ms.get("och_user"))
                oc = pg.inc(V.tensor_copy(c.och[0:64, hg * 256:(hg + 1) * 256],
                                          c.ps_s[0:64, hg * 512:hg * 512 + 256]), "vec")
                c.ms[("scanps", hg)] = oc
            och_ms = pg.at("vec")
            pg.wait(TE, vn_ms, ke1, och_ms, c.ms.get("supd_ps_free"),
                    c.ms.get(("t64", 1)), c.ms.get(("t64", 2)),
                    c.ms.get(("pktst", 0)), c.ms.get(("pktst", 1)))
            for h in range(H):
                msu = nc.tensor.matmul(c.ps_s[0:64, 1024 + h * 64: 1024 + (h + 1) * 64],
                                       c.cscr[0:64, 2048 + h * 64: 2048 + (h + 1) * 64],
                                       c.vnew[0:64, h * 64:(h + 1) * 64],
                                       start=True, stop=True)
            su_ms = pg.inc(msu, "tns")
            eglseg = segb(c.ecolsb[0:64, 16:32], 16)
            pg.wait(V, su_ms, eglb, och_ms)
            s1 = pg.inc(V.tensor_tensor(seg3(c.s_all[0:64, :]), seg3(c.s_all[0:64, :]),
                                        eglseg, ALU.mult), "vec")
            pg.wait(V, s1)
            s2 = pg.inc(V.tensor_tensor(c.s_all[0:64, :], c.s_all[0:64, :],
                                        c.ps_s[0:64, 1024:2048], ALU.add), "vec")
            c.ms["supd_ps_free"] = s2
            c.ms["lt_ps_free"] = s2
            pg.wait(V, s2)
            s_ready = pg.inc(V.tensor_copy(c.s_b[0:64, :], c.s_all[0:64, :]), "vec")
            for _hg in range(4):
                c.ms[("scanps", _hg)] = [c.ms.get(("scanps", _hg)), s2]
            c.ms["lt2_user"] = pg.at("tns")
            c.ms["ecols_user"] = s_ready
            c.ms["pp_free0"] = s_ready
            c.ms["pp_free1"] = s_ready
            c.ms["bcast_free"] = s_ready
            c.ms["cscr_user"] = pg.at("tns")
            c.ms["vnew_user"] = pg.at("tns")
            c.ms["fm_user"] = pg.at("tns")
            c.ms["solv_user"] = pg.at("tns")
            c.ms["ptall_user"] = pg.at("tns")
            c.ms["atall_user"] = pg.at("tns")
            c.ms[("gflatu", gsl)] = pg.at("tns")
            pg.wait(S, och_ms, c.ms.get("msc_l2_free"))
            sqo = pg.inc(S.activation(c.mscb[0:64, 0:1024], c.och[0:64, :], AF.Square), "act")
            pg.wait(V, sqo)
            rdo = pg.inc(V.tensor_reduce(c.l2c[0:64, 0:16], seg3(c.mscb[0:64, 0:1024]),
                                         AX.X, ALU.add), "vec")
            pg.wait(V, rdo)
            sc1 = pg.inc(V.tensor_scalar(c.l2c[0:64, 0:16], c.l2c[0:64, 0:16], 1.0 / 64,
                                         None, ALU.mult), "vec")
            pg.wait(S, sc1, eps5)
            sr = pg.inc(S.activation(c.l2c[0:64, 16:32], c.l2c[0:64, 0:16], AF.Sqrt,
                                     bias=c.eps5[0:64, 0:1]), "act")
            pg.wait(V, sr)
            rco = pg.inc(V.reciprocal(c.l2c[0:64, 0:16], c.l2c[0:64, 16:32]), "vec")
            pg.wait(V, rco)
            rcb = pg.inc(V.tensor_copy(c.l2cb[0:64, 0:16], c.l2c[0:64, 0:16]), "vec")
            rseg = segb(c.l2cb[0:64, 0:16], 16)
            pg.wait(S, phn_ms, rdo)
            sgt = pg.inc(S.activation(c.cscr[0:64, 0:1024], ctok(c.g_tok)[pb:pb + 64, :],
                                      AF.Silu), "act")
            c.ms["msc_l2_free"] = sgt
            pg.wait(V, rcb, sgt)
            o1 = pg.inc(V.tensor_tensor(seg3(c.och[0:64, :]), seg3(c.och[0:64, :]),
                                        rseg, ALU.mult), "vec")
            pg.wait(V, o1)
            o2 = pg.inc(V.tensor_tensor(c.och[0:64, :], c.och[0:64, :],
                                        c.cscr[0:64, 0:1024], ALU.mult), "vec")
            c.ms["cscr_user"] = [c.ms["cscr_user"], o2]
            for h in range(H):
                t64(c.och[0:64, h * 64:(h + 1) * 64],
                    c.otb[(h % 2) * 64:(h % 2) * 64 + 64,
                          (h // 2) * LP + cc * 64: (h // 2) * LP + (cc + 1) * 64],
                    h, extra=[o2, c.ms.get("ot_user")])
            c.ms["och_user"] = pg.at("tns")
        ot_ms = [pg.at("vec"), pg.at("act")]
        wod = c.wdram["l1b_wo"]
        for g2 in range(2):
            wms = load_w_group(pg, c, wod, g2, NCH, 512, g2 % 2, buf=c.bslot, key="bs", sem="dq")
            pg.wait(V, wms, nwl)
            wsc = pg.inc(V.tensor_scalar(c.bslot[g2 % 2][:, 0:4096], c.bslot[g2 % 2][:, 0:4096],
                                         c.nwcol[:, 0:1], None, ALU.mult), "vec")
            wv = c.bslot[g2 % 2][:, 0:4096].rearrange("p (kc m) -> p kc m", m=512)
            for mi in range(4):
                m = g2 * 4 + mi
                par = m % 2
                psum = c.ps_a if par == 0 else c.ps_b
                pg.wait(TE, wsc, ot_ms, c.ms.get("pp_free%d" % par),
                        c.ms.get("sv_ps_free0"), c.ms.get("sv_ps_free1"),
                        c.ms.get("at_ps_free0"), c.ms.get("at_ps_free1"))
                for kc in range(NCH):
                    mm = nc.tensor.matmul(psum[:, 0:LP], wv[:, kc, mi * 128:(mi + 1) * 128],
                                          c.otb[:, kc * LP: (kc + 1) * LP],
                                          start=(kc == 0), stop=(kc == NCH - 1))
                mm_ms = pg.inc(mm, "tns")
                c.ms[("bs_user", g2 % 2)] = mm_ms
                pg.wait(V, mm_ms, resid_ms, c.ms.get("sc_free"))
                sv = pg.inc(V.tensor_copy(c.sc[:, 0:LP], psum[:, 0:LP]), "vec")
                c.ms["pp_free%d" % par] = sv
                pg.wait(V, sv)
                ra = pg.inc(V.tensor_tensor(c.resid[:, m * T + t0: m * T + t0 + LP],
                                            c.resid[:, m * T + t0: m * T + t0 + LP],
                                            c.sc[:, 0:LP], ALU.add), "vec")
                c.ms["sc_free"] = ra
        c.ms["ot_user"] = pg.at("tns")
    c.ms["s_user"] = pg.at("tns")
    pg.barrier()
    es.close()
    return pg.at("vec")


def emit_mlp(pg, c, T, resid_ms):
    nc = pg.nc
    V, S, TE = nc.vector, nc.scalar, nc.tensor
    NT = 512
    ntiles = T // NT
    es = contextlib.ExitStack()
    _u = "_%d" % pg.uid()
    sb = lambda name, shape, dt: es.enter_context(nc.sbuf_tensor(name + _u, shape, dt))
    c.mslot = [sb("mslot%d" % i, [128, 4096], BF16) for i in range(4)]
    c.hbuf1 = sb("hbuf1", [128, 32 * 512], BF16)
    c.hbuf2 = sb("hbuf2", [128, 32 * 512], BF16)
    c.l1fs = sb("l1fs", [128, 136], F32)
    c.msc = sb("msc", [128, 3 * 512], F32)
    xnq = sb("xnq", [128, NCH * NT], BF16)
    sload = pg.inc(nc.sync.dma_start(out=c.l1fs[:], in_=c.wdram["l1f_s"][:, :]), "dfs", 16)

    def scol(which, m):
        off = {"w1": 0, "w2": 32, "h1": 64, "h2": 96, "w3": 128}[which]
        return c.l1fs[:, off + m: off + m + 1]

    for q in range(ntiles):
        t0 = q * NT
        ln_ms = emit_ln(pg, c, T, t0, NT, resid_ms, xnq, NT, dst_t0=0)
        w1d, w2d = c.wdram["l1f_w1"], c.wdram["l1f_w2"]
        wr = {0: (load_w_group(pg, c, w1d, 0, 8, 512, 0, buf=c.mslot, key="ms", sem="dm"),
                  load_w_group(pg, c, w2d, 0, 8, 512, 1, buf=c.mslot, key="ms", sem="dm"))}
        for g in range(8):
            sl = (g % 2) * 2
            if g + 1 < 8:
                nsl = ((g + 1) % 2) * 2
                wr[g + 1] = (
                    load_w_group(pg, c, w1d, g + 1, 8, 512, nsl, buf=c.mslot, key="ms", sem="dm"),
                    load_w_group(pg, c, w2d, g + 1, 8, 512, nsl + 1, buf=c.mslot, key="ms", sem="dm"))
            v1 = c.mslot[sl][:, 0:4096].rearrange("p (kc m) -> p kc m", m=512)
            v2 = c.mslot[sl + 1][:, 0:4096].rearrange("p (kc m) -> p kc m", m=512)
            for mi in range(4):
                m = g * 4 + mi
                pg.wait(TE, wr[g][0], ln_ms, c.ms.get("pp_free0"), c.ms.get("rows_read"))
                for kc in range(NCH):
                    mm1 = nc.tensor.matmul(c.ps_a[:, 0:NT], v1[:, kc, mi * 128:(mi + 1) * 128],
                                           xnq[:, kc * NT: (kc + 1) * NT],
                                           start=(kc == 0), stop=(kc == NCH - 1))
                pg.wait(TE, wr[g][1], c.ms.get("pp_free1"))
                for kc in range(NCH):
                    mm2 = nc.tensor.matmul(c.ps_b[:, 0:NT], v2[:, kc, mi * 128:(mi + 1) * 128],
                                           xnq[:, kc * NT: (kc + 1) * NT],
                                           start=(kc == 0), stop=(kc == NCH - 1))
                mm_ms = pg.inc(mm2, "tns")
                c.ms[("ms_user", sl)] = mm_ms
                c.ms[("ms_user", sl + 1)] = mm_ms
                pg.wait(S, mm_ms, sload, c.ms.get("msc0_free"))
                a1 = pg.inc(S.activation(c.msc[:, 0:NT], c.ps_a[:, 0:NT], AF.Silu,
                                         scale=scol("w1", m)), "act")
                c.ms["pp_free0"] = a1
                pg.wait(V, mm_ms, sload, c.ms.get("msc1_free"))
                a2 = pg.inc(V.tensor_scalar(c.msc[:, NT:2 * NT], c.ps_b[:, 0:NT],
                                            scol("w2", m), None, ALU.mult), "vec")
                c.ms["pp_free1"] = a2
                pg.wait(V, a1, a2)
                hv = pg.inc(V.tensor_tensor(c.hbuf1[:, m * NT:(m + 1) * NT],
                                            c.msc[:, 0:NT], c.msc[:, NT:2 * NT], ALU.mult), "vec")
                c.ms["msc0_free"] = hv
                c.ms["msc1_free"] = hv
        h_done = pg.at("vec")

        def stage(wname, src, dst, sname, src_ready):
            wd = c.wdram[wname]
            rr = {}
            for k in range(3):
                rr[k] = load_w_group(pg, c, wd, k, 32, 128, k % 4, buf=c.mslot,
                                     key="ms", sem="dm")
            last = None
            for m in range(32):
                sl2 = m % 4
                if m + 3 < 32:
                    rr[m + 3] = load_w_group(pg, c, wd, m + 3, 32, 128, (m + 3) % 4,
                                             buf=c.mslot, key="ms", sem="dm")
                wv = c.mslot[sl2][:, 0:4096].rearrange("p (kc m) -> p kc m", m=128)
                par = m % 2
                psum = c.ps_a if par == 0 else c.ps_b
                pg.wait(TE, rr[m], src_ready, c.ms.get("pp_free%d" % par))
                for kc in range(32):
                    mm = nc.tensor.matmul(psum[:, 0:NT], wv[:, kc, :],
                                          src[:, kc * NT:(kc + 1) * NT],
                                          start=(kc == 0), stop=(kc == 31))
                mm_ms = pg.inc(mm, "tns")
                c.ms[("ms_user", sl2)] = mm_ms
                pg.wait(S, mm_ms, c.ms.get("dst_war"))
                last = pg.inc(S.activation(dst[:, m * NT:(m + 1) * NT], psum[:, 0:NT],
                                           AF.Gelu, scale=scol(sname, m)), "act")
                c.ms["pp_free%d" % par] = last
            return pg.at("act")

        g1_done = stage("l1f_h1", c.hbuf1, c.hbuf2, "h1", h_done)
        c.ms["dst_war"] = pg.at("tns")
        g2_done = stage("l1f_h2", c.hbuf2, c.hbuf1, "h2", g1_done)
        c.ms["dst_war"] = None

        w3d = c.wdram["l1f_w3"]
        rr = {0: load_w_group(pg, c, w3d, 0, 32, 128, 0, buf=c.mslot, key="ms", sem="dm")}
        for m in range(NCH):
            sl2 = m % 2
            if m + 1 < NCH:
                rr[m + 1] = load_w_group(pg, c, w3d, m + 1, 32, 128, (m + 1) % 2,
                                         buf=c.mslot, key="ms", sem="dm")
            wv = c.mslot[sl2][:, 0:4096].rearrange("p (kc m) -> p kc m", m=128)
            par = m % 2
            psum = c.ps_a if par == 0 else c.ps_b
            pg.wait(TE, rr[m], g2_done, c.ms.get("pp_free%d" % par))
            for kc in range(32):
                mm = nc.tensor.matmul(psum[:, 0:NT], wv[:, kc, :],
                                      c.hbuf1[:, kc * NT:(kc + 1) * NT],
                                      start=(kc == 0), stop=(kc == 31))
            mm_ms = pg.inc(mm, "tns")
            c.ms[("ms_user", sl2)] = mm_ms
            pg.wait(V, mm_ms, sload, c.ms.get("sc_free"))
            sv = pg.inc(V.tensor_scalar(c.sc[:, 0:NT], psum[:, 0:NT],
                                        scol("w3", m), None, ALU.mult), "vec")
            c.ms["pp_free%d" % par] = sv
            pg.wait(V, sv, resid_ms)
            ra = pg.inc(V.tensor_tensor(c.resid[:, m * T + t0: m * T + t0 + NT],
                                        c.resid[:, m * T + t0: m * T + t0 + NT],
                                        c.sc[:, 0:NT], ALU.add), "vec")
            c.ms["sc_free"] = ra
    pg.barrier()
    es.close()
    return pg.at("vec")


def emit_ln(pg, c, T, t0, tn, resid_ms, dst, dstT, save_r=False, dst_t0=None):
    nc = pg.nc
    V, S, TE = nc.vector, nc.scalar, nc.tensor
    if dst_t0 is None:
        dst_t0 = t0
    mm = None
    for ch in range(NCH):
        sl = ch % 2
        rs = c.resid[:, ch * T + t0: ch * T + t0 + tn]
        pg.wait(V, resid_ms, c.ms.get(("xbslot", sl)))
        cp = pg.inc(V.tensor_copy(c.xb[:, sl * tn:(sl + 1) * tn], rs), "vec")
        pg.wait(S, cp)
        sq = pg.inc(S.activation(c.sq[:, sl * tn:(sl + 1) * tn],
                                 c.xb[:, sl * tn:(sl + 1) * tn], AF.Square), "act")
        pg.wait(TE, cp, sq)
        if ch == 0:
            pg.wait(TE, c.ms.get("pp_free0"), c.ms.get("pp_free1"), c.ms.get("rows_read"))
        nc.tensor.matmul(c.ps_a[0:1, 0:tn], c.ones_col, c.xb[:, sl * tn:(sl + 1) * tn],
                         start=(ch == 0), stop=(ch == NCH - 1), skip_group_check=True)
        mm = nc.tensor.matmul(c.ps_b[0:1, 0:tn], c.ones_col, c.sq[:, sl * tn:(sl + 1) * tn],
                              start=(ch == 0), stop=(ch == NCH - 1), skip_group_check=True)
        c.ms[("xbslot", sl)] = pg.inc(mm, "tns")
    mm_ms = pg.at("tns")
    pg.wait(V, mm_ms, c.ms.get("rows_free"))
    pg.inc(V.tensor_scalar(c.r_mean[:, 0:tn], c.ps_a[0:1, 0:tn], 1.0 / D, None, ALU.mult), "vec")
    rr = pg.inc(V.tensor_scalar(c.r_var[:, 0:tn], c.ps_b[0:1, 0:tn], 1.0 / D, None, ALU.mult), "vec")
    c.ms["rows_read"] = rr
    pg.wait(V, rr)
    pg.inc(V.tensor_tensor(c.r_t1[:, 0:tn], c.r_mean[:, 0:tn], c.r_mean[:, 0:tn], ALU.mult), "vec")
    pg.wait(V, pg.at("vec"))
    vv = pg.inc(V.tensor_tensor(c.r_var[:, 0:tn], c.r_var[:, 0:tn], c.r_t1[:, 0:tn], ALU.subtract), "vec")
    pg.wait(S, vv)
    sm = pg.inc(S.activation(c.r_t1[:, 0:tn], c.r_var[:, 0:tn], AF.Sqrt,
                             bias=c.eps_col[0:1, :]), "act")
    pg.wait(V, sm)
    pg.inc(V.reciprocal(c.r_rstd[:, 0:tn], c.r_t1[:, 0:tn]), "vec")
    pg.wait(V, pg.at("vec"))
    if save_r:
        pg.inc(V.tensor_copy(c.rrow[0:1, t0:t0 + tn], c.r_rstd[:, 0:tn]), "vec")
    mr = pg.inc(V.tensor_tensor(c.r_t1[:, 0:tn], c.r_mean[:, 0:tn], c.r_rstd[:, 0:tn], ALU.mult), "vec")
    if save_r:
        pg.wait(V, mr)
        pg.inc(V.tensor_copy(c.mrow[0:1, t0:t0 + tn], c.r_t1[:, 0:tn]), "vec")
    pg.wait(TE, mr, c.ms.get("bcast_free"), c.ms.get("scores_free"))
    nc.tensor.matmul(c.ps_s[:, 0:tn], c.ones_row_f, c.r_rstd[:, 0:tn], start=True, stop=True)
    bc = pg.inc(nc.tensor.matmul(c.ps_s[:, 512:512 + tn], c.ones_row_f, c.r_t1[:, 0:tn],
                                 start=True, stop=True), "tns")
    c.ms["rows_free"] = bc
    ov = None
    for ch in range(NCH):
        rs = c.resid[:, ch * T + t0: ch * T + t0 + tn]
        pg.wait(V, bc, resid_ms, c.ms.get("xb32_free"))
        t_ms = pg.inc(V.tensor_tensor(c.xb32[:, 0:tn], rs, c.ps_s[:, 0:tn], ALU.mult), "vec")
        pg.wait(V, t_ms)
        dsl = dst[:, ch * dstT + dst_t0: ch * dstT + dst_t0 + tn]
        ov = pg.inc(V.tensor_tensor(dsl, c.xb32[:, 0:tn], c.ps_s[:, 512:512 + tn], ALU.subtract), "vec")
        c.ms["xb32_free"] = ov
    c.ms["bcast_free"] = ov
    return ov


def load_w_group(pg, c, dram, g, kc_n, m_n, slot, buf=None, key="wslot", sem="dw"):
    nc = pg.nc
    buf = buf if buf is not None else c.wslot
    src = dram[g, :, :, :].rearrange("p kc m -> p (kc m)")
    dst = buf[slot][:, 0:kc_n * m_n]
    pg.wait(nc.gpsimd, c.ms.get((key + "_user", slot)))
    return pg.inc(nc.gpsimd.dma_start(out=dst, in_=src), "%s%d" % (sem, slot), 16)


def emit_attn(pg, c, T, resid_ms, layer):
    nc = pg.nc
    V, S, TE = nc.vector, nc.scalar, nc.tensor
    NT = min(512, T)
    nq = T // 128
    rope = (layer == "l0b")
    l1a = (layer == "l1a")
    MW = 256
    qkv_M = 3072
    wd = c.wdram[layer + "_w"]
    wod = c.wdram[layer + "_wo"]
    es = contextlib.ExitStack()
    _u = "_%d" % pg.uid()
    sb = lambda name, shape, dt: es.enter_context(nc.sbuf_tensor(name + _u, shape, dt))
    c.qbuf = sb("qbuf", [128, NCH * T], BF16)
    c.kbuf = sb("kbuf", [128, NCH * T], BF16)
    c.vbuf = sb("vbuf", [128, (T // 128) * D], BF16)
    c.vtmp = sb("vtmp", [128, NT], BF16)
    c.bias24 = sb("bias24", [128, 24], F32)
    c.bias_bo = sb("bias_bo", [128, 8], F32)
    c.qbias = sb("qbias", [128, 8], F32)
    esx = contextlib.ExitStack()
    sbx = lambda name, shape, dt: esx.enter_context(nc.sbuf_tensor(name + _u, shape, dt))
    if rope:
        c.ropet = sbx("ropet", [128, 2 * T], BF16)
        c.rp = sbx("rp", [128, 2 * NT], BF16)
        c.scb = sbx("scb", [128, NT], BF16)
        r1 = pg.inc(nc.sync.dma_start(out=c.ropet[:], in_=c.ropetd[:, :]), "dc3", 16)
        c.ms["rope_const"] = r1

    if layer in ("l0a", "l1a"):
        b1 = pg.inc(nc.sync.dma_start(out=c.bias24[:], in_=c.wdram[layer + "_b"][:, :]), "db1", 16)
        b2 = pg.inc(nc.sync.dma_start(out=c.bias_bo[:], in_=c.wdram[layer + "_bo"][:, :]), "db2", 16)
        c.ms["bias_ready"] = [b1, b2]
    else:
        c.ms["bias_ready"] = None

    pg.wait(V, c.ms.get("bias_ready"))
    if rope:
        qbs = pg.inc(V.memset(c.qbias[:, 0:8], 0.0), "vec")
        zb = pg.inc(V.memset(c.bias24[:], 0.0), "vec")
    else:
        qbs = pg.inc(V.tensor_scalar(c.qbias[:, 0:8], c.bias24[:, 0:8], 0.125, None, ALU.mult), "vec")
        zb = qbs

    HT = min(512, T)
    ngroups = qkv_M // MW
    xnq = sbx("xnq", [128, NCH * HT], BF16)
    c.wslot = [sbx("wslot0", [128, NCH * MW], BF16), sbx("wslot1", [128, NCH * MW], BF16)]
    for half in range(T // HT):
        ht0 = half * HT
        for t in range(HT // NT):
            ln_ms = emit_ln(pg, c, T, ht0 + t * NT, NT, resid_ms, xnq, HT, dst_t0=t * NT)
        wready = {}
        wready[0] = load_w_group(pg, c, wd, 0, NCH, MW, 0)
        for g in range(ngroups):
            slot = g % 2
            if g + 1 < ngroups:
                wready[g + 1] = load_w_group(pg, c, wd, g + 1, NCH, MW, (g + 1) % 2)
            wv = c.wslot[slot][:, 0:NCH * MW].rearrange("p (kc m) -> p kc m", m=MW)
            for t in range(HT // NT):
                for mi in range(MW // 128):
                    m = g * (MW // 128) + mi
                    par = m % 2
                    psum = c.ps_a if par == 0 else c.ps_b
                    pg.wait(TE, wready[g], ln_ms, c.ms.get("pp_free%d" % par),
                            c.ms.get("rows_read"))
                    for kc in range(NCH):
                        mm = nc.tensor.matmul(psum[:, 0:NT],
                                              wv[:, kc, mi * 128:(mi + 1) * 128],
                                              xnq[:, kc * HT + t * NT: kc * HT + t * NT + NT],
                                              start=(kc == 0), stop=(kc == NCH - 1))
                    mm_ms = pg.inc(mm, "tns")
                    c.ms[("wslot_user", slot)] = mm_ms
                    col = ht0 + t * NT
                    if m < 8:
                        if rope:
                            pg.wait(S, mm_ms, qbs, c.ms.get("scb_free"))
                            ev = pg.inc(S.activation(c.scb[:, 0:NT], psum[:, 0:NT], AF.Identity,
                                                     bias=c.qbias[:, m:m + 1], scale=0.125), "act")
                            ev = emit_rope(pg, c, T, c.qbuf, m, col, NT, ev)
                            c.ms["scb_free"] = ev
                        else:
                            pg.wait(S, mm_ms, qbs)
                            ev = pg.inc(S.activation(c.qbuf[:, m * T + col: m * T + col + NT],
                                                     psum[:, 0:NT], AF.Identity,
                                                     bias=c.qbias[:, m:m + 1], scale=0.125), "act")
                    elif m < 16:
                        ch = m - 8
                        if rope:
                            pg.wait(S, mm_ms, zb, c.ms.get("scb_free"))
                            ev = pg.inc(S.activation(c.scb[:, 0:NT], psum[:, 0:NT], AF.Identity,
                                                     bias=c.bias24[:, m:m + 1]), "act")
                            ev = emit_rope(pg, c, T, c.kbuf, ch, col, NT, ev)
                            c.ms["scb_free"] = ev
                        else:
                            pg.wait(S, mm_ms, zb)
                            ev = pg.inc(S.activation(c.kbuf[:, ch * T + col: ch * T + col + NT],
                                                     psum[:, 0:NT], AF.Identity,
                                                     bias=c.bias24[:, m:m + 1]), "act")
                    else:
                        ch = m - 16
                        pg.wait(S, mm_ms, zb, c.ms.get("vtmp_free"))
                        ev = pg.inc(S.activation(c.vtmp[:, 0:NT], psum[:, 0:NT], AF.Identity,
                                                 bias=c.bias24[:, m:m + 1]), "act")
                        for j4 in range(NT // 128):
                            pg.wait(TE, ev, c.ms.get("ps_t_free"))
                            tr = pg.inc(nc.tensor.transpose(
                                c.ps_t[:], c.vtmp[:, j4 * 128:(j4 + 1) * 128], c.identb), "tns")
                            pg.wait(V, tr)
                            ktile = (col // 128) + j4
                            cp = pg.inc(V.tensor_copy(
                                c.vbuf[:, ktile * D + ch * 128: ktile * D + (ch + 1) * 128],
                                c.ps_t[:]), "vec")
                            c.ms["ps_t_free"] = cp
                        c.ms["vtmp_free"] = pg.at("tns")
                    c.ms["pp_free%d" % par] = ev
    qkv_done = [pg.at("act"), pg.at("vec"), pg.at("tns")]
    pg.barrier()
    esx.close()

    c.pbuf = sb("pbuf", [128, 2 * T], BF16)
    c.pts = [sb("pts0", [128, 128], BF16), sb("pts1", [128, 128], BF16),
             sb("pts2", [128, 128], BF16)]
    c.otbuf = sb("otbuf", [128, NCH * 512], BF16)
    c.rowmax = sb("rowmax", [128, 2], F32)
    c.rowsum = sb("rowsum", [128, 2], F32)
    c.rowrec = sb("rowrec", [128, 2], F32)
    c.gmax = sb("gmax", [128, 10], F32)
    c.wo = sb("wo", [128, NCH * 512], BF16)

    causal = not l1a

    def stage_view(s):
        if s == 0:
            return c.ps_a[:, 0:64].bitcast(BF16)
        if s == 1:
            return c.ps_b[:, 0:64].bitcast(BF16)
        return c.ps_t[:]

    def emit_pv(i, qi, hh, nk):
        par2 = hh % 2
        chq2 = hh // 2
        pb2 = (hh % 2) * 64
        PB = par2 * T
        pn = c.ms[("pn", par2)]

        def emit_tr(j):
            s = j % 3
            fk = "ptc_free%d" % s if s < 2 else "ps_t_free"
            pv = stage_view(s)
            pg.wait(TE, pn, c.ms.get(fk),
                    c.ms.get("pp_free%d" % s) if s < 2 else None)
            tr = pg.inc(nc.tensor.transpose(pv, c.pbuf[:, PB + j * 128: PB + (j + 1) * 128],
                                            c.identb), "tns")
            if s < 2:
                c.ms["pp_free%d" % s] = None
            pg.wait(S, tr, c.ms.get(("pts_user", s)))
            cp = pg.inc(S.copy(c.pts[s][:], pv), "act")
            c.ms[fk] = cp
            c.ms[("flcp", j)] = cp

        emit_tr(0)
        if nk > 1:
            emit_tr(1)
        mo = None
        for j in range(nk):
            if j + 2 < nk:
                emit_tr(j + 2)
            s = j % 3
            pg.wait(TE, c.ms.pop(("flcp", j)), None if j > 0 else c.ms.get("ps_o_free"))
            mo = nc.tensor.matmul(c.ps_o[0:64, 0:128],
                                  c.vbuf[:, j * D + hh * 64: j * D + (hh + 1) * 64],
                                  c.pts[s][:],
                                  start=(j == 0), stop=(j == nk - 1))
            c.ms[("pts_user", s)] = pg.at("tns")
        c.ms["pbf%d" % par2] = pg.at("tns")
        mo_ms = pg.inc(mo, "tns")
        pg.wait(S, mo_ms, c.ms.get("ot_user"))
        oe = pg.inc(S.copy(
            c.otbuf[pb2:pb2 + 64, chq2 * 512 + qi * 128: chq2 * 512 + qi * 128 + 128],
            c.ps_o[0:64, 0:128]), "act")
        c.ms["ps_o_free"] = oe

    for qg in range(nq // 4):
        for qi in range(4):
            i = qg * 4 + qi
            nk = (i + 1) if causal else nq
            L = nk * 128
            split = (L <= 1024)
            for h in range(HEADS):
                par = h % 2
                chq = h // 2
                pb = (h % 2) * 64
                RB = par * 1024 if split else 0
                PB = par * T
                qsl = c.qbuf[pb:pb + 64, chq * T + i * 128: chq * T + (i + 1) * 128]
                if split:
                    pg.wait(TE, qkv_done, c.ms.get("sfree%d" % par), c.ms.get("bcast_free"))
                else:
                    pg.wait(TE, qkv_done, c.ms.get("sfree0"), c.ms.get("sfree1"),
                            c.ms.get("bcast_free"))
                ngf = L // 512
                nrem = nk - ngf * 4
                glast = ngf - 1 if nrem == 0 else ngf
                gmms = {}
                for kg in range(ngf):
                    mm = nc.tensor.matmul(
                        c.ps_s[:, RB + kg * 512: RB + (kg + 1) * 512], qsl,
                        c.kbuf[pb:pb + 64, chq * T + kg * 512: chq * T + (kg + 1) * 512],
                        start=True, stop=True)
                    gmms[kg] = pg.inc(mm, "tns")
                for j in range(ngf * 4, nk):
                    mm = nc.tensor.matmul(
                        c.ps_s[:, RB + j * 128: RB + (j + 1) * 128], qsl,
                        c.kbuf[pb:pb + 64, chq * T + j * 128: chq * T + (j + 1) * 128],
                        start=True, stop=True)
                if nrem:
                    gmms[ngf] = pg.inc(mm, "tns")
                mm_ms = gmms[glast]
                if h > 0:
                    emit_pv(i, qi, h - 1, nk)
                if causal:
                    pg.wait(V, mm_ms)
                    mm_ms = pg.inc(V.tensor_tensor(
                        c.ps_s[:, RB + (nk - 1) * 128: RB + nk * 128],
                        c.ps_s[:, RB + (nk - 1) * 128: RB + nk * 128],
                        c.maskf[:], ALU.add), "vec")
                    gmms[glast] = mm_ms
                for kg in sorted(gmms):
                    wdt = 512 if kg < ngf else nrem * 128
                    pg.wait(V, gmms[kg], c.ms.get(("gmf", par)))
                    pg.inc(V.tensor_reduce(c.gmax[:, par * 5 + kg: par * 5 + kg + 1],
                                           c.ps_s[:, RB + kg * 512: RB + kg * 512 + wdt],
                                           AX.X, ALU.max), "vec")
                pg.wait(V, pg.at("vec"), c.ms.get(("rmf", par)))
                rm = pg.inc(V.tensor_reduce(c.rowmax[:, par:par + 1],
                                            c.gmax[:, par * 5: par * 5 + len(gmms)],
                                            AX.X, ALU.max, negate=True), "vec")
                c.ms[("gmf", par)] = rm
                pg.wait(S, rm, c.ms.get("pbf%d" % par), c.ms.get(("rsf", par)))
                ex = pg.inc(S.activation(c.pbuf[:, PB:PB + L], c.ps_s[:, RB:RB + L], AF.Exp,
                                         bias=c.rowmax[:, par:par + 1],
                                         accum_out=c.rowsum[:, par:par + 1]), "act")
                if split:
                    c.ms["sfree%d" % par] = ex
                else:
                    c.ms["sfree0"] = ex
                    c.ms["sfree1"] = ex
                c.ms[("rmf", par)] = ex
                pg.wait(V, ex, c.ms.get(("rrf", par)))
                rc = pg.inc(V.reciprocal(c.rowrec[:, par:par + 1], c.rowsum[:, par:par + 1]), "vec")
                c.ms[("rsf", par)] = rc
                pg.wait(V, rc)
                pn = pg.inc(V.tensor_scalar(c.pbuf[:, PB:PB + L], c.pbuf[:, PB:PB + L],
                                            c.rowrec[:, par:par + 1], None, ALU.mult), "vec")
                c.ms[("rrf", par)] = pn
                c.ms[("pn", par)] = pn
            emit_pv(i, qi, HEADS - 1, nk)
        ot_ms = [pg.at("vec"), pg.at("act")]
        t0 = qg * 512
        wvv = c.wo[:, :].rearrange("p (kc m) -> p kc m", m=512)
        for m in range(NCH):
            par = m % 2
            psum = c.ps_a if par == 0 else c.ps_b
            if m % 4 == 0:
                pg.wait(nc.gpsimd, c.ms.get("wo_user"))
                wo_ms = pg.inc(nc.gpsimd.dma_start(
                    out=c.wo[:, :].rearrange("p (kc m) -> p kc m", m=512),
                    in_=wod[0, :, :, (m // 4) * 512:(m // 4) * 512 + 512]), "dwo", 16)
            pg.wait(TE, ot_ms, wo_ms, c.ms.get("pp_free%d" % par), c.ms.get("ptc_free%d" % par))
            for kc in range(NCH):
                mm = nc.tensor.matmul(psum[:, 0:512], wvv[:, kc, (m % 4) * 128:(m % 4) * 128 + 128],
                                      c.otbuf[:, kc * 512:(kc + 1) * 512],
                                      start=(kc == 0), stop=(kc == NCH - 1))
            mm_ms = pg.inc(mm, "tns")
            c.ms["wo_user"] = mm_ms
            bia = c.zero_col[:, 0:1] if rope else c.bias_bo[:, m:m + 1]
            pg.wait(S, mm_ms, c.ms.get("bias_ready"), c.ms.get("sc_free"))
            ad = pg.inc(S.activation(c.sc[:, 0:512], psum[:, 0:512], AF.Identity, bias=bia), "act")
            c.ms["pp_free%d" % par] = ad
            c.ms["ptc_free%d" % par] = ad
            pg.wait(V, ad, resid_ms)
            ra = pg.inc(V.tensor_tensor(c.resid[:, m * T + t0: m * T + t0 + 512],
                                        c.resid[:, m * T + t0: m * T + t0 + 512],
                                        c.sc[:, 0:512], ALU.add), "vec")
            c.ms["sc_free"] = ra
        c.ms["ot_user"] = pg.at("tns")
    pg.barrier()
    es.close()
    return pg.at("vec")


def emit_rope(pg, c, T, buf, ch, col, NT, ev):
    nc = pg.nc
    V = nc.vector
    last = ev
    for hh in range(2):
        be, bo = hh * 64, hh * 64 + 32
        es = c.scb[be:be + 32, 0:NT]
        os_ = c.scb[bo:bo + 32, 0:NT]
        ed = buf[be:be + 32, ch * T + col: ch * T + col + NT]
        od = buf[bo:bo + 32, ch * T + col: ch * T + col + NT]
        cos_e = c.ropet[be:be + 32, col:col + NT]
        sin_e = c.ropet[be:be + 32, T + col: T + col + NT]
        cos_o = c.ropet[bo:bo + 32, col:col + NT]
        sin_o = c.ropet[bo:bo + 32, T + col: T + col + NT]
        sA = c.rp[:, 0:NT]
        sB = c.rp[:, NT:2 * NT]
        pg.wait(V, ev, c.ms.get("rope_scr_free"), c.ms.get("rope_const"))
        pg.inc(V.tensor_tensor(sA[be:be + 32, :], es, cos_e, ALU.mult), "vec")
        pg.inc(V.tensor_tensor(sB[be:be + 32, :], os_, sin_o, ALU.mult), "vec")
        pg.inc(V.tensor_tensor(sA[bo:bo + 32, :], os_, cos_o, ALU.mult), "vec")
        t4 = pg.inc(V.tensor_tensor(sB[bo:bo + 32, :], es, sin_e, ALU.mult), "vec")
        pg.wait(V, t4)
        pg.inc(V.tensor_tensor(ed, sA[be:be + 32, :], sB[be:be + 32, :], ALU.subtract), "vec")
        last = pg.inc(V.tensor_tensor(od, sA[bo:bo + 32, :], sB[bo:bo + 32, :], ALU.add), "vec")
        c.ms["rope_scr_free"] = last
    return last



WSPECS = {
    "l0a_w": [12, 128, 8, 256], "l0a_b": [128, 24], "l0a_wo": [1, 128, 8, 1024], "l0a_bo": [128, 8],
    "l0b_w": [12, 128, 8, 256], "l0b_wo": [1, 128, 8, 1024],
    "l1a_w": [12, 128, 8, 256], "l1a_b": [128, 24], "l1a_wo": [1, 128, 8, 1024], "l1a_bo": [128, 8],
    "moe_r": [D, 16], "moe_gb": [128, 128], "moe_db": [16, D],
    "moe_gu": [16, 128, 8, 1024], "moe_dn": [16, 128, 4, 1024],
    "l1f_w1": [8, 128, 8, 512], "l1f_w2": [8, 128, 8, 512],
    "l1f_h1": [32, 128, 32, 128], "l1f_h2": [32, 128, 32, 128],
    "l1f_w3": [8, 128, 32, 128], "l1f_s": [128, 136],
    "l1b_w": [8, 128, 8, 512], "l1b_ba": [1, 128, 8, 64],
    "l1b_wo": [2, 128, 8, 512], "l1b_dtb": [16, 1], "l1b_alog": [16, 1],
    "l1b_nw": [128, 1],
}
CF2_SHAPE = [128, 192]
SL_WEIGHTS = {
    "l0a": ["l0a_w", "l0a_b", "l0a_wo", "l0a_bo"],
    "l0b": ["l0b_w", "l0b_wo"],
    "l1a": ["l1a_w", "l1a_b", "l1a_wo", "l1a_bo"],
    "l0m": ["moe_r", "moe_gb", "moe_db", "moe_gu", "moe_dn"],
    "l1f": ["l1f_w1", "l1f_w2", "l1f_h1", "l1f_h2", "l1f_w3", "l1f_s"],
    "l1b": ["l1b_w", "l1b_ba", "l1b_wo", "l1b_dtb", "l1b_alog", "l1b_nw"],
}
ATTN_SET = ("l0a", "l0b", "l1a")


def build_program(T, sublayers, n_seq=2):
    nc = bass.Bass("TRN2", target_bir_lowering=False)
    stack = contextlib.ExitStack()
    pg = Prog(nc, stack)
    c = Ctx()
    c.ms = {}

    xin = nc.dram_tensor("xin", [n_seq, D, T], F32, kind="ExternalInput").ap()
    yout = nc.dram_tensor("yout", [n_seq, D, T], F32, kind="ExternalOutput").ap()
    cb = nc.dram_tensor("cb", [128, 256], BF16, kind="ExternalInput").ap()
    cf = nc.dram_tensor("cf", [128, 128], F32, kind="ExternalInput").ap()
    if "l0b" in sublayers:
        c.ropetd = nc.dram_tensor("ropet", [128, 2 * T], BF16, kind="ExternalInput").ap()
    if "l0m" in sublayers:
        c.cindd = nc.dram_tensor("cind", [16, 16 * 128], BF16, kind="ExternalInput").ap()
    if "l1b" in sublayers:
        c.cf2d = nc.dram_tensor("cf2d", [128, 192], F32, kind="ExternalInput").ap()
        c.cfid = nc.dram_tensor("cfi", [128, 128], F32, kind="ExternalInput").ap()
        c.g_scr = nc.dram_tensor("g_scr", [16, 256], F32).ap()
    c.wdram = {}
    need = []
    for sl in sublayers:
        for wn in SL_WEIGHTS[sl]:
            if wn not in need:
                need.append(wn)
    for name in need:
        c.wdram[name] = nc.dram_tensor(name, WSPECS[name], F32, kind="ExternalInput").ap()

    sb = lambda name, shape, dt: stack.enter_context(nc.sbuf_tensor(name, shape, dt))
    ps = lambda name, shape, dt: stack.enter_context(nc.psum_tensor(name, shape, dt))

    NT = min(512, T)
    c.resid = sb("resid", [128, NCH * T], F32)
    c.constb = sb("constb", [128, 256], BF16)
    c.maskf = sb("maskf", [128, 128], F32)
    c.identb = c.constb[:, 0:128]
    c.ones_col = c.constb[:, 128:129]
    c.onesf = sb("onesf", [1, 128], F32)
    c.eps_col = sb("epsc", [1, 1], F32)
    c.zero_col = sb("zeroc", [128, 1], F32)
    c.xb = sb("xb", [128, 2 * NT], BF16)
    c.xb32 = sb("xb32", [128, NT], F32)
    c.sq = sb("sq", [128, 2 * NT], BF16)
    c.r4 = sb("r4", [1, 4 * NT], F32)
    c.r_mean = c.r4[0:1, 0:NT]
    c.r_var = c.r4[0:1, NT:2 * NT]
    c.r_t1 = c.r4[0:1, 2 * NT:3 * NT]
    c.r_rstd = c.r4[0:1, 3 * NT:4 * NT]
    c.sc = sb("sc", [128, 512], F32)

    c.ps_s = ps("ps_s", [128, 2048], F32)
    c.ps_a = ps("ps_a", [128, 512], F32)
    c.ps_b = ps("ps_b", [128, 512], F32)
    c.ps_o = ps("ps_o", [128, 128], F32)
    c.ps_t = ps("ps_t", [128, 128], BF16)

    c.ones_row_f = c.onesf[0:1, :]

    ld1 = pg.inc(nc.sync.dma_start(out=c.constb[:], in_=cb[:, :]), "dc1", 16)
    ld2 = pg.inc(nc.sync.dma_start(out=c.maskf[:], in_=cf[:, :]), "dc2", 16)
    pg.wait(nc.vector, ld1, ld2)
    pg.inc(nc.vector.memset(c.onesf[:], 1.0), "vec")
    pg.inc(nc.vector.memset(c.eps_col[:], 1e-5), "vec")
    pg.inc(nc.vector.memset(c.zero_col[:], 0.0), "vec")
    init_v = pg.at("vec")
    for eng in (nc.tensor, nc.scalar):
        pg.wait(eng, init_v, ld1, ld2)

    for s in range(n_seq):
        pg.wait(nc.sync, c.ms.get("resid_stored"))
        rl = pg.inc(nc.sync.dma_start(
            out=c.resid[:].rearrange("p (ch t) -> p ch t", ch=NCH),
            in_=xin[s, :, :].rearrange("(ch p) t -> p ch t", p=128)), "dx", 16)
        resid_ms = rl
        for sl in sublayers:
            pg.barrier()
            if sl == "l0m":
                emit_moe(pg, c, T, resid_ms)
            elif sl == "l1f":
                emit_mlp(pg, c, T, resid_ms)
            elif sl == "l1b":
                emit_dnet(pg, c, T, resid_ms)
            else:
                emit_attn(pg, c, T, resid_ms, layer=sl)
            pg.barrier()
        done = pg.at("vec")
        pg.wait(nc.sync, done)
        st = pg.inc(nc.sync.dma_start(
            out=yout[s, :, :].rearrange("(ch p) t -> p ch t", p=128),
            in_=c.resid[:].rearrange("p (ch t) -> p ch t", ch=NCH)), "dy", 16)
        c.ms["resid_stored"] = st
        pg.barrier()

    pg.finish()
    stack.close()
    return nc



def const_inputs(sublayers, consts):
    m = {"cb": consts["cb"], "cf": consts["cf"]}
    if "l0b" in sublayers:
        m["ropet"] = consts["ropet"]
    if "l0m" in sublayers:
        m["cind"] = consts["cind"]
    if "l1b" in sublayers:
        m["cf2d"] = consts["cf2"]
        m["cfi"] = consts["cfi"]
    return m


def kernel(**inputs):
    T = 2048
    x = np.ascontiguousarray(np.transpose(inputs["L0_attn_a_hidden_states"], (0, 2, 1)))
    w = prep_weights(inputs)
    consts = make_consts(T)
    groups = [["l0a", "l0m", "l0b", "l0m", "l1a", "l1f", "l1b", "l1f"]]
    cur = x
    for sublayers in groups:
        nc = build_program(T, sublayers)
        need = []
        for sl in sublayers:
            for wn in SL_WEIGHTS[sl]:
                if wn not in need:
                    need.append(wn)
        in_maps = []
        for core in range(8):
            m = {"xin": cur[core * 2:(core + 1) * 2]}
            m.update(const_inputs(sublayers, consts))
            for name in need:
                m[name] = w[name]
            in_maps.append(m)
        res = run_bass_kernel_spmd(nc, in_maps, core_ids=list(range(8)))
        cur = np.concatenate([r["yout"] for r in res.results], axis=0)
    return np.ascontiguousarray(np.transpose(cur, (0, 2, 1)))
